# version 02
# baseline (speedup 1.0000x reference)
import numpy as np
import ml_dtypes
from contextlib import ExitStack

import concourse.bass as bass
import concourse.mybir as mybir
from concourse.bass_utils import run_bass_kernel_spmd

F32 = mybir.dt.float32
BF16 = mybir.dt.bfloat16
AF = mybir.ActivationFunctionType
ALU = mybir.AluOpType
AX = mybir.AxisListType.X
BFNP = ml_dtypes.bfloat16

D = 2048
T = 4096
H = 16
HD = 128
NTOK = 2048
NTILE = 16
NJS = 4
DFF = 8192
MAGIC = 12582912.0
INV_SQRT_HD = float(1.0 / np.sqrt(128.0))
Q8 = 127.0
INV_Q8SQ = float(1.0 / (127.0 * 127.0))
EPS = 1e-5
N_CORES = 8

I0 = [0, 0, 1, 1, 2, 2, 3, 3]


class Sem:
    __slots__ = ("h", "n")

    def __init__(self, handle):
        self.h = handle
        self.n = 0


class Prog:

    ENGINES = ("sync", "scalar", "vector", "tensor", "gpsimd")

    def __init__(self, nc):
        self.nc = nc
        self.ops = {e: [] for e in self.ENGINES}
        self.waited = {e: {} for e in self.ENGINES}

    def emit(self, eng, fn, waits=(), inc=None):
        wl = []
        for item in waits:
            if item is None:
                continue
            s, v = item
            assert v <= s.n, f"wait on future sem value {v} > {s.n}"
            key = id(s)
            if self.waited[eng].get(key, -1) >= v:
                continue
            self.waited[eng][key] = v
            wl.append((s.h, v))
        post = None
        if inc is not None:
            s, amt = inc
            s.n += amt
            post = (s.h, amt)
        wl = tuple(wl)

        def f(e, _fn=fn, _wl=wl, _post=post):
            for h, v in _wl:
                e.wait_ge(h, v)
            ins = _fn(e)
            if _post is not None:
                ins.then_inc(_post[0], _post[1])
        self.ops[eng].append(f)

    def dma(self, q, dst, src, waits=(), sem=None, slow_ok=False):
        self.emit(q,
                  lambda e: e.dma_start(dst, src, allow_slow_non_contiguous=slow_ok),
                  waits, (sem, 16) if sem is not None else None)

    def mm(self, lhsT, rhs, out, start, stop, waits=(), inc=None):
        self.emit("tensor",
                  lambda e: e.matmul(out, lhsT, rhs, start=start, stop=stop),
                  waits, inc)

    def act(self, out, in_, func, bias=0.0, scale=1.0, accum=None, waits=(), inc=None):
        self.emit("scalar",
                  lambda e: e.activation(out, in_, func, bias=bias, scale=scale,
                                         accum_out=accum),
                  waits, inc)

    def tt(self, op, out, a, b, waits=(), inc=None, eng="vector"):
        self.emit(eng, lambda e: e.tensor_tensor(out, a, b, op), waits, inc)

    def ts(self, out, in0, s1, s2, op0, op1=ALU.bypass, accum=None, waits=(), inc=None,
           eng="vector"):
        self.emit(eng,
                  lambda e: e.tensor_scalar(out, in0, s1, s2, op0, op1,
                                            accum_out=accum),
                  waits, inc)

    def stt(self, out, in0, scalar, in1, op0, op1, accum=None, waits=(), inc=None):
        self.emit("vector",
                  lambda e: e.scalar_tensor_tensor(out, in0, scalar, in1, op0, op1,
                                                   accum_out=accum),
                  waits, inc)

    def red(self, out, in_, op, absval=False, waits=(), inc=None):
        self.emit("vector",
                  lambda e: e.tensor_reduce(out, in_, AX, op,
                                            apply_absolute_value=absval),
                  waits, inc)

    def copy(self, out, in_, waits=(), inc=None, eng="vector"):
        self.emit(eng, lambda e: e.tensor_copy(out, in_), waits, inc)

    def recip(self, out, in_, waits=(), inc=None):
        self.emit("vector", lambda e: e.reciprocal(out, in_), waits, inc)

    def memset(self, ap, val, waits=(), inc=None, eng="vector"):
        self.emit(eng, lambda e: e.memset(ap, val), waits, inc)

    def barrier(self, bsem):
        base = bsem.n
        for eng in self.ENGINES:
            self.emit(eng, lambda e: e.nop(), (), (bsem, 1))
        for eng in self.ENGINES:
            self.emit(eng, lambda e: e.nop(), ((bsem, base + 5),), None)

    def replay(self):
        nc = self.nc
        with nc.Block() as block:
            @block.sync
            def _(e):
                for f in self.ops["sync"]:
                    f(e)

            @block.scalar
            def _(e):
                for f in self.ops["scalar"]:
                    f(e)

            @block.vector
            def _(e):
                for f in self.ops["vector"]:
                    f(e)

            @block.tensor
            def _(e):
                for f in self.ops["tensor"]:
                    f(e)

            @block.gpsimd
            def _(e):
                for f in self.ops["gpsimd"]:
                    f(e)


def w(sem):
    return (sem, sem.n)


class Scanner:

    def __init__(self, prog, slots, parts, sems):
        self.prog = prog
        self.slots = slots
        self.parts = parts
        self.sScan, self.sV = sems
        self.work = []
        self.i = 0
        self.slot_free = [None] * len(self.slots)

    def step(self):
        if self.i >= len(self.work):
            return
        src_ap, col = self.work[self.i]
        slot = self.i % len(self.slots)
        p = self.prog
        waits = [self.slot_free[slot]] if self.slot_free[slot] else []
        p.dma("sync", self.slots[slot][:, :], src_ap, waits=waits, sem=self.sScan)
        p.red(self.parts[:, col:col + 1], self.slots[slot][:, :], ALU.max,
              absval=True, waits=[w(self.sScan)], inc=(self.sV, 1))
        self.slot_free[slot] = w(self.sV)
        self.i += 1

    def drain(self):
        while self.i < len(self.work):
            self.step()
        return w(self.sV)


def build_program(upto=99, debug=False):
    nc = bass.Bass("TRN2", target_bir_lowering=False)
    scratch_kind = "ExternalOutput" if debug else "Internal"

    x_fm = nc.dram_tensor("x_fm", [D, T], F32, kind="ExternalInput").ap()
    x_own_fm = nc.dram_tensor("x_own_fm", [D, NTOK], F32, kind="ExternalInput").ap()
    x_own_tm = nc.dram_tensor("x_own_tm", [NTOK, D], F32, kind="ExternalInput").ap()
    wqT = nc.dram_tensor("wqT", [D, D], F32, kind="ExternalInput").ap()
    wkT = nc.dram_tensor("wkT", [D, D], F32, kind="ExternalInput").ap()
    wvT = nc.dram_tensor("wvT", [D, D], F32, kind="ExternalInput").ap()
    bq_col = nc.dram_tensor("bq_col", [128, H], F32, kind="ExternalInput").ap()
    bk_col = nc.dram_tensor("bk_col", [128, H], F32, kind="ExternalInput").ap()
    bv_bc = nc.dram_tensor("bv_bc", [128, D], F32, kind="ExternalInput").ap()
    wprojT = nc.dram_tensor("wprojT", [D, D], F32, kind="ExternalInput").ap()
    bproj_bc = nc.dram_tensor("bproj_bc", [128, D], F32, kind="ExternalInput").ap()
    w1T = nc.dram_tensor("w1T", [D, DFF], F32, kind="ExternalInput").ap()
    b1_bc = nc.dram_tensor("b1_bc", [128, DFF], F32, kind="ExternalInput").ap()
    w2T = nc.dram_tensor("w2T", [DFF, D], F32, kind="ExternalInput").ap()
    b2_bc = nc.dram_tensor("b2_bc", [128, D], F32, kind="ExternalInput").ap()
    masks = nc.dram_tensor("masks", [8 * 128, 128], BF16, kind="ExternalInput").ap()
    ident_bf = nc.dram_tensor("ident_bf", [128, 128], BF16, kind="ExternalInput").ap()
    ident_f = nc.dram_tensor("ident_f", [128, 128], F32, kind="ExternalInput").ap()
    ones_col_bf = nc.dram_tensor("ones_col_bf", [128, 1], BF16, kind="ExternalInput").ap()
    ones_row_f = nc.dram_tensor("ones_row_f", [1, 128], F32, kind="ExternalInput").ap()
    ones_row_bf = nc.dram_tensor("ones_row_bf", [1, 128], BF16, kind="ExternalInput").ap()

    out = nc.dram_tensor("out", [NTOK, D], F32, kind="ExternalOutput").ap()

    q_scr = nc.dram_tensor("q_scr", [H * 128, NTOK], BF16, kind=scratch_kind).ap()
    k_scr = nc.dram_tensor("k_scr", [H * 128, T], BF16, kind=scratch_kind).ap()
    v_scr = nc.dram_tensor("v_scr", [H * T, HD], BF16, kind=scratch_kind).ap()
    y_scr = nc.dram_tensor("y_scr", [NTOK, D], BF16, kind=scratch_kind).ap()
    x1_scr = nc.dram_tensor("x1_scr", [NTOK, D], BF16, kind=scratch_kind).ap()
    g_scr = nc.dram_tensor("g_scr", [NTOK, DFF], BF16, kind=scratch_kind).ap()
    if debug:
        dbg_sc = nc.dram_tensor("dbg_sc", [128, 16], F32, kind="ExternalOutput").ap()
        dbg_cols = nc.dram_tensor("dbg_cols", [128, 96], F32, kind="ExternalOutput").ap()
        dbg_x1q = nc.dram_tensor("dbg_x1q", [128, H * D], BF16,
                                 kind="ExternalOutput").ap()

    prog = Prog(nc)
    ctx = ExitStack()
    sb = lambda name, shape, dt: ctx.enter_context(nc.sbuf_tensor(name, shape, dt))
    mksem = lambda name: Sem(ctx.enter_context(nc.semaphore(name)))

    cst = {
        "ident_bf": sb("c_ident_bf", [128, 128], BF16),
        "ident_f": sb("c_ident_f", [128, 128], F32),
        "ones_col_bf": sb("c_ones_col_bf", [128, 1], BF16),
        "ones_row_f": sb("c_ones_row_f", [1, 128], F32),
        "ones_row_bf": sb("c_ones_row_bf", [1, 128], BF16),
        "masks": sb("c_masks", [128, 8 * 128], BF16),
        "magic_col": sb("c_magic_col", [128, 1], F32),
    }
    sw1_col = sb("sw1_col", [128, 1], F32)
    gq1_col = sb("gq1_col", [128, 1], F32)
    sw2_col = sb("sw2_col", [128, 1], F32)
    gq2_col = sb("gq2_col", [128, 1], F32)
    amax1_cols = sb("amax1_cols", [128, NTILE], F32)
    dq1_cols = sb("dq1_cols", [128, NTILE], F32)
    s1_cols = sb("s1_cols", [128, NTILE], F32)
    amax2_parts = sb("amax2_parts", [128, NTILE * 16], F32)
    dq2_cols = sb("dq2_cols", [128, NTILE], F32)
    s2_cols = sb("s2_cols", [128, NTILE], F32)
    negc_cols = sb("negc_cols", [128, H], F32)

    p_small = ctx.enter_context(nc.psum_tensor("p_small", [128, 512], F32))

    sLd = mksem("sLd")
    sCast = mksem("sCast")
    sSt = mksem("sSt")
    sT = mksem("sT")
    sV = mksem("sV")
    sA = mksem("sA")
    sP = mksem("sP")
    sB = mksem("sB")
    sScan = mksem("sScan")
    sY = mksem("sY")

    prog.dma("sync", cst["ident_bf"][:, :], ident_bf, sem=sLd)
    prog.dma("sync", cst["ident_f"][:, :], ident_f, sem=sLd)
    prog.dma("sync", cst["ones_col_bf"][:, :], ones_col_bf, sem=sLd)
    prog.dma("sync", cst["ones_row_f"][:, :], ones_row_f, sem=sLd)
    prog.dma("sync", cst["ones_row_bf"][:, :], ones_row_bf, sem=sLd)
    prog.dma("sync", cst["masks"][:, :].rearrange("p (c n) -> p c n", n=128),
             masks.rearrange("(c p) n -> p c n", p=128), sem=sLd)
    prog.memset(cst["magic_col"][:, :], MAGIC, inc=(sV, 1))
    const_ld = w(sLd)

    p1 = ExitStack()
    p1sb = lambda name, shape, dt: p1.enter_context(nc.sbuf_tensor(name, shape, dt))
    wgt_sb = p1sb("wgt_sb", [128, H * D], BF16)
    xg = [p1sb(f"xg{i}", [128, 16 * 1024], BF16) for i in range(2)]
    vst = [p1sb(f"vst{i}", [128, 1024], BF16) for i in range(4)]
    kst = [p1sb(f"kst{i}", [128, 512], BF16) for i in range(4)]
    bias_sb = p1sb("bias_sb", [128, D + 2 * H], F32)
    scan_slots = [p1sb(f"scan{i}", [128, 2048], F32) for i in range(4)]
    wparts = p1sb("wparts", [128, 128], F32)
    pv = [p1.enter_context(nc.psum_tensor(f"pv{i}", [128, 1024], F32)) for i in range(2)]
    pk = [p1.enter_context(nc.psum_tensor(f"pk{i}", [128, 512], F32)) for i in range(2)]

    prog.dma("sync", bias_sb[:, 0:D], bv_bc, sem=sLd)
    prog.dma("sync", bias_sb[:, D:D + H], bq_col, sem=sLd)
    prog.dma("sync", bias_sb[:, D + H:D + 2 * H], bk_col, sem=sLd)
    bias_ld = w(sLd)

    scanner = Scanner(prog, scan_slots, wparts, (sScan, sV))
    for t in range(64):
        r, c = t % 16, t // 16
        scanner.work.append(
            (w1T[128 * r:128 * (r + 1), 2048 * c:2048 * (c + 1)], t))
    for t in range(64):
        scanner.work.append((w2T[128 * t:128 * (t + 1), :], 64 + t))

    def load_weight_bf16(dst_sb, srcT, waits):
        for c in range(16):
            prog.dma("gpsimd", dst_sb[:, D * c:D * (c + 1)],
                     srcT[128 * c:128 * (c + 1), :], waits=waits, sem=sCast)
        return w(sCast)

    def load_x_group(slot, src, col0, waits):
        for c in range(16):
            prog.dma("gpsimd", xg[slot][:, 1024 * c:1024 * (c + 1)],
                     src[128 * c:128 * (c + 1), col0:col0 + 1024], waits=waits,
                     sem=sCast)
        return w(sCast)

    wv_ready = load_weight_bf16(wgt_sb, wvT, [])
    xg_free = [None, None]
    vst_free = [None] * 4
    pv_free = [None, None]
    grp_loaded = [load_x_group(0, x_fm, 0, []), load_x_group(1, x_fm, 1024, [])]
    for grp in range(4):
        for blk in range(8):
            b2 = grp * 8 + blk
            for oh in range(2):
                i2 = b2 * 2 + oh
                ps = pv[i2 % 2]
                mmw = [grp_loaded[grp], wv_ready, pv_free[i2 % 2]]
                for c in range(16):
                    for o2 in range(2):
                        last = (c == 15 and o2 == 1)
                        prog.mm(
                            xg[grp % 2][:, 1024 * c + 128 * blk:1024 * c + 128 * (blk + 1)],
                            wgt_sb[:, D * c + 1024 * oh + 512 * o2:
                                   D * c + 1024 * oh + 512 * (o2 + 1)],
                            ps[:, 512 * o2:512 * (o2 + 1)],
                            c == 0, c == 15,
                            waits=mmw if (c == 0 and o2 == 0) else (),
                            inc=(sT, 1) if last else None)
                mm_done = w(sT)
                vslot = i2 % 4
                prog.tt(ALU.add, vst[vslot][:, :], ps[:, :],
                        bias_sb[:, 1024 * oh:1024 * (oh + 1)],
                        waits=[mm_done, bias_ld, vst_free[vslot]], inc=(sV, 1))
                pv_free[i2 % 2] = w(sV)
                ev_done = w(sV)
                src3 = vst[vslot][:, :].rearrange("p (h j) -> p h j", j=128)
                dst3 = v_scr.rearrange("(h t) j -> t h j", h=H)
                for hh in range(2):
                    h0 = oh * 8 + hh * 4
                    prog.dma("sync",
                             dst3[128 * b2:128 * (b2 + 1), h0:h0 + 4, :],
                             src3[:, 4 * hh:4 * (hh + 1), :],
                             waits=[ev_done], sem=sSt)
                vst_free[vslot] = w(sSt)
                scanner.step()
        if grp + 2 < 4:
            xg_free[grp % 2] = w(sT)
            grp_loaded.append(load_x_group(grp % 2, x_fm, 1024 * (grp + 2),
                                           [xg_free[grp % 2]]))
    v_tensor_done = w(sT)
    v_stores_done = w(sSt)

    def emit_kq(srcT, src_x, ngrp, bias_off, dst_scr, prev_done,
                ps_init=None, st_init=None):
        wrdy = load_weight_bf16(wgt_sb, srcT, [prev_done])
        st_free = [st_init] * 4
        ps_free = [ps_init, ps_init]
        loaded = [load_x_group(0, src_x, 0, [prev_done]),
                  load_x_group(1, src_x, 1024, [prev_done])]
        cnt = 0
        for grp in range(ngrp):
            for h in range(H):
                for half in range(2):
                    ps = pk[cnt % 2]
                    mmw = [loaded[grp], wrdy, ps_free[cnt % 2]]
                    for c in range(16):
                        prog.mm(
                            wgt_sb[:, D * c + 128 * h:D * c + 128 * (h + 1)],
                            xg[grp % 2][:, 1024 * c + 512 * half:1024 * c + 512 * (half + 1)],
                            ps[:, :], c == 0, c == 15,
                            waits=mmw if c == 0 else (),
                            inc=(sT, 1) if c == 15 else None)
                    mm_done = w(sT)
                    slot = cnt % 4
                    prog.act(kst[slot][:, :], ps[:, :], AF.Identity,
                             bias=bias_sb[:, bias_off + h:bias_off + h + 1],
                             waits=[mm_done, bias_ld, st_free[slot]], inc=(sA, 1))
                    ps_free[cnt % 2] = w(sA)
                    ev_done = w(sA)
                    prog.dma("sync",
                             dst_scr[128 * h:128 * (h + 1),
                                     1024 * grp + 512 * half:1024 * grp + 512 * (half + 1)],
                             kst[slot][:, :], waits=[ev_done], sem=sSt)
                    st_free[slot] = w(sSt)
                    cnt += 1
                    scanner.step()
            if grp + 2 < ngrp:
                gfree = w(sT)
                loaded.append(load_x_group(grp % 2, src_x, 1024 * (grp + 2), [gfree]))
        return w(sT), w(sSt)

    k_tensor_done, k_stores_done = emit_kq(wkT, x_fm, 4, D + H, k_scr, v_tensor_done)
    k_evac_done = w(sA)
    q_tensor_done, q_stores_done = emit_kq(wqT, x_own_fm, 2, D, q_scr, k_tensor_done,
                                           ps_init=k_evac_done, st_init=k_stores_done)

    scan_red_done = scanner.drain()

    def finish_scan(tag, part_off, s_col, gq_col):
        col = p1sb(f"{tag}_col", [128, 1], F32)
        g1 = p1sb(f"{tag}_g1", [1, 1], F32)
        g2 = p1sb(f"{tag}_g2", [1, 2], F32)
        prog.red(col[:, :], wparts[:, part_off:part_off + 64], ALU.max,
                 waits=[scan_red_done], inc=(sV, 1))
        vdone = w(sV)
        prog.mm(col[:, :], cst["ident_f"][:, :], p_small[0:1, 0:128], True, True,
                waits=[const_ld, vdone], inc=(sT, 1))
        prog.red(g1[:, :], p_small[0:1, 0:128], ALU.max, waits=[w(sT)], inc=(sV, 1))
        prog.recip(g2[:, 0:1], g1[:, :], waits=[w(sV)], inc=(sV, 1))
        prog.ts(g2[:, 0:1], g2[:, 0:1], Q8, None, ALU.mult, waits=[w(sV)], inc=(sV, 1))
        prog.ts(g2[:, 1:2], g1[:, :], INV_Q8SQ, None, ALU.mult, waits=[w(sV)],
                inc=(sV, 1))
        prog.mm(cst["ones_row_f"][:, :], g2[:, :], p_small[:, 130:132], True, True,
                waits=[w(sV)], inc=(sT, 1))
        tdone = w(sT)
        prog.copy(s_col[:, :], p_small[:, 130:131], waits=[tdone], inc=(sV, 1))
        prog.copy(gq_col[:, :], p_small[:, 131:132], waits=[tdone], inc=(sV, 1))
        return w(sV)

    sc1_ready = finish_scan("w1s", 0, sw1_col, gq1_col)
    sc2_ready = finish_scan("w2s", 64, sw2_col, gq2_col)

    if debug:
        prog.dma("gpsimd", dbg_sc[:, 0:1], sw1_col[:, :], waits=[sc1_ready], sem=sSt, slow_ok=True)
        prog.dma("gpsimd", dbg_sc[:, 1:2], gq1_col[:, :], waits=[sc1_ready], sem=sSt, slow_ok=True)
        prog.dma("gpsimd", dbg_sc[:, 2:3], sw2_col[:, :], waits=[sc2_ready], sem=sSt, slow_ok=True)
        prog.dma("gpsimd", dbg_sc[:, 3:4], gq2_col[:, :], waits=[sc2_ready], sem=sSt, slow_ok=True)

    def finish(last_waits):
        prog.emit("gpsimd", lambda e: e.nop(), tuple(last_waits), None)
        prog.barrier(sB)
        prog.replay()
        ctx.close()
        return nc

    prog.barrier(sB)
    p1.close()

    if upto <= 1:
        return finish([w(sSt), w(sA), w(sV), w(sT), w(sCast), w(sScan)])

    p2 = ExitStack()
    p2sb = lambda name, shape, dt: p2.enter_context(nc.sbuf_tensor(name, shape, dt))
    k_sb = [p2sb(f"k_sb{i}", [128, T], BF16) for i in range(2)]
    v_sb = [p2sb(f"v_sb{i}", [128, T], BF16) for i in range(2)]
    qh_sb = [p2sb(f"qh_sb{i}", [128, NTOK], BF16) for i in range(2)]
    sq_scr = p2sb("sq_scr", [128, T], BF16)
    ring1 = p2sb("ring1", [128, 8 * 512], BF16)
    ring2 = p2sb("ring2", [128, 8 * 512], BF16)
    linv_sb = p2sb("linv_sb", [1, 512], F32)
    linv_bc = p2sb("linv_bc", [128, 512], F32)
    lnl8 = p2sb("lnl8", [1, 512], F32)
    negmh = [p2sb(f"negmh{i}", [1, 512], BF16) for i in range(2)]
    cstar = p2sb("cstar", [1, 8], F32)
    nrm_col = p2sb("nrm_col", [128, 1], F32)
    negc8_cols = p2sb("negc8_cols", [128, H], F32)
    negc_sc = p2sb("negc_sc", [1, H], F32)
    yst = [p2sb(f"yst{i}", [128, 512], BF16) for i in range(2)]
    ps_s = [p2.enter_context(nc.psum_tensor(f"ps_s{i}", [128, 1024], F32))
            for i in range(2)]
    ps_y = p2.enter_context(nc.psum_tensor("ps_y", [128, 512], F32))
    ps_l = p2.enter_context(nc.psum_tensor("ps_l", [128, 512], F32))
    ps_misc = p2.enter_context(nc.psum_tensor("ps_misc", [128, 512], F32))

    p1_done = [q_tensor_done, q_stores_done, k_stores_done, v_stores_done,
               sc1_ready, sc2_ready]

    def load_head(h, extra):
        hb = h % 2
        waits = list(extra)
        prog.dma("sync", k_sb[hb][:, :], k_scr[128 * h:128 * (h + 1), :],
                 waits=waits, sem=sY)
        vsrc = v_scr[h * T:(h + 1) * T, :].rearrange("(c p) j -> p c j", p=128)
        vdst = v_sb[hb][:, :].rearrange("p (c j) -> p c j", j=128)
        for part in range(4):
            prog.dma("sync", vdst[:, 8 * part:8 * (part + 1), :],
                     vsrc[:, 8 * part:8 * (part + 1), :], waits=waits, sem=sY)
        prog.dma("sync", qh_sb[hb][:, :], q_scr[128 * h:128 * (h + 1), :],
                 waits=waits, sem=sY)
        return w(sY)

    loads_done = [None] * (H + 1)
    loads_done[0] = load_head(0, p1_done)
    sbuf_cons = [None] * (H + 2)
    misc_free = None
    psmall_free = sc2_ready
    psy_free = None
    psl_free = None
    yst_free = [None, None]
    s_exp_done = {}
    ring1_cons = {}
    ring2_cons = {}
    spin = 0
    g1 = 0
    g2 = 0
    negmh_ready = [None] * NJS
    negmh_free = [None, None]
    lnl8_free = [None]

    def emit_sweep1(h, js):
        nonlocal spin, g1, misc_free
        hb = h % 2
        C = 8 * js + 8
        negc8_ready = (sA, negc8_done[h])
        for g in range(C // 2):
            sw_first = [loads_done[h], negc8_ready, s_exp_done.get(spin - 2)]
            for ci in (2 * g, 2 * g + 1):
                cp = ci - 8 * js
                qlo = 128 * I0[cp] if cp >= 0 else 0
                strip = 512 * (ci - 2 * g)
                prog.mm(k_sb[hb][:, 128 * ci:128 * (ci + 1)],
                        qh_sb[hb][:, 512 * js + qlo:512 * (js + 1)],
                        ps_s[spin % 2][:, strip + qlo:strip + 512],
                        True, True,
                        waits=sw_first if ci == 2 * g else (),
                        inc=(sT, 1) if ci == 2 * g + 1 else None)
            smm_done = w(sT)
            slot0 = (2 * g) % 8
            prog.act(ring1[:, 512 * slot0:512 * (slot0 + 2)],
                     ps_s[spin % 2][:, :], AF.Exp,
                     bias=negc8_cols[:, h:h + 1], scale=INV_SQRT_HD / 8.0,
                     waits=[smm_done, ring1_cons.get(g1 - 4)], inc=(sA, 1))
            s_exp_done[spin] = w(sA)
            expd = w(sA)
            nmask = 0
            for ci in (2 * g, 2 * g + 1):
                cp = ci - 8 * js
                if cp >= 0:
                    qlo = 128 * I0[cp]
                    col = 512 * (ci % 8) + qlo
                    prog.tt(ALU.mult, ring1[:, col:col + 128],
                            ring1[:, col:col + 128],
                            cst["masks"][:, 128 * cp:128 * (cp + 1)],
                            waits=[expd, const_ld], inc=(sV, 1))
                    nmask += 1
            mask_done = w(sV) if nmask else expd
            lw = [expd, mask_done, misc_free if g == 0 else None]
            for ci in (2 * g, 2 * g + 1):
                cp = ci - 8 * js
                qlo = 128 * I0[cp] if cp >= 0 else 0
                slot = ci % 8
                prog.mm(cst["ones_col_bf"][:, :],
                        ring1[:, 512 * slot + qlo:512 * (slot + 1)],
                        ps_misc[0:1, qlo:512], ci == 0, ci == C - 1,
                        waits=lw if ci == 2 * g else (),
                        inc=(sT, 1) if ci == 2 * g + 1 else None)
            ring1_cons[g1] = w(sT)
            spin += 1
            g1 += 1
        l8_done = w(sT)
        prog.act(lnl8[:, :], ps_misc[0:1, 0:512], AF.Ln,
                 waits=[l8_done, lnl8_free[0]], inc=(sA, 1))
        misc_free = w(sA)
        prog.ts(negmh[js % 2][:, :], lnl8[:, :], float(-8.0 / INV_SQRT_HD),
                negc_sc[0:1, h:h + 1], ALU.mult, ALU.add,
                waits=[w(sA), negmh_free[js % 2]], inc=(sV, 1))
        lnl8_free[0] = w(sV)
        negmh_ready[js] = w(sV)

    def emit_sweep2(h, js):
        nonlocal spin, g2, misc_free, psy_free, psl_free
        hb = h % 2
        C = 8 * js + 8
        for g in range(C // 2):
            sw_first = [loads_done[h], negmh_ready[js], s_exp_done.get(spin - 2)]
            for ci in (2 * g, 2 * g + 1):
                cp = ci - 8 * js
                qlo = 128 * I0[cp] if cp >= 0 else 0
                strip = 512 * (ci - 2 * g)
                reg = ps_s[spin % 2][:, strip + qlo:strip + 512]
                prog.mm(cst["ones_row_bf"][:, :], negmh[js % 2][0:1, qlo:512],
                        reg, True, False,
                        waits=sw_first if ci == 2 * g else ())
                prog.mm(k_sb[hb][:, 128 * ci:128 * (ci + 1)],
                        qh_sb[hb][:, 512 * js + qlo:512 * (js + 1)],
                        reg, False, True,
                        inc=(sT, 1) if ci == 2 * g + 1 else None)
            smm_done = w(sT)
            slot0 = (2 * g) % 8
            prog.act(ring2[:, 512 * slot0:512 * (slot0 + 2)],
                     ps_s[spin % 2][:, :], AF.Exp, scale=INV_SQRT_HD,
                     waits=[smm_done, ring2_cons.get(g2 - 4)], inc=(sA, 1))
            s_exp_done[spin] = w(sA)
            expd = w(sA)
            nmask = 0
            for ci in (2 * g, 2 * g + 1):
                cp = ci - 8 * js
                if cp >= 0:
                    qlo = 128 * I0[cp]
                    col = 512 * (ci % 8) + qlo
                    prog.tt(ALU.mult, ring2[:, col:col + 128],
                            ring2[:, col:col + 128],
                            cst["masks"][:, 128 * cp:128 * (cp + 1)],
                            waits=[expd, const_ld], inc=(sV, 1))
                    nmask += 1
            mask_done = w(sV) if nmask else expd
            lw = [expd, mask_done,
                  psl_free if g == 0 else None, psy_free if g == 0 else None]
            for ci in (2 * g, 2 * g + 1):
                cp = ci - 8 * js
                qlo = 128 * I0[cp] if cp >= 0 else 0
                slot = ci % 8
                prog.mm(cst["ones_col_bf"][:, :],
                        ring2[:, 512 * slot + qlo:512 * (slot + 1)],
                        ps_l[0:1, qlo:512], ci == 0, ci == C - 1,
                        waits=lw if ci == 2 * g else ())
                prog.mm(v_sb[hb][:, 128 * ci:128 * (ci + 1)],
                        ring2[:, 512 * slot + qlo:512 * (slot + 1)],
                        ps_y[:, qlo:512], ci == 0, ci == C - 1,
                        inc=(sT, 1) if ci == 2 * g + 1 else None)
            ring2_cons[g2] = w(sT)
            spin += 1
            g2 += 1
        av_done = w(sT)
        prog.recip(linv_sb[:, :], ps_l[0:1, 0:512], waits=[av_done], inc=(sV, 1))
        prog.mm(cst["ones_row_f"][:, :], linv_sb[:, :], ps_l[:, 0:512],
                True, True, waits=[w(sV)], inc=(sT, 1))
        prog.act(linv_bc[:, :], ps_l[:, 0:512], AF.Identity,
                 waits=[w(sT)], inc=(sA, 1))
        psl_free = w(sA)
        ysl = (h * NJS + js) % 2
        prog.tt(ALU.mult, yst[ysl][:, :], ps_y[:, :], linv_bc[:, :],
                waits=[psl_free, yst_free[ysl]], inc=(sV, 1))
        psy_free = w(sV)
        ynorm_done = w(sV)
        for qt in range(4):
            qtg = 4 * js + qt
            prog.dma("sync",
                     y_scr[128 * qtg:128 * (qtg + 1), 128 * h:128 * (h + 1)],
                     yst[ysl][:, 128 * qt:128 * (qt + 1)],
                     waits=[ynorm_done], sem=sSt)
        yst_free[ysl] = w(sSt)
        negmh_free[js % 2] = av_done

    negc8_done = [None] * H
    for h in range(H):
        hb = h % 2
        prog.tt(ALU.mult, sq_scr[:, :], k_sb[hb][:, :], k_sb[hb][:, :],
                waits=[loads_done[h]], inc=(sV, 1))
        sqk = w(sV)
        for c in range(32):
            prog.mm(sq_scr[:, 128 * c:128 * (c + 1)], cst["ones_col_bf"][:, :],
                    ps_misc[:, c:c + 1], True, True,
                    waits=[sqk, misc_free] if c == 0 else (),
                    inc=(sT, 1) if c == 31 else None)
        knorm_mm = w(sT)
        prog.red(nrm_col[:, :], ps_misc[:, 0:32], ALU.max, waits=[knorm_mm],
                 inc=(sV, 1))
        misc_free = w(sV)
        prog.mm(nrm_col[:, :], cst["ident_f"][:, :], p_small[0:1, 0:128],
                True, True, waits=[w(sV), psmall_free], inc=(sT, 1))
        prog.red(cstar[:, 1:2], p_small[0:1, 0:128], ALU.max, waits=[w(sT)],
                 inc=(sV, 1))
        kn_done = w(sV)
        prog.tt(ALU.mult, sq_scr[:, 0:NTOK], qh_sb[hb][:, :], qh_sb[hb][:, :],
                waits=[knorm_mm], inc=(sV, 1))
        sqq = w(sV)
        for c in range(16):
            prog.mm(sq_scr[:, 128 * c:128 * (c + 1)], cst["ones_col_bf"][:, :],
                    ps_misc[:, 32 + c:33 + c], True, True,
                    waits=[sqq, misc_free] if c == 0 else (),
                    inc=(sT, 1) if c == 15 else None)
        qnorm_mm = w(sT)
        prog.red(nrm_col[:, :], ps_misc[:, 32:48], ALU.max,
                 waits=[qnorm_mm, kn_done], inc=(sV, 1))
        misc_free = w(sV)
        prog.mm(nrm_col[:, :], cst["ident_f"][:, :], p_small[0:1, 0:128],
                True, True, waits=[w(sV), kn_done], inc=(sT, 1))
        prog.red(cstar[:, 0:1], p_small[0:1, 0:128], ALU.max, waits=[w(sT)],
                 inc=(sV, 1))
        prog.tt(ALU.mult, cstar[:, 2:3], cstar[:, 0:1], cstar[:, 1:2],
                waits=[w(sV)], inc=(sV, 1))
        prog.act(cstar[:, 3:4], cstar[:, 2:3], AF.Ln, scale=float(1.0 / HD),
                 waits=[w(sV)], inc=(sA, 1))
        prog.act(cstar[:, 4:5], cstar[:, 3:4], AF.Exp, scale=0.5,
                 waits=[w(sA)], inc=(sA, 1))
        cs_done = w(sA)
        prog.ts(cstar[:, 5:6], cstar[:, 4:5], -0.125, None, ALU.mult,
                waits=[cs_done], inc=(sV, 1))
        prog.ts(negc_sc[0:1, h:h + 1], cstar[:, 4:5],
                float(-1.0 / INV_SQRT_HD), None, ALU.mult,
                waits=[cs_done], inc=(sV, 1))
        prog.mm(cst["ones_row_f"][:, :], cstar[:, 5:6], p_small[:, 130:131],
                True, True, waits=[w(sV)], inc=(sT, 1))
        prog.act(negc8_cols[:, h:h + 1], p_small[:, 130:131], AF.Identity,
                 waits=[w(sT)], inc=(sA, 1))
        psmall_free = w(sA)
        negc8_done[h] = sA.n

        if h + 1 < H:
            loads_done[h + 1] = load_head(h + 1, [sbuf_cons[h + 1]])

        emit_sweep1(h, 0)
        for js in range(1, NJS):
            emit_sweep1(h, js)
            emit_sweep2(h, js - 1)
        emit_sweep2(h, NJS - 1)
        sbuf_cons[h + 2] = w(sT)

    y_stores_done = w(sSt)
    prog.barrier(sB)
    p2.close()

    if upto <= 2:
        return finish([w(sSt), w(sA), w(sV), w(sT), w(sCast), w(sScan), w(sY)])

    p2_done = [w(sT), w(sA), w(sV), y_stores_done]

    ln_m = sb("ln_m", [128, NTILE], F32)
    ln_rs = sb("ln_rs", [128, NTILE], F32)
    ln_mrs = sb("ln_mrs", [128, NTILE], F32)
    p3ab = ExitStack()
    x1q_fm = p3ab.enter_context(nc.sbuf_tensor("x1q_fm", [128, H * D], BF16))

    p3a = ExitStack()
    asb = lambda name, shape, dt: p3a.enter_context(nc.sbuf_tensor(name, shape, dt))
    wproj_sb = asb("wproj_sb", [128, H * D], BF16)
    yt_sb = [asb(f"yt{i}", [128, D], BF16) for i in range(2)]
    xot_sb = [asb(f"xot{i}", [128, D], BF16) for i in range(2)]
    bprj_sb = asb("bprj_sb", [128, D], BF16)
    tmp_sb = [asb(f"tmpa{i}", [128, D], BF16) for i in range(2)]
    r_sb = [asb(f"ra{i}", [128, D], BF16) for i in range(2)]
    rq_scr = asb("rq_scr", [128, 1024], BF16)
    x1f = asb("x1f", [128, D], F32)
    rndt = asb("rndt", [128, D], F32)
    x1q_tm = asb("x1q_tm", [128, D], BF16)
    lnp_sum = asb("lnp_sum", [128, 32], F32)
    lnp_sq = asb("lnp_sq", [128, 32], F32)
    cols_a = asb("cols_a", [128, 8], F32)
    nmag_col = asb("nmag_col", [128, 1], F32)
    ps_pj = [p3a.enter_context(nc.psum_tensor(f"ps_pj{i}", [128, 1024], F32))
             for i in range(2)]
    ps_tr = [p3a.enter_context(nc.psum_tensor(f"ps_tr{i}", [128, 512], F32))
             for i in range(2)]

    prog.memset(nmag_col[:, :], -MAGIC, waits=p2_done, inc=(sV, 1))
    for c in range(16):
        prog.dma("gpsimd", wproj_sb[:, D * c:D * (c + 1)],
                 wprojT[128 * c:128 * (c + 1), :], waits=p2_done, sem=sCast)
    prog.dma("gpsimd", bprj_sb[:, :], bproj_bc, waits=p2_done, sem=sCast)
    wproj_ready = w(sCast)

    yt_free = [p2_done[0], p2_done[0]]
    xot_free = [None, None]
    tmp_free = [None, None]
    r_free = [None, None]
    cols_free = None
    x1f_free = None
    pj_free = [None, None]
    tr_free = [None, None]
    rndt_free = None
    x1qtm_free = None
    rq_free = None

    def tile_loads_a(t):
        tb = t % 2
        prog.dma("sync", yt_sb[tb][:, :], y_scr[128 * t:128 * (t + 1), :],
                 waits=[yt_free[tb], y_stores_done], sem=sY)
        prog.dma("gpsimd", xot_sb[tb][:, :], x_own_tm[128 * t:128 * (t + 1), :],
                 waits=[xot_free[tb]], sem=sCast)
        return w(sY), w(sCast)

    loads_a = [tile_loads_a(0), tile_loads_a(1)]
    for t in range(NTILE):
        tb = t % 2
        y_ld, x_ld = loads_a[t]
        prog.tt(ALU.add, tmp_sb[tb][:, :], xot_sb[tb][:, :], bprj_sb[:, :],
                waits=[x_ld, wproj_ready, tmp_free[tb]], inc=(sP, 1), eng="gpsimd")
        tmp_ready = w(sP)
        for half in range(2):
            ps = ps_pj[half]
            mmw = [y_ld, wproj_ready, pj_free[half]]
            for c in range(16):
                for o2 in range(2):
                    of0 = 1024 * half + 512 * o2
                    prog.mm(yt_sb[tb][:, 128 * c:128 * (c + 1)],
                            wproj_sb[:, D * c + of0:D * c + of0 + 512],
                            ps[:, 512 * o2:512 * (o2 + 1)],
                            c == 0, c == 15,
                            waits=mmw if (c == 0 and o2 == 0) else (),
                            inc=(sT, 1) if (c == 15 and o2 == 1) else None)
            mm_done = w(sT)
            prog.stt(r_sb[tb][:, 1024 * half:1024 * (half + 1)], ps[:, :], 1.0,
                     tmp_sb[tb][:, 1024 * half:1024 * (half + 1)], ALU.mult, ALU.add,
                     accum=lnp_sum[:, 2 * t % 32 + half:2 * t % 32 + half + 1],
                     waits=[mm_done, tmp_ready] + (r_free[tb] or []), inc=(sV, 1))
            pj_free[half] = w(sV)
            prog.act(rq_scr[:, :], r_sb[tb][:, 1024 * half:1024 * (half + 1)],
                     AF.Square, accum=lnp_sq[:, 2 * t % 32 + half:2 * t % 32 + half + 1],
                     waits=[w(sV), rq_free], inc=(sA, 1))
            rq_free = w(sA)
        r_ready = w(sV)
        sq_ready = w(sA)
        prog.dma("sync", x1_scr[128 * t:128 * (t + 1), :], r_sb[tb][:, :],
                 waits=[r_ready], sem=sSt)
        r_spill = w(sSt)
        i0c, i1c = 2 * t % 32, 2 * t % 32 + 1
        prog.tt(ALU.add, cols_a[:, 0:1], lnp_sum[:, i0c:i0c + 1],
                lnp_sum[:, i1c:i1c + 1], waits=[r_ready, cols_free], inc=(sV, 1))
        prog.ts(ln_m[:, t:t + 1], cols_a[:, 0:1], 1.0 / D, None, ALU.mult,
                waits=[w(sV)], inc=(sV, 1))
        prog.tt(ALU.add, cols_a[:, 1:2], lnp_sq[:, i0c:i0c + 1],
                lnp_sq[:, i1c:i1c + 1], waits=[sq_ready], inc=(sV, 1))
        prog.ts(cols_a[:, 1:2], cols_a[:, 1:2], 1.0 / D, None, ALU.mult,
                waits=[w(sV)], inc=(sV, 1))
        prog.act(cols_a[:, 2:3], ln_m[:, t:t + 1], AF.Square, waits=[w(sV)],
                 inc=(sA, 1))
        prog.tt(ALU.subtract, cols_a[:, 3:4], cols_a[:, 1:2], cols_a[:, 2:3],
                waits=[w(sA), w(sV)], inc=(sV, 1))
        prog.ts(cols_a[:, 3:4], cols_a[:, 3:4], EPS, None, ALU.add,
                waits=[w(sV)], inc=(sV, 1))
        prog.recip(cols_a[:, 4:5], cols_a[:, 3:4], waits=[w(sV)], inc=(sV, 1))
        prog.act(ln_rs[:, t:t + 1], cols_a[:, 4:5], AF.Sqrt, waits=[w(sV)],
                 inc=(sA, 1))
        rstd_ready = w(sA)
        prog.tt(ALU.mult, ln_mrs[:, t:t + 1], ln_m[:, t:t + 1], ln_rs[:, t:t + 1],
                waits=[rstd_ready], inc=(sV, 1))
        prog.ts(ln_mrs[:, t:t + 1], ln_mrs[:, t:t + 1], -1.0, None, ALU.mult,
                waits=[w(sV)], inc=(sV, 1))
        prog.red(cols_a[:, 5:6], r_sb[tb][:, :], ALU.max, waits=[r_ready],
                 inc=(sV, 1))
        prog.red(cols_a[:, 6:7], r_sb[tb][:, :], ALU.min, waits=[w(sV)],
                 inc=(sV, 1))
        prog.tt(ALU.subtract, cols_a[:, 5:6], cols_a[:, 5:6], ln_m[:, t:t + 1],
                waits=[w(sV)], inc=(sV, 1))
        prog.tt(ALU.subtract, cols_a[:, 6:7], ln_m[:, t:t + 1], cols_a[:, 6:7],
                waits=[w(sV)], inc=(sV, 1))
        prog.tt(ALU.max, cols_a[:, 5:6], cols_a[:, 5:6], cols_a[:, 6:7],
                waits=[w(sV)], inc=(sV, 1))
        prog.tt(ALU.mult, amax1_cols[:, t:t + 1], cols_a[:, 5:6], ln_rs[:, t:t + 1],
                waits=[w(sV), rstd_ready], inc=(sV, 1))
        amax_ready = w(sV)
        prog.tt(ALU.mult, dq1_cols[:, t:t + 1], amax1_cols[:, t:t + 1],
                gq1_col[:, :], waits=[amax_ready, sc1_ready], inc=(sV, 1))
        prog.recip(cols_a[:, 7:8], amax1_cols[:, t:t + 1], waits=[amax_ready],
                   inc=(sV, 1))
        prog.ts(s1_cols[:, t:t + 1], cols_a[:, 7:8], Q8, None, ALU.mult,
                waits=[w(sV)], inc=(sV, 1))
        qsc_ready = w(sV)
        prog.act(x1f[:, :], r_sb[tb][:, :], AF.Identity, bias=ln_mrs[:, t:t + 1],
                 scale=ln_rs[:, t:t + 1], waits=[qsc_ready, x1f_free], inc=(sA, 1))
        r_free[tb] = [r_spill, w(sA)]
        cols_free = w(sA)
        prog.act(rndt[:, :], x1f[:, :], AF.Identity, bias=cst["magic_col"][:, :],
                 scale=s1_cols[:, t:t + 1], waits=[w(sA), rndt_free], inc=(sA, 1))
        x1f_free = w(sA)
        prog.ts(x1q_tm[:, :], rndt[:, :], -MAGIC, None, ALU.add,
                waits=[w(sA), x1qtm_free], inc=(sV, 1))
        rndt_free = w(sV)
        x1q_ready = w(sV)
        for c in range(16):
            prog.mm(x1q_tm[:, 128 * c:128 * (c + 1)], cst["ident_bf"][:, :],
                    ps_tr[(c // 4) % 2][:, 128 * (c % 4):128 * (c % 4 + 1)],
                    True, True,
                    waits=[x1q_ready, const_ld, tr_free[(c // 4) % 2]]
                    if c % 4 == 0 else (),
                    inc=(sT, 1) if c % 4 == 3 else None)
            if c % 4 == 3:
                grp_done = w(sT)
                dst = x1q_fm[:, :].rearrange("p (cc tk) -> p cc tk", tk=D)[
                    :, c - 3:c + 1, 128 * t:128 * (t + 1)]
                prog.act(dst, ps_tr[(c // 4) % 2][:, :], AF.Identity,
                         waits=[grp_done], inc=(sA, 1))
                tr_free[(c // 4) % 2] = w(sA)
        x1qtm_free = w(sT)
        yt_free[tb] = w(sT)
        xot_free[tb] = w(sP)
        tmp_free[tb] = w(sV)
        if t + 2 < NTILE:
            loads_a.append(tile_loads_a(t + 2))
    x1q_done = w(sA)
    r_spills_done = w(sSt)

    if debug:
        prog.dma("gpsimd", dbg_x1q, x1q_fm[:, :], waits=[x1q_done], sem=sSt)
        dcw = [x1q_done, w(sV)]
        prog.dma("gpsimd", dbg_cols[:, 0:16], amax1_cols[:, :], waits=dcw, sem=sSt)
        prog.dma("gpsimd", dbg_cols[:, 16:32], s1_cols[:, :], waits=dcw, sem=sSt)
        prog.dma("gpsimd", dbg_cols[:, 32:48], ln_m[:, :], waits=dcw, sem=sSt)
        prog.dma("gpsimd", dbg_cols[:, 48:64], ln_rs[:, :], waits=dcw, sem=sSt)

    prog.barrier(sB)
    p3a.close()

    if upto <= 31:
        p3ab.close()
        return finish([w(sSt), w(sA), w(sV), w(sT), w(sCast), w(sScan), w(sY)])

    p3b = ExitStack()
    bsb = lambda name, shape, dt: p3b.enter_context(nc.sbuf_tensor(name, shape, dt))
    w1q_sb = [bsb(f"w1q{i}", [128, 16 * 512], BF16) for i in range(2)]
    w1st = [bsb(f"w1st{i}", [128, 512], F32) for i in range(2)]
    w1rnd = bsb("w1rnd", [128, 512], F32)
    b1_sb = bsb("b1_sb", [128, DFF], BF16)
    x2pre = [bsb(f"x2pre{i}", [128, 512], F32) for i in range(2)]
    gt_sb = [bsb(f"gt{i}", [128, 512], BF16) for i in range(2)]
    ps_f1 = [p3b.enter_context(nc.psum_tensor(f"ps_f1{i}", [128, 512], F32))
             for i in range(2)]

    p3a_done = [x1q_done, r_spills_done, w(sT), w(sV), w(sP)]
    for c in range(4):
        prog.dma("gpsimd", b1_sb[:, 2048 * c:2048 * (c + 1)],
                 b1_bc[:, 2048 * c:2048 * (c + 1)], waits=p3a_done, sem=sCast)
    b1_ready = w(sCast)

    w1st_free = [None, None]
    w1rnd_free = None
    w1q_free = [None, None]
    x2_free = [None, None]
    gt_free = [None, None]
    f1_free = [None, None]

    def stream_w1_block(o):
        nonlocal w1rnd_free
        ob = o % 2
        rdys = []
        for c in range(16):
            cs = c % 2
            prog.dma("sync", w1st[cs][:, :],
                     w1T[128 * c:128 * (c + 1), 512 * o:512 * (o + 1)],
                     waits=[w1st_free[cs], w1q_free[ob] if c == 0 else None]
                     + (p3a_done if o == 0 and c == 0 else []), sem=sLd)
            prog.act(w1rnd[:, :], w1st[cs][:, :], AF.Identity,
                     bias=cst["magic_col"][:, :], scale=sw1_col[:, :],
                     waits=[w(sLd), sc1_ready, w1rnd_free], inc=(sA, 1))
            w1st_free[cs] = w(sA)
            prog.ts(w1q_sb[ob][:, 512 * c:512 * (c + 1)], w1rnd[:, :], -MAGIC,
                    None, ALU.add, waits=[w(sA)], inc=(sV, 1))
            w1rnd_free = w(sV)
        return w(sV)

    w1_ready = [stream_w1_block(0)]
    for o in range(16):
        ob = o % 2
        if o + 1 < 16:
            w1_ready.append(stream_w1_block(o + 1))
        for t in range(NTILE):
            ps = ps_f1[t % 2]
            mmw = [w1_ready[o], x1q_done, f1_free[t % 2]]
            for c in range(16):
                prog.mm(x1q_fm[:, D * c + 128 * t:D * c + 128 * (t + 1)],
                        w1q_sb[ob][:, 512 * c:512 * (c + 1)],
                        ps[:, :], c == 0, c == 15,
                        waits=mmw if c == 0 else (),
                        inc=(sT, 1) if c == 15 else None)
            mm_done = w(sT)
            tb2 = t % 2
            prog.stt(x2pre[tb2][:, :], ps[:, :], dq1_cols[:, t:t + 1],
                     b1_sb[:, 512 * o:512 * (o + 1)], ALU.mult, ALU.add,
                     waits=[mm_done, b1_ready, x2_free[tb2]], inc=(sV, 1))
            f1_free[t % 2] = w(sV)
            prog.act(gt_sb[tb2][:, :], x2pre[tb2][:, :], AF.Gelu_apprx_tanh,
                     waits=[w(sV), gt_free[tb2]], inc=(sA, 1))
            x2_free[tb2] = w(sA)
            gelu_done = w(sA)
            prog.red(amax2_parts[:, 16 * t + o:16 * t + o + 1], gt_sb[tb2][:, :],
                     ALU.max, absval=True, waits=[gelu_done], inc=(sV, 1))
            prog.dma("sync", g_scr[128 * t:128 * (t + 1), 512 * o:512 * (o + 1)],
                     gt_sb[tb2][:, :], waits=[gelu_done], sem=sSt)
            gt_free[tb2] = w(sSt)
        w1q_free[ob] = w(sT)
    amax2c = bsb("amax2c", [128, NTILE], F32)
    for t in range(NTILE):
        prog.red(amax2c[:, t:t + 1], amax2_parts[:, 16 * t:16 * (t + 1)],
                 ALU.max, waits=[w(sV)], inc=(sV, 1))
        prog.tt(ALU.mult, dq2_cols[:, t:t + 1], amax2c[:, t:t + 1],
                gq2_col[:, :], waits=[w(sV), sc2_ready], inc=(sV, 1))
        prog.recip(s2_cols[:, t:t + 1], amax2c[:, t:t + 1],
                   waits=[w(sV)], inc=(sV, 1))
        prog.ts(s2_cols[:, t:t + 1], s2_cols[:, t:t + 1], Q8, None, ALU.mult,
                waits=[w(sV)], inc=(sV, 1))
    s2_ready = w(sV)
    g_stores_done = w(sSt)

    if debug:
        prog.dma("gpsimd", dbg_cols[:, 64:80], s2_cols[:, :], waits=[s2_ready], sem=sSt)
        prog.dma("gpsimd", dbg_cols[:, 80:96], dq2_cols[:, :], waits=[s2_ready], sem=sSt)

    prog.barrier(sB)
    p3b.close()
    p3ab.close()

    if upto <= 32:
        return finish([w(sSt), w(sA), w(sV), w(sT), w(sCast), w(sScan), w(sY)])

    NB = 8
    p3c = ExitStack()
    csb = lambda name, shape, dt: p3c.enter_context(nc.sbuf_tensor(name, shape, dt))
    oacc = csb("oacc", [128, NTILE * 1024], F32)
    w2q_sb = [csb(f"w2q{i}", [128, 8 * 1024], BF16) for i in range(2)]
    w2st = [csb(f"w2st{i}", [128, 1024], F32) for i in range(2)]
    rnd_scr = csb("rnd_scr", [128, 2048], F32)
    g_in = [csb(f"g_in{i}", [128, 1024], BF16) for i in range(2)]
    x2q_tm = csb("x2q_tm", [128, 1024], BF16)
    x2q_fm = [csb(f"x2q_fm{i}", [128, 1024], BF16) for i in range(2)]
    rres = [csb(f"rres{i}", [128, 1024], BF16) for i in range(2)]
    xrec = csb("xrec", [128, 1024], F32)
    tmp2 = csb("tmp2", [128, 1024], F32)
    b2_sb = csb("b2_sb", [128, D], F32)
    r2h0 = csb("r2h0", [128, 1024], BF16)
    r2h1 = csb("r2h1", [128, 1024], F32)
    outst = [csb(f"outst{i}", [128, 1024], F32) for i in range(2)]
    ln2_sum = csb("ln2_sum", [128, 32], F32)
    ln2_sq = csb("ln2_sq", [128, 32], F32)
    cols_c = csb("cols_c", [128, 8], F32)
    sq2_scr = csb("sq2_scr", [128, 1024], BF16)
    r2_scr = nc.dram_tensor("r2_scr", [NTOK, 1024], BF16, kind=scratch_kind).ap()
    ps_f2 = [p3c.enter_context(nc.psum_tensor(f"ps_f2{i}", [128, 1024], F32))
             for i in range(2)]
    ps_t2 = [p3c.enter_context(nc.psum_tensor(f"ps_t2{i}", [128, 512], F32))
             for i in range(2)]

    p3b_done = [g_stores_done, s2_ready, w(sT), w(sA), w(sP)]
    prog.dma("sync", b2_sb[:, :], b2_bc, waits=p3b_done, sem=sLd)
    b2_ready = w(sLd)

    w2st_free = [None, None]
    w2rnd_free = None
    x2rnd_free = None
    w2q_free = [None, None]
    oacc_free = [None] * NTILE
    gin_free = [None, None]
    x2qtm_free = None
    x2qfm_free = [None, None]
    t2_free = [None, None]
    f2_free = [None, None]
    rres_free = [None, None]
    xrec_free = None
    tmp2_free = None
    r2h0_free = None
    r2h1_free = None
    outst_free = [None, None]
    sq2_free = None
    colsc_free = None

    def stream_w2_block(h2, b):
        nonlocal w2rnd_free
        wb = (h2 * NB + b) % 2
        for c in range(8):
            cs = c % 2
            row0 = 1024 * b + 128 * c
            prog.dma("sync", w2st[cs][:, :],
                     w2T[row0:row0 + 128, 1024 * h2:1024 * (h2 + 1)],
                     waits=[w2st_free[cs], w2q_free[wb] if c == 0 else None]
                     + (p3b_done if h2 + b == 0 and c == 0 else []), sem=sLd)
            prog.act(rnd_scr[:, 0:1024], w2st[cs][:, :], AF.Identity,
                     bias=cst["magic_col"][:, :], scale=sw2_col[:, :],
                     waits=[w(sLd), sc2_ready, w2rnd_free], inc=(sA, 1))
            w2st_free[cs] = w(sA)
            prog.ts(w2q_sb[wb][:, 1024 * c:1024 * (c + 1)], rnd_scr[:, 0:1024],
                    -MAGIC, None, ALU.add, waits=[w(sA)], inc=(sV, 1))
            w2rnd_free = w(sV)
        return w(sV)

    w2_ready = {}
    w2_ready[(0, 0)] = stream_w2_block(0, 0)
    seq = [(h2, b) for h2 in range(2) for b in range(NB)]
    for si, (h2, b) in enumerate(seq):
        wb = (h2 * NB + b) % 2
        if si + 1 < len(seq):
            w2_ready[seq[si + 1]] = stream_w2_block(*seq[si + 1])
        for t in range(NTILE):
            tb = t % 2
            prog.dma("sync", g_in[tb][:, :],
                     g_scr[128 * t:128 * (t + 1), 1024 * b:1024 * (b + 1)],
                     waits=[gin_free[tb], g_stores_done], sem=sY)
            prog.act(rnd_scr[:, 1024:2048], g_in[tb][:, :], AF.Identity,
                     bias=cst["magic_col"][:, :], scale=s2_cols[:, t:t + 1],
                     waits=[w(sY), s2_ready, x2rnd_free], inc=(sA, 1))
            gin_free[tb] = w(sA)
            prog.ts(x2q_tm[:, :], rnd_scr[:, 1024:2048], -MAGIC, None, ALU.add,
                    waits=[w(sA), x2qtm_free], inc=(sV, 1))
            x2rnd_free = w(sV)
            x2q_ready = w(sV)
            fmb = t % 2
            for c in range(8):
                prog.mm(x2q_tm[:, 128 * c:128 * (c + 1)], cst["ident_bf"][:, :],
                        ps_t2[(c // 4) % 2][:, 128 * (c % 4):128 * (c % 4 + 1)],
                        True, True,
                        waits=[x2q_ready, t2_free[(c // 4) % 2]] if c % 4 == 0 else (),
                        inc=(sT, 1) if c % 4 == 3 else None)
                if c % 4 == 3:
                    prog.act(x2q_fm[fmb][:, 512 * (c // 4):512 * (c // 4 + 1)],
                             ps_t2[(c // 4) % 2][:, :], AF.Identity,
                             waits=[w(sT), x2qfm_free[fmb] if c == 3 else None],
                             inc=(sA, 1))
                    t2_free[(c // 4) % 2] = w(sA)
            x2qtm_free = w(sT)
            x2fm_ready = w(sA)
            ps = ps_f2[tb]
            mmw = [x2fm_ready, w2_ready[(h2, b)], f2_free[tb]]
            for c in range(8):
                for o2 in range(2):
                    prog.mm(x2q_fm[fmb][:, 128 * c:128 * (c + 1)],
                            w2q_sb[wb][:, 1024 * c + 512 * o2:1024 * c + 512 * (o2 + 1)],
                            ps[:, 512 * o2:512 * (o2 + 1)],
                            c == 0, c == 7,
                            waits=mmw if (c == 0 and o2 == 0) else (),
                            inc=(sT, 1) if (c == 7 and o2 == 1) else None)
            mm_done = w(sT)
            x2qfm_free[fmb] = mm_done
            oreg = oacc[:, 1024 * t:1024 * (t + 1)]
            if b == 0:
                prog.copy(oreg, ps[:, :], waits=[mm_done, oacc_free[t]], inc=(sV, 1))
            else:
                prog.tt(ALU.add, oreg, oreg, ps[:, :], waits=[mm_done], inc=(sV, 1))
            f2_free[tb] = w(sV)
            if b == NB - 1 and t == NTILE - 1:
                w2q_free[wb] = w(sT)
            elif t == NTILE - 1:
                w2q_free[wb] = w(sT)
            if b < NB - 1:
                continue
            acc_done = w(sV)
            prog.dma("sync", rres[tb][:, :],
                     x1_scr[128 * t:128 * (t + 1), 1024 * h2:1024 * (h2 + 1)],
                     waits=[rres_free[tb], r_spills_done], sem=sY)
            prog.act(xrec[:, :], rres[tb][:, :], AF.Identity,
                     bias=ln_mrs[:, t:t + 1], scale=ln_rs[:, t:t + 1],
                     waits=[w(sY), xrec_free], inc=(sA, 1))
            prog.tt(ALU.add, tmp2[:, :], xrec[:, :], b2_sb[:, 1024 * h2:1024 * (h2 + 1)],
                    waits=[w(sA), b2_ready, tmp2_free], inc=(sP, 1), eng="gpsimd")
            xrec_free = w(sP)
            rres_free[tb] = w(sP)
            if h2 == 0:
                prog.stt(r2h0[:, :], oreg, dq2_cols[:, t:t + 1], tmp2[:, :],
                         ALU.mult, ALU.add, accum=ln2_sum[:, 2 * t:2 * t + 1],
                         waits=[acc_done, w(sP), r2h0_free], inc=(sV, 1))
                tmp2_free = w(sV)
                oacc_free[t] = w(sV)
                r2_ready = w(sV)
                prog.act(sq2_scr[:, :], r2h0[:, :], AF.Square,
                         accum=ln2_sq[:, 2 * t:2 * t + 1],
                         waits=[r2_ready, sq2_free], inc=(sA, 1))
                sq2_free = w(sA)
                prog.dma("sync", r2_scr[128 * t:128 * (t + 1), :], r2h0[:, :],
                         waits=[r2_ready], sem=sSt)
                r2h0_free = w(sSt)
            else:
                prog.stt(r2h1[:, :], oreg, dq2_cols[:, t:t + 1], tmp2[:, :],
                         ALU.mult, ALU.add, accum=ln2_sum[:, 2 * t + 1:2 * t + 2],
                         waits=[acc_done, w(sP), r2h1_free], inc=(sV, 1))
                tmp2_free = w(sV)
                r2_ready = w(sV)
                prog.act(sq2_scr[:, :], r2h1[:, :], AF.Square,
                         accum=ln2_sq[:, 2 * t + 1:2 * t + 2],
                         waits=[r2_ready, sq2_free], inc=(sA, 1))
                sq2_stat = w(sA)
                sq2_free = sq2_stat
                prog.tt(ALU.add, cols_c[:, 0:1], ln2_sum[:, 2 * t:2 * t + 1],
                        ln2_sum[:, 2 * t + 1:2 * t + 2],
                        waits=[r2_ready, colsc_free], inc=(sV, 1))
                prog.ts(cols_c[:, 0:1], cols_c[:, 0:1], 1.0 / D, None, ALU.mult,
                        waits=[w(sV)], inc=(sV, 1))
                prog.tt(ALU.add, cols_c[:, 1:2], ln2_sq[:, 2 * t:2 * t + 1],
                        ln2_sq[:, 2 * t + 1:2 * t + 2], waits=[sq2_stat],
                        inc=(sV, 1))
                prog.ts(cols_c[:, 1:2], cols_c[:, 1:2], 1.0 / D, None, ALU.mult,
                        waits=[w(sV)], inc=(sV, 1))
                prog.act(cols_c[:, 2:3], cols_c[:, 0:1], AF.Square,
                         waits=[w(sV)], inc=(sA, 1))
                prog.tt(ALU.subtract, cols_c[:, 3:4], cols_c[:, 1:2], cols_c[:, 2:3],
                        waits=[w(sA), w(sV)], inc=(sV, 1))
                prog.ts(cols_c[:, 3:4], cols_c[:, 3:4], EPS, None, ALU.add,
                        waits=[w(sV)], inc=(sV, 1))
                prog.recip(cols_c[:, 4:5], cols_c[:, 3:4], waits=[w(sV)], inc=(sV, 1))
                prog.act(cols_c[:, 4:5], cols_c[:, 4:5], AF.Sqrt, waits=[w(sV)],
                         inc=(sA, 1))
                prog.tt(ALU.mult, cols_c[:, 5:6], cols_c[:, 0:1], cols_c[:, 4:5],
                        waits=[w(sA)], inc=(sV, 1))
                prog.ts(cols_c[:, 5:6], cols_c[:, 5:6], -1.0, None, ALU.mult,
                        waits=[w(sV)], inc=(sV, 1))
                ln2_ready = w(sV)
                prog.dma("sync", r2h0[:, :], r2_scr[128 * t:128 * (t + 1), :],
                         waits=[r2h0_free], sem=sY)
                prog.act(outst[0][:, :], r2h0[:, :], AF.Identity,
                         bias=cols_c[:, 5:6], scale=cols_c[:, 4:5],
                         waits=[w(sY), ln2_ready, outst_free[0]], inc=(sA, 1))
                r2h0_free = w(sA)
                prog.dma("sync", out[128 * t:128 * (t + 1), 0:1024], outst[0][:, :],
                         waits=[w(sA)], sem=sSt)
                outst_free[0] = w(sSt)
                prog.act(outst[1][:, :], r2h1[:, :], AF.Identity,
                         bias=cols_c[:, 5:6], scale=cols_c[:, 4:5],
                         waits=[ln2_ready, outst_free[1]], inc=(sA, 1))
                r2h1_free = w(sA)
                colsc_free = w(sA)
                prog.dma("sync", out[128 * t:128 * (t + 1), 1024:2048], outst[1][:, :],
                         waits=[w(sA)], sem=sSt)
                outst_free[1] = w(sSt)

    prog.barrier(sB)
    p3c.close()
    return finish([w(sSt), w(sA), w(sV), w(sT), w(sCast), w(sScan), w(sY),
                   w(sLd), w(sP)])


def host_prepare(inputs):
    x = np.asarray(inputs["L0_attn_x"], dtype=np.float32)
    w_attn = np.asarray(inputs["L0_attn_w_attn"], dtype=np.float32)
    b_attn = np.asarray(inputs["L0_attn_b_attn"], dtype=np.float32)
    w_proj = np.asarray(inputs["L0_attn_w_proj"], dtype=np.float32)
    b_proj = np.asarray(inputs["L0_attn_b_proj"], dtype=np.float32)
    W1 = np.asarray(inputs["L0_ffn_W1"], dtype=np.float32)
    B1 = np.asarray(inputs["L0_ffn_B1"], dtype=np.float32)
    W2 = np.asarray(inputs["L0_ffn_W2"], dtype=np.float32)
    B2 = np.asarray(inputs["L0_ffn_B2"], dtype=np.float32)

    shared = dict(
        wqT=np.ascontiguousarray(w_attn[0:D].T),
        wkT=np.ascontiguousarray(w_attn[D:2 * D].T),
        wvT=np.ascontiguousarray(w_attn[2 * D:3 * D].T),
        bq_col=np.ascontiguousarray(b_attn[0:D].reshape(H, 128).T),
        bk_col=np.ascontiguousarray(b_attn[D:2 * D].reshape(H, 128).T),
        bv_bc=np.ascontiguousarray(np.broadcast_to(b_attn[2 * D:3 * D], (128, D))),
        wprojT=np.ascontiguousarray(w_proj.T),
        bproj_bc=np.ascontiguousarray(np.broadcast_to(b_proj, (128, D))),
        w1T=np.ascontiguousarray(W1.T),
        b1_bc=np.ascontiguousarray(np.broadcast_to(B1, (128, DFF))),
        w2T=np.ascontiguousarray(W2.T),
        b2_bc=np.ascontiguousarray(np.broadcast_to(B2, (128, D))),
        ident_bf=np.eye(128, dtype=BFNP),
        ident_f=np.eye(128, dtype=np.float32),
        ones_col_bf=np.ones((128, 1), BFNP),
        ones_row_f=np.ones((1, 128), np.float32),
        ones_row_bf=np.ones((1, 128), BFNP),
    )

    tri = np.triu(np.ones((128, 128), np.float32))
    zeros = np.zeros((128, 128), np.float32)
    ones = np.ones((128, 128), np.float32)
    mask_p = {}
    for p in range(2):
        tiles = [(tri if cp % 2 == p else (zeros if p == 0 else ones))
                 for cp in range(8)]
        mask_p[p] = np.concatenate(tiles, axis=0).astype(BFNP)

    in_maps = []
    own_idx = {}
    for core in range(N_CORES):
        b, p = core // 2, core % 2
        blocks = np.arange(p, 32, 2)
        tok_idx = (blocks[:, None] * 128 + np.arange(128)[None, :]).reshape(-1)
        own_idx[core] = tok_idx
        xb = x[b]
        x_own = xb[tok_idx]
        m = dict(shared)
        m["x_fm"] = np.ascontiguousarray(xb.T)
        m["x_own_fm"] = np.ascontiguousarray(x_own.T)
        m["x_own_tm"] = np.ascontiguousarray(x_own)
        m["masks"] = mask_p[p]
        in_maps.append(m)
    return in_maps, own_idx


_PROG_CACHE = {}


def kernel(**inputs):
    in_maps, own_idx = host_prepare(inputs)
    if "full" not in _PROG_CACHE:
        _PROG_CACHE["full"] = build_program(upto=99, debug=False)
    nc = _PROG_CACHE["full"]
    res = run_bass_kernel_spmd(nc, in_maps, core_ids=list(range(N_CORES)))
    out = np.zeros((4, T, D), dtype=np.float32)
    for core in range(N_CORES):
        b = core // 2
        out[b, own_idx[core]] = res.results[core]["out"]
    return out


# version 04
# speedup vs baseline: 1.2808x; 1.2808x over previous
import numpy as np
import ml_dtypes
from contextlib import ExitStack

import concourse.bass as bass
import concourse.mybir as mybir
from concourse.bass_utils import run_bass_kernel_spmd

F32 = mybir.dt.float32
BF16 = mybir.dt.bfloat16
AF = mybir.ActivationFunctionType
ALU = mybir.AluOpType
AX = mybir.AxisListType.X
BFNP = ml_dtypes.bfloat16

D = 2048
T = 4096
H = 16
HD = 128
NTOK = 2048
NTILE = 16
NJS = 4
DFF = 8192
MAGIC = 12582912.0
INV_SQRT_HD = float(1.0 / np.sqrt(128.0))
Q8 = 127.0
INV_Q8SQ = float(1.0 / (127.0 * 127.0))
EPS = 1e-5
N_CORES = 8

I0 = [0, 0, 1, 1, 2, 2, 3, 3]


class Sem:
    __slots__ = ("h", "n")

    def __init__(self, handle):
        self.h = handle
        self.n = 0


class Prog:

    ENGINES = ("sync", "scalar", "vector", "tensor", "gpsimd")

    def __init__(self, nc):
        self.nc = nc
        self.ops = {e: [] for e in self.ENGINES}
        self.waited = {e: {} for e in self.ENGINES}

    def emit(self, eng, fn, waits=(), inc=None):
        wl = []
        for item in waits:
            if item is None:
                continue
            s, v = item
            assert v <= s.n, f"wait on future sem value {v} > {s.n}"
            key = id(s)
            if self.waited[eng].get(key, -1) >= v:
                continue
            self.waited[eng][key] = v
            wl.append((s.h, v))
        post = None
        if inc is not None:
            s, amt = inc
            s.n += amt
            post = (s.h, amt)
        wl = tuple(wl)

        def f(e, _fn=fn, _wl=wl, _post=post):
            for h, v in _wl:
                e.wait_ge(h, v)
            ins = _fn(e)
            if _post is not None:
                ins.then_inc(_post[0], _post[1])
        self.ops[eng].append(f)

    def dma(self, q, dst, src, waits=(), sem=None, slow_ok=False):
        self.emit(q,
                  lambda e: e.dma_start(dst, src, allow_slow_non_contiguous=slow_ok),
                  waits, (sem, 16) if sem is not None else None)

    def mm(self, lhsT, rhs, out, start, stop, waits=(), inc=None):
        self.emit("tensor",
                  lambda e: e.matmul(out, lhsT, rhs, start=start, stop=stop),
                  waits, inc)

    def act(self, out, in_, func, bias=0.0, scale=1.0, accum=None, waits=(), inc=None):
        self.emit("scalar",
                  lambda e: e.activation(out, in_, func, bias=bias, scale=scale,
                                         accum_out=accum),
                  waits, inc)

    def tt(self, op, out, a, b, waits=(), inc=None, eng="vector"):
        self.emit(eng, lambda e: e.tensor_tensor(out, a, b, op), waits, inc)

    def ts(self, out, in0, s1, s2, op0, op1=ALU.bypass, accum=None, waits=(), inc=None,
           eng="vector"):
        self.emit(eng,
                  lambda e: e.tensor_scalar(out, in0, s1, s2, op0, op1,
                                            accum_out=accum),
                  waits, inc)

    def stt(self, out, in0, scalar, in1, op0, op1, accum=None, waits=(), inc=None):
        self.emit("vector",
                  lambda e: e.scalar_tensor_tensor(out, in0, scalar, in1, op0, op1,
                                                   accum_out=accum),
                  waits, inc)

    def red(self, out, in_, op, absval=False, waits=(), inc=None):
        self.emit("vector",
                  lambda e: e.tensor_reduce(out, in_, AX, op,
                                            apply_absolute_value=absval),
                  waits, inc)

    def copy(self, out, in_, waits=(), inc=None, eng="vector"):
        self.emit(eng, lambda e: e.tensor_copy(out, in_), waits, inc)

    def recip(self, out, in_, waits=(), inc=None):
        self.emit("vector", lambda e: e.reciprocal(out, in_), waits, inc)

    def memset(self, ap, val, waits=(), inc=None, eng="vector"):
        self.emit(eng, lambda e: e.memset(ap, val), waits, inc)

    def barrier(self, bsem):
        base = bsem.n
        for eng in self.ENGINES:
            self.emit(eng, lambda e: e.nop(), (), (bsem, 1))
        for eng in self.ENGINES:
            self.emit(eng, lambda e: e.nop(), ((bsem, base + 5),), None)

    def replay(self):
        nc = self.nc
        with nc.Block() as block:
            @block.sync
            def _(e):
                for f in self.ops["sync"]:
                    f(e)

            @block.scalar
            def _(e):
                for f in self.ops["scalar"]:
                    f(e)

            @block.vector
            def _(e):
                for f in self.ops["vector"]:
                    f(e)

            @block.tensor
            def _(e):
                for f in self.ops["tensor"]:
                    f(e)

            @block.gpsimd
            def _(e):
                for f in self.ops["gpsimd"]:
                    f(e)


def w(sem):
    return (sem, sem.n)


class Scanner:

    def __init__(self, prog, slots, parts, sems):
        self.prog = prog
        self.slots = slots
        self.parts = parts
        self.sScan, self.sV = sems
        self.work = []
        self.i = 0
        self.slot_free = [None] * len(self.slots)

    def step(self):
        if self.i >= len(self.work):
            return
        src_ap, col = self.work[self.i]
        slot = self.i % len(self.slots)
        p = self.prog
        waits = [self.slot_free[slot]] if self.slot_free[slot] else []
        p.dma("sync", self.slots[slot][:, :], src_ap, waits=waits, sem=self.sScan)
        p.red(self.parts[:, col:col + 1], self.slots[slot][:, :], ALU.max,
              absval=True, waits=[w(self.sScan)], inc=(self.sV, 1))
        self.slot_free[slot] = w(self.sV)
        self.i += 1

    def drain(self):
        while self.i < len(self.work):
            self.step()
        return w(self.sV)


def build_program(upto=99, debug=False):
    nc = bass.Bass("TRN2", target_bir_lowering=False)
    scratch_kind = "ExternalOutput" if debug else "Internal"

    x_fm = nc.dram_tensor("x_fm", [D, T], F32, kind="ExternalInput").ap()
    x_own_fm = nc.dram_tensor("x_own_fm", [D, NTOK], F32, kind="ExternalInput").ap()
    x_own_tm = nc.dram_tensor("x_own_tm", [NTOK, D], F32, kind="ExternalInput").ap()
    wqT = nc.dram_tensor("wqT", [D, D], F32, kind="ExternalInput").ap()
    wkT = nc.dram_tensor("wkT", [D, D], F32, kind="ExternalInput").ap()
    wvT = nc.dram_tensor("wvT", [D, D], F32, kind="ExternalInput").ap()
    bq_col = nc.dram_tensor("bq_col", [128, H], F32, kind="ExternalInput").ap()
    bk_col = nc.dram_tensor("bk_col", [128, H], F32, kind="ExternalInput").ap()
    bv_bc = nc.dram_tensor("bv_bc", [128, D], F32, kind="ExternalInput").ap()
    wprojT = nc.dram_tensor("wprojT", [D, D], F32, kind="ExternalInput").ap()
    bproj_bc = nc.dram_tensor("bproj_bc", [128, D], F32, kind="ExternalInput").ap()
    w1T = nc.dram_tensor("w1T", [D, DFF], F32, kind="ExternalInput").ap()
    b1_bc = nc.dram_tensor("b1_bc", [128, DFF], F32, kind="ExternalInput").ap()
    w2T = nc.dram_tensor("w2T", [DFF, D], F32, kind="ExternalInput").ap()
    b2_bc = nc.dram_tensor("b2_bc", [128, D], F32, kind="ExternalInput").ap()
    masks = nc.dram_tensor("masks", [8 * 128, 128], BF16, kind="ExternalInput").ap()
    ident_bf = nc.dram_tensor("ident_bf", [128, 128], BF16, kind="ExternalInput").ap()
    ident_f = nc.dram_tensor("ident_f", [128, 128], F32, kind="ExternalInput").ap()
    ones_col_bf = nc.dram_tensor("ones_col_bf", [128, 1], BF16, kind="ExternalInput").ap()
    ones_row_f = nc.dram_tensor("ones_row_f", [1, 128], F32, kind="ExternalInput").ap()
    ones_row_bf = nc.dram_tensor("ones_row_bf", [1, 128], BF16, kind="ExternalInput").ap()

    out = nc.dram_tensor("out", [NTOK, D], F32, kind="ExternalOutput").ap()

    q_scr = nc.dram_tensor("q_scr", [H * 128, NTOK], BF16, kind=scratch_kind).ap()
    k_scr = nc.dram_tensor("k_scr", [H * 128, T], BF16, kind=scratch_kind).ap()
    v_scr = nc.dram_tensor("v_scr", [H * T, HD], BF16, kind=scratch_kind).ap()
    y_scr = nc.dram_tensor("y_scr", [NTOK, D], BF16, kind=scratch_kind).ap()
    x1_scr = nc.dram_tensor("x1_scr", [NTOK, D], BF16, kind=scratch_kind).ap()
    g_scr = nc.dram_tensor("g_scr", [NTOK, DFF], BF16, kind=scratch_kind).ap()
    if debug:
        dbg_sc = nc.dram_tensor("dbg_sc", [128, 16], F32, kind="ExternalOutput").ap()
        dbg_cols = nc.dram_tensor("dbg_cols", [128, 96], F32, kind="ExternalOutput").ap()
        dbg_x1q = nc.dram_tensor("dbg_x1q", [128, H * D], BF16,
                                 kind="ExternalOutput").ap()

    prog = Prog(nc)
    ctx = ExitStack()
    sb = lambda name, shape, dt: ctx.enter_context(nc.sbuf_tensor(name, shape, dt))
    mksem = lambda name: Sem(ctx.enter_context(nc.semaphore(name)))

    cst = {
        "ident_bf": sb("c_ident_bf", [128, 128], BF16),
        "ident_f": sb("c_ident_f", [128, 128], F32),
        "ones_col_bf": sb("c_ones_col_bf", [128, 1], BF16),
        "ones_row_f": sb("c_ones_row_f", [1, 128], F32),
        "ones_row_bf": sb("c_ones_row_bf", [1, 128], BF16),
        "masks": sb("c_masks", [128, 8 * 128], BF16),
        "magic_col": sb("c_magic_col", [128, 1], F32),
    }
    sw1_col = sb("sw1_col", [128, 1], F32)
    gq1_col = sb("gq1_col", [128, 1], F32)
    sw2_col = sb("sw2_col", [128, 1], F32)
    gq2_col = sb("gq2_col", [128, 1], F32)
    amax1_cols = sb("amax1_cols", [128, NTILE], F32)
    dq1_cols = sb("dq1_cols", [128, NTILE], F32)
    s1_cols = sb("s1_cols", [128, NTILE], F32)
    amax2_parts = sb("amax2_parts", [128, NTILE * 16], F32)
    dq2_cols = sb("dq2_cols", [128, NTILE], F32)
    s2_cols = sb("s2_cols", [128, NTILE], F32)
    negc_cols = sb("negc_cols", [128, H], F32)

    p_small = ctx.enter_context(nc.psum_tensor("p_small", [128, 512], F32))

    sLd = mksem("sLd")
    sCast = mksem("sCast")
    sSt = mksem("sSt")
    sT = mksem("sT")
    sV = mksem("sV")
    sA = mksem("sA")
    sP = mksem("sP")
    sB = mksem("sB")
    sScan = mksem("sScan")
    sY = mksem("sY")

    prog.dma("sync", cst["ident_bf"][:, :], ident_bf, sem=sLd)
    prog.dma("sync", cst["ident_f"][:, :], ident_f, sem=sLd)
    prog.dma("sync", cst["ones_col_bf"][:, :], ones_col_bf, sem=sLd)
    prog.dma("sync", cst["ones_row_f"][:, :], ones_row_f, sem=sLd)
    prog.dma("sync", cst["ones_row_bf"][:, :], ones_row_bf, sem=sLd)
    prog.dma("sync", cst["masks"][:, :].rearrange("p (c n) -> p c n", n=128),
             masks.rearrange("(c p) n -> p c n", p=128), sem=sLd)
    prog.memset(cst["magic_col"][:, :], MAGIC, inc=(sV, 1))
    const_ld = w(sLd)

    p1 = ExitStack()
    p1sb = lambda name, shape, dt: p1.enter_context(nc.sbuf_tensor(name, shape, dt))
    wgt_sb = p1sb("wgt_sb", [128, H * D], BF16)
    xg = [p1sb(f"xg{i}", [128, 16 * 1024], BF16) for i in range(2)]
    vst = [p1sb(f"vst{i}", [128, 1024], BF16) for i in range(4)]
    kst = [p1sb(f"kst{i}", [128, 512], BF16) for i in range(4)]
    bias_sb = p1sb("bias_sb", [128, D + 2 * H], F32)
    scan_slots = [p1sb(f"scan{i}", [128, 2048], F32) for i in range(4)]
    wparts = p1sb("wparts", [128, 128], F32)
    pv = [p1.enter_context(nc.psum_tensor(f"pv{i}", [128, 1024], F32)) for i in range(2)]
    pk = [p1.enter_context(nc.psum_tensor(f"pk{i}", [128, 512], F32)) for i in range(2)]

    prog.dma("sync", bias_sb[:, 0:D], bv_bc, sem=sLd)
    prog.dma("sync", bias_sb[:, D:D + H], bq_col, sem=sLd)
    prog.dma("sync", bias_sb[:, D + H:D + 2 * H], bk_col, sem=sLd)
    bias_ld = w(sLd)

    scanner = Scanner(prog, scan_slots, wparts, (sScan, sV))
    for t in range(64):
        r, c = t % 16, t // 16
        scanner.work.append(
            (w1T[128 * r:128 * (r + 1), 2048 * c:2048 * (c + 1)], t))
    for t in range(64):
        scanner.work.append((w2T[128 * t:128 * (t + 1), :], 64 + t))

    def load_weight_bf16(dst_sb, srcT, waits):
        for c in range(16):
            prog.dma("gpsimd", dst_sb[:, D * c:D * (c + 1)],
                     srcT[128 * c:128 * (c + 1), :], waits=waits, sem=sCast)
        return w(sCast)

    def load_x_group(slot, src, col0, waits):
        for c in range(16):
            prog.dma("gpsimd", xg[slot][:, 1024 * c:1024 * (c + 1)],
                     src[128 * c:128 * (c + 1), col0:col0 + 1024], waits=waits,
                     sem=sCast)
        return w(sCast)

    wv_ready = load_weight_bf16(wgt_sb, wvT, [])
    xg_free = [None, None]
    vst_free = [None] * 4
    pv_free = [None, None]
    grp_loaded = [load_x_group(0, x_fm, 0, []), load_x_group(1, x_fm, 1024, [])]
    for grp in range(4):
        for blk in range(8):
            b2 = grp * 8 + blk
            for oh in range(2):
                i2 = b2 * 2 + oh
                ps = pv[i2 % 2]
                mmw = [grp_loaded[grp], wv_ready, pv_free[i2 % 2]]
                for c in range(16):
                    for o2 in range(2):
                        last = (c == 15 and o2 == 1)
                        prog.mm(
                            xg[grp % 2][:, 1024 * c + 128 * blk:1024 * c + 128 * (blk + 1)],
                            wgt_sb[:, D * c + 1024 * oh + 512 * o2:
                                   D * c + 1024 * oh + 512 * (o2 + 1)],
                            ps[:, 512 * o2:512 * (o2 + 1)],
                            c == 0, c == 15,
                            waits=mmw if (c == 0 and o2 == 0) else (),
                            inc=(sT, 1) if last else None)
                mm_done = w(sT)
                vslot = i2 % 4
                prog.tt(ALU.add, vst[vslot][:, :], ps[:, :],
                        bias_sb[:, 1024 * oh:1024 * (oh + 1)],
                        waits=[mm_done, bias_ld, vst_free[vslot]], inc=(sV, 1))
                pv_free[i2 % 2] = w(sV)
                ev_done = w(sV)
                src3 = vst[vslot][:, :].rearrange("p (h j) -> p h j", j=128)
                dst3 = v_scr.rearrange("(h t) j -> t h j", h=H)
                for hh in range(2):
                    h0 = oh * 8 + hh * 4
                    prog.dma("sync",
                             dst3[128 * b2:128 * (b2 + 1), h0:h0 + 4, :],
                             src3[:, 4 * hh:4 * (hh + 1), :],
                             waits=[ev_done], sem=sSt)
                vst_free[vslot] = w(sSt)
                scanner.step()
        if grp + 2 < 4:
            xg_free[grp % 2] = w(sT)
            grp_loaded.append(load_x_group(grp % 2, x_fm, 1024 * (grp + 2),
                                           [xg_free[grp % 2]]))
    v_tensor_done = w(sT)
    v_stores_done = w(sSt)

    def emit_kq(srcT, src_x, ngrp, bias_off, dst_scr, prev_done,
                ps_init=None, st_init=None):
        wrdy = load_weight_bf16(wgt_sb, srcT, [prev_done])
        st_free = [st_init] * 4
        ps_free = [ps_init, ps_init]
        loaded = [load_x_group(0, src_x, 0, [prev_done]),
                  load_x_group(1, src_x, 1024, [prev_done])]
        cnt = 0
        for grp in range(ngrp):
            for h in range(H):
                for half in range(2):
                    ps = pk[cnt % 2]
                    mmw = [loaded[grp], wrdy, ps_free[cnt % 2]]
                    for c in range(16):
                        prog.mm(
                            wgt_sb[:, D * c + 128 * h:D * c + 128 * (h + 1)],
                            xg[grp % 2][:, 1024 * c + 512 * half:1024 * c + 512 * (half + 1)],
                            ps[:, :], c == 0, c == 15,
                            waits=mmw if c == 0 else (),
                            inc=(sT, 1) if c == 15 else None)
                    mm_done = w(sT)
                    slot = cnt % 4
                    prog.act(kst[slot][:, :], ps[:, :], AF.Identity,
                             bias=bias_sb[:, bias_off + h:bias_off + h + 1],
                             waits=[mm_done, bias_ld, st_free[slot]], inc=(sA, 1))
                    ps_free[cnt % 2] = w(sA)
                    ev_done = w(sA)
                    prog.dma("sync",
                             dst_scr[128 * h:128 * (h + 1),
                                     1024 * grp + 512 * half:1024 * grp + 512 * (half + 1)],
                             kst[slot][:, :], waits=[ev_done], sem=sSt)
                    st_free[slot] = w(sSt)
                    cnt += 1
                    scanner.step()
            if grp + 2 < ngrp:
                gfree = w(sT)
                loaded.append(load_x_group(grp % 2, src_x, 1024 * (grp + 2), [gfree]))
        return w(sT), w(sSt)

    k_tensor_done, k_stores_done = emit_kq(wkT, x_fm, 4, D + H, k_scr, v_tensor_done)
    k_evac_done = w(sA)
    q_tensor_done, q_stores_done = emit_kq(wqT, x_own_fm, 2, D, q_scr, k_tensor_done,
                                           ps_init=k_evac_done, st_init=k_stores_done)

    scan_red_done = scanner.drain()

    def finish_scan(tag, part_off, s_col, gq_col):
        col = p1sb(f"{tag}_col", [128, 1], F32)
        g1 = p1sb(f"{tag}_g1", [1, 1], F32)
        g2 = p1sb(f"{tag}_g2", [1, 2], F32)
        prog.red(col[:, :], wparts[:, part_off:part_off + 64], ALU.max,
                 waits=[scan_red_done], inc=(sV, 1))
        vdone = w(sV)
        prog.mm(col[:, :], cst["ident_f"][:, :], p_small[0:1, 0:128], True, True,
                waits=[const_ld, vdone], inc=(sT, 1))
        prog.red(g1[:, :], p_small[0:1, 0:128], ALU.max, waits=[w(sT)], inc=(sV, 1))
        prog.recip(g2[:, 0:1], g1[:, :], waits=[w(sV)], inc=(sV, 1))
        prog.ts(g2[:, 0:1], g2[:, 0:1], Q8, None, ALU.mult, waits=[w(sV)], inc=(sV, 1))
        prog.ts(g2[:, 1:2], g1[:, :], INV_Q8SQ, None, ALU.mult, waits=[w(sV)],
                inc=(sV, 1))
        prog.mm(cst["ones_row_f"][:, :], g2[:, :], p_small[:, 130:132], True, True,
                waits=[w(sV)], inc=(sT, 1))
        tdone = w(sT)
        prog.copy(s_col[:, :], p_small[:, 130:131], waits=[tdone], inc=(sV, 1))
        prog.copy(gq_col[:, :], p_small[:, 131:132], waits=[tdone], inc=(sV, 1))
        return w(sV)

    sc1_ready = finish_scan("w1s", 0, sw1_col, gq1_col)
    sc2_ready = finish_scan("w2s", 64, sw2_col, gq2_col)

    if debug:
        prog.dma("gpsimd", dbg_sc[:, 0:1], sw1_col[:, :], waits=[sc1_ready], sem=sSt, slow_ok=True)
        prog.dma("gpsimd", dbg_sc[:, 1:2], gq1_col[:, :], waits=[sc1_ready], sem=sSt, slow_ok=True)
        prog.dma("gpsimd", dbg_sc[:, 2:3], sw2_col[:, :], waits=[sc2_ready], sem=sSt, slow_ok=True)
        prog.dma("gpsimd", dbg_sc[:, 3:4], gq2_col[:, :], waits=[sc2_ready], sem=sSt, slow_ok=True)

    def finish(last_waits):
        prog.emit("gpsimd", lambda e: e.nop(), tuple(last_waits), None)
        prog.barrier(sB)
        prog.replay()
        ctx.close()
        return nc

    prog.barrier(sB)
    p1.close()

    if upto <= 1:
        return finish([w(sSt), w(sA), w(sV), w(sT), w(sCast), w(sScan)])

    p2 = ExitStack()
    p2sb = lambda name, shape, dt: p2.enter_context(nc.sbuf_tensor(name, shape, dt))
    k_sb = [p2sb(f"k_sb{i}", [128, T], BF16) for i in range(2)]
    v_sb = [p2sb(f"v_sb{i}", [128, T], BF16) for i in range(2)]
    qh_sb = [p2sb(f"qh_sb{i}", [128, NTOK], BF16) for i in range(2)]
    sq_scr = p2sb("sq_scr", [128, T], BF16)
    ring1 = p2sb("ring1", [128, 8 * 512], BF16)
    ring2 = p2sb("ring2", [128, 8 * 512], BF16)
    linv_sb = p2sb("linv_sb", [1, 512], F32)
    linv_bc = p2sb("linv_bc", [128, 512], F32)
    lnl8 = p2sb("lnl8", [1, 512], F32)
    negmh = [p2sb(f"negmh{i}", [1, 512], BF16) for i in range(2)]
    cstar = p2sb("cstar", [1, 8], F32)
    nrm_col = p2sb("nrm_col", [128, 1], F32)
    negc8_cols = p2sb("negc8_cols", [128, H], F32)
    negc_sc = p2sb("negc_sc", [1, H], F32)
    yst = [p2sb(f"yst{i}", [128, 512], BF16) for i in range(2)]
    ps_s = [p2.enter_context(nc.psum_tensor(f"ps_s{i}", [128, 512], F32))
            for i in range(4)]
    ps_y = p2.enter_context(nc.psum_tensor("ps_y", [128, 512], F32))
    ps_l = p2.enter_context(nc.psum_tensor("ps_l", [128, 512], F32))
    ps_misc = p2.enter_context(nc.psum_tensor("ps_misc", [128, 512], F32))

    p1_done = [q_tensor_done, q_stores_done, k_stores_done, v_stores_done,
               sc1_ready, sc2_ready]

    def load_head(h, extra):
        hb = h % 2
        waits = list(extra)
        prog.dma("sync", k_sb[hb][:, :], k_scr[128 * h:128 * (h + 1), :],
                 waits=waits, sem=sY)
        vsrc = v_scr[h * T:(h + 1) * T, :].rearrange("(c p) j -> p c j", p=128)
        vdst = v_sb[hb][:, :].rearrange("p (c j) -> p c j", j=128)
        for part in range(4):
            prog.dma("sync", vdst[:, 8 * part:8 * (part + 1), :],
                     vsrc[:, 8 * part:8 * (part + 1), :], waits=waits, sem=sY)
        prog.dma("sync", qh_sb[hb][:, :], q_scr[128 * h:128 * (h + 1), :],
                 waits=waits, sem=sY)
        return w(sY)

    loads_done = [None] * (H + 1)
    loads_done[0] = load_head(0, p1_done)
    sbuf_cons = [None] * (H + 2)
    misc_free = None
    psmall_free = sc2_ready
    psy_free = None
    psl_free = None
    yst_free = [None, None]
    s_exp_done = {}
    ring1_cons = {}
    ring2_cons = {}
    spin = 0
    g1 = 0
    g2 = 0
    negmh_ready = [None] * NJS
    negmh_free = [None, None]
    lnl8_free = [None]

    LAG = 3

    def emit_sweep1(h, js):
        nonlocal spin, g1, misc_free
        hb = h % 2
        C = 8 * js + 8
        negc8_ready = (sA, negc8_done[h])
        pend = []

        def flush1():
            qlo_, slot_, lw_, ci_ = pend.pop(0)
            prog.mm(cst["ones_col_bf"][:, :],
                    ring1[:, 512 * slot_ + qlo_:512 * (slot_ + 1)],
                    ps_misc[0:1, qlo_:512], ci_ == 0, ci_ == C - 1,
                    waits=lw_, inc=(sT, 1))
            ring1_cons[g1_base + ci_] = w(sT)

        g1_base = g1
        for ci in range(C):
            cp = ci - 8 * js
            qlo = 128 * I0[cp] if cp >= 0 else 0
            slot = ci % 8
            prog.mm(k_sb[hb][:, 128 * ci:128 * (ci + 1)],
                    qh_sb[hb][:, 512 * js + qlo:512 * (js + 1)],
                    ps_s[spin % 4][:, qlo:512], True, True,
                    waits=[loads_done[h], negc8_ready, s_exp_done.get(spin - 4)],
                    inc=(sT, 1))
            smm_done = w(sT)
            prog.act(ring1[:, 512 * slot + qlo:512 * (slot + 1)],
                     ps_s[spin % 4][:, qlo:512], AF.Exp,
                     bias=negc8_cols[:, h:h + 1], scale=INV_SQRT_HD / 8.0,
                     waits=[smm_done, ring1_cons.get(g1_base + ci - 8)], inc=(sA, 1))
            s_exp_done[spin] = w(sA)
            expd = w(sA)
            if cp >= 0:
                col = 512 * slot + qlo
                prog.tt(ALU.mult, ring1[:, col:col + 128],
                        ring1[:, col:col + 128],
                        cst["masks"][:, 128 * cp:128 * (cp + 1)],
                        waits=[expd, const_ld], inc=(sV, 1))
                mask_done = w(sV)
            else:
                mask_done = expd
            pend.append((qlo, slot, [expd, mask_done,
                                     misc_free if ci == 0 else None], ci))
            if len(pend) > LAG:
                flush1()
            spin += 1
        while pend:
            flush1()
        g1 += C
        l8_done = w(sT)
        prog.act(lnl8[:, :], ps_misc[0:1, 0:512], AF.Ln,
                 waits=[l8_done, lnl8_free[0]], inc=(sA, 1))
        misc_free = w(sA)
        prog.ts(negmh[js % 2][:, :], lnl8[:, :], float(-8.0 / INV_SQRT_HD),
                negc_sc[0:1, h:h + 1], ALU.mult, ALU.add,
                waits=[w(sA), negmh_free[js % 2]], inc=(sV, 1))
        lnl8_free[0] = w(sV)
        negmh_ready[js] = w(sV)

    def emit_sweep2(h, js):
        nonlocal spin, g2, misc_free, psy_free, psl_free
        hb = h % 2
        C = 8 * js + 8
        pend = []

        def flush2():
            qlo_, slot_, lw_, ci_ = pend.pop(0)
            prog.mm(cst["ones_col_bf"][:, :],
                    ring2[:, 512 * slot_ + qlo_:512 * (slot_ + 1)],
                    ps_l[0:1, qlo_:512], ci_ == 0, ci_ == C - 1,
                    waits=lw_)
            prog.mm(v_sb[hb][:, 128 * ci_:128 * (ci_ + 1)],
                    ring2[:, 512 * slot_ + qlo_:512 * (slot_ + 1)],
                    ps_y[:, qlo_:512], ci_ == 0, ci_ == C - 1,
                    inc=(sT, 1))
            ring2_cons[g2_base + ci_] = w(sT)

        g2_base = g2
        for ci in range(C):
            cp = ci - 8 * js
            qlo = 128 * I0[cp] if cp >= 0 else 0
            slot = ci % 8
            reg = ps_s[spin % 4][:, qlo:512]
            prog.mm(cst["ones_row_bf"][:, :], negmh[js % 2][0:1, qlo:512],
                    reg, True, False,
                    waits=[loads_done[h], negmh_ready[js],
                           s_exp_done.get(spin - 4)])
            prog.mm(k_sb[hb][:, 128 * ci:128 * (ci + 1)],
                    qh_sb[hb][:, 512 * js + qlo:512 * (js + 1)],
                    reg, False, True, inc=(sT, 1))
            smm_done = w(sT)
            prog.act(ring2[:, 512 * slot + qlo:512 * (slot + 1)],
                     ps_s[spin % 4][:, qlo:512], AF.Exp, scale=INV_SQRT_HD,
                     waits=[smm_done, ring2_cons.get(g2_base + ci - 8)], inc=(sA, 1))
            s_exp_done[spin] = w(sA)
            expd = w(sA)
            if cp >= 0:
                col = 512 * slot + qlo
                prog.tt(ALU.mult, ring2[:, col:col + 128],
                        ring2[:, col:col + 128],
                        cst["masks"][:, 128 * cp:128 * (cp + 1)],
                        waits=[expd, const_ld], inc=(sV, 1))
                mask_done = w(sV)
            else:
                mask_done = expd
            pend.append((qlo, slot,
                         [expd, mask_done,
                          psl_free if ci == 0 else None,
                          psy_free if ci == 0 else None], ci))
            if len(pend) > LAG:
                flush2()
            spin += 1
        while pend:
            flush2()
        g2 += C
        av_done = w(sT)
        prog.recip(linv_sb[:, :], ps_l[0:1, 0:512], waits=[av_done], inc=(sV, 1))
        prog.mm(cst["ones_row_f"][:, :], linv_sb[:, :], ps_l[:, 0:512],
                True, True, waits=[w(sV)], inc=(sT, 1))
        prog.act(linv_bc[:, :], ps_l[:, 0:512], AF.Identity,
                 waits=[w(sT)], inc=(sA, 1))
        psl_free = w(sA)
        ysl = (h * NJS + js) % 2
        prog.tt(ALU.mult, yst[ysl][:, :], ps_y[:, :], linv_bc[:, :],
                waits=[psl_free, yst_free[ysl]], inc=(sV, 1))
        psy_free = w(sV)
        ynorm_done = w(sV)
        for qt in range(4):
            qtg = 4 * js + qt
            prog.dma("sync",
                     y_scr[128 * qtg:128 * (qtg + 1), 128 * h:128 * (h + 1)],
                     yst[ysl][:, 128 * qt:128 * (qt + 1)],
                     waits=[ynorm_done], sem=sSt)
        yst_free[ysl] = w(sSt)
        negmh_free[js % 2] = av_done

    negc8_done = [None] * H
    for h in range(H):
        hb = h % 2
        prog.tt(ALU.mult, sq_scr[:, :], k_sb[hb][:, :], k_sb[hb][:, :],
                waits=[loads_done[h]], inc=(sV, 1))
        sqk = w(sV)
        for c in range(32):
            prog.mm(sq_scr[:, 128 * c:128 * (c + 1)], cst["ones_col_bf"][:, :],
                    ps_misc[:, c:c + 1], True, True,
                    waits=[sqk, misc_free] if c == 0 else (),
                    inc=(sT, 1) if c == 31 else None)
        knorm_mm = w(sT)
        prog.red(nrm_col[:, :], ps_misc[:, 0:32], ALU.max, waits=[knorm_mm],
                 inc=(sV, 1))
        misc_free = w(sV)
        prog.mm(nrm_col[:, :], cst["ident_f"][:, :], p_small[0:1, 0:128],
                True, True, waits=[w(sV), psmall_free], inc=(sT, 1))
        prog.red(cstar[:, 1:2], p_small[0:1, 0:128], ALU.max, waits=[w(sT)],
                 inc=(sV, 1))
        kn_done = w(sV)
        prog.tt(ALU.mult, sq_scr[:, 0:NTOK], qh_sb[hb][:, :], qh_sb[hb][:, :],
                waits=[knorm_mm], inc=(sV, 1))
        sqq = w(sV)
        for c in range(16):
            prog.mm(sq_scr[:, 128 * c:128 * (c + 1)], cst["ones_col_bf"][:, :],
                    ps_misc[:, 32 + c:33 + c], True, True,
                    waits=[sqq, misc_free] if c == 0 else (),
                    inc=(sT, 1) if c == 15 else None)
        qnorm_mm = w(sT)
        prog.red(nrm_col[:, :], ps_misc[:, 32:48], ALU.max,
                 waits=[qnorm_mm, kn_done], inc=(sV, 1))
        misc_free = w(sV)
        prog.mm(nrm_col[:, :], cst["ident_f"][:, :], p_small[0:1, 0:128],
                True, True, waits=[w(sV), kn_done], inc=(sT, 1))
        prog.red(cstar[:, 0:1], p_small[0:1, 0:128], ALU.max, waits=[w(sT)],
                 inc=(sV, 1))
        prog.tt(ALU.mult, cstar[:, 2:3], cstar[:, 0:1], cstar[:, 1:2],
                waits=[w(sV)], inc=(sV, 1))
        prog.act(cstar[:, 3:4], cstar[:, 2:3], AF.Ln, scale=float(1.0 / HD),
                 waits=[w(sV)], inc=(sA, 1))
        prog.act(cstar[:, 4:5], cstar[:, 3:4], AF.Exp, scale=0.5,
                 waits=[w(sA)], inc=(sA, 1))
        cs_done = w(sA)
        prog.ts(cstar[:, 5:6], cstar[:, 4:5], -0.125, None, ALU.mult,
                waits=[cs_done], inc=(sV, 1))
        prog.ts(negc_sc[0:1, h:h + 1], cstar[:, 4:5],
                float(-1.0 / INV_SQRT_HD), None, ALU.mult,
                waits=[cs_done], inc=(sV, 1))
        prog.mm(cst["ones_row_f"][:, :], cstar[:, 5:6], p_small[:, 130:131],
                True, True, waits=[w(sV)], inc=(sT, 1))
        prog.act(negc8_cols[:, h:h + 1], p_small[:, 130:131], AF.Identity,
                 waits=[w(sT)], inc=(sA, 1))
        psmall_free = w(sA)
        negc8_done[h] = sA.n

        if h + 1 < H:
            loads_done[h + 1] = load_head(h + 1, [sbuf_cons[h + 1]])

        emit_sweep1(h, 0)
        for js in range(1, NJS):
            emit_sweep1(h, js)
            emit_sweep2(h, js - 1)
        emit_sweep2(h, NJS - 1)
        sbuf_cons[h + 2] = w(sT)

    y_stores_done = w(sSt)
    prog.barrier(sB)
    p2.close()

    if upto <= 2:
        return finish([w(sSt), w(sA), w(sV), w(sT), w(sCast), w(sScan), w(sY)])

    p2_done = [w(sT), w(sA), w(sV), y_stores_done]

    ln_m = sb("ln_m", [128, NTILE], F32)
    ln_rs = sb("ln_rs", [128, NTILE], F32)
    ln_mrs = sb("ln_mrs", [128, NTILE], F32)
    p3ab = ExitStack()
    x1q_fm = p3ab.enter_context(nc.sbuf_tensor("x1q_fm", [128, H * D], BF16))

    p3a = ExitStack()
    asb = lambda name, shape, dt: p3a.enter_context(nc.sbuf_tensor(name, shape, dt))
    wproj_sb = asb("wproj_sb", [128, H * D], BF16)
    yt_sb = [asb(f"yt{i}", [128, D], BF16) for i in range(2)]
    xot_sb = [asb(f"xot{i}", [128, D], BF16) for i in range(2)]
    bprj_sb = asb("bprj_sb", [128, D], BF16)
    tmp_sb = [asb(f"tmpa{i}", [128, D], BF16) for i in range(2)]
    r_sb = [asb(f"ra{i}", [128, D], BF16) for i in range(2)]
    rq_scr = asb("rq_scr", [128, 1024], BF16)
    x1f = asb("x1f", [128, D], F32)
    rndt = asb("rndt", [128, D], F32)
    x1q_tm = [asb(f"x1q_tm{i}", [128, D], BF16) for i in range(2)]
    lnp_sum = asb("lnp_sum", [128, 32], F32)
    lnp_sq = asb("lnp_sq", [128, 32], F32)
    cols_a = asb("cols_a", [128, 8], F32)
    nmag_col = asb("nmag_col", [128, 1], F32)
    ps_pj = [p3a.enter_context(nc.psum_tensor(f"ps_pj{i}", [128, 1024], F32))
             for i in range(2)]
    ps_tr = [p3a.enter_context(nc.psum_tensor(f"ps_tr{i}", [128, 512], F32))
             for i in range(2)]

    prog.memset(nmag_col[:, :], -MAGIC, waits=p2_done, inc=(sV, 1))
    for c in range(16):
        prog.dma("gpsimd", wproj_sb[:, D * c:D * (c + 1)],
                 wprojT[128 * c:128 * (c + 1), :], waits=p2_done, sem=sCast)
    prog.dma("gpsimd", bprj_sb[:, :], bproj_bc, waits=p2_done, sem=sCast)
    wproj_ready = w(sCast)

    yt_free = [p2_done[0], p2_done[0]]
    xot_free = [None, None]
    tmp_free = [None, None]
    r_free = [None, None]
    cols_free = None
    x1f_free = None
    pj_free = [None, None]
    tr_free = [None, None]
    rndt_free = None
    x1qtm_free = [None, None]
    rq_free = None
    pend_tr = []

    def flush_tr():
        if not pend_tr:
            return
        t_, tb_, ready_ = pend_tr.pop(0)
        for c in range(16):
            prog.mm(x1q_tm[tb_][:, 128 * c:128 * (c + 1)], cst["ident_bf"][:, :],
                    ps_tr[(c // 4) % 2][:, 128 * (c % 4):128 * (c % 4 + 1)],
                    True, True,
                    waits=[ready_, const_ld, tr_free[(c // 4) % 2]]
                    if c % 4 == 0 else (),
                    inc=(sT, 1) if c % 4 == 3 else None)
            if c % 4 == 3:
                grp_done = w(sT)
                dst = x1q_fm[:, :].rearrange("p (cc tk) -> p cc tk", tk=D)[
                    :, c - 3:c + 1, 128 * t_:128 * (t_ + 1)]
                prog.act(dst, ps_tr[(c // 4) % 2][:, :], AF.Identity,
                         waits=[grp_done], inc=(sA, 1))
                tr_free[(c // 4) % 2] = w(sA)
        x1qtm_free[tb_] = w(sT)

    def tile_loads_a(t):
        tb = t % 2
        prog.dma("sync", yt_sb[tb][:, :], y_scr[128 * t:128 * (t + 1), :],
                 waits=[yt_free[tb], y_stores_done], sem=sY)
        prog.dma("gpsimd", xot_sb[tb][:, :], x_own_tm[128 * t:128 * (t + 1), :],
                 waits=[xot_free[tb]], sem=sCast)
        return w(sY), w(sCast)

    loads_a = [tile_loads_a(0), tile_loads_a(1)]
    for t in range(NTILE):
        tb = t % 2
        y_ld, x_ld = loads_a[t]
        prog.tt(ALU.add, tmp_sb[tb][:, :], xot_sb[tb][:, :], bprj_sb[:, :],
                waits=[x_ld, wproj_ready, tmp_free[tb]], inc=(sP, 1), eng="gpsimd")
        tmp_ready = w(sP)
        for half in range(2):
            ps = ps_pj[half]
            mmw = [y_ld, wproj_ready, pj_free[half]]
            for c in range(16):
                for o2 in range(2):
                    of0 = 1024 * half + 512 * o2
                    prog.mm(yt_sb[tb][:, 128 * c:128 * (c + 1)],
                            wproj_sb[:, D * c + of0:D * c + of0 + 512],
                            ps[:, 512 * o2:512 * (o2 + 1)],
                            c == 0, c == 15,
                            waits=mmw if (c == 0 and o2 == 0) else (),
                            inc=(sT, 1) if (c == 15 and o2 == 1) else None)
            mm_done = w(sT)
            prog.stt(r_sb[tb][:, 1024 * half:1024 * (half + 1)], ps[:, :], 1.0,
                     tmp_sb[tb][:, 1024 * half:1024 * (half + 1)], ALU.mult, ALU.add,
                     accum=lnp_sum[:, 2 * t % 32 + half:2 * t % 32 + half + 1],
                     waits=[mm_done, tmp_ready] + (r_free[tb] or []), inc=(sV, 1))
            pj_free[half] = w(sV)
            prog.act(rq_scr[:, :], r_sb[tb][:, 1024 * half:1024 * (half + 1)],
                     AF.Square, accum=lnp_sq[:, 2 * t % 32 + half:2 * t % 32 + half + 1],
                     waits=[w(sV), rq_free], inc=(sA, 1))
            rq_free = w(sA)
        yt_free[tb] = w(sT)
        flush_tr()
        r_ready = w(sV)
        sq_ready = w(sA)
        prog.dma("sync", x1_scr[128 * t:128 * (t + 1), :], r_sb[tb][:, :],
                 waits=[r_ready], sem=sSt)
        r_spill = w(sSt)
        i0c, i1c = 2 * t % 32, 2 * t % 32 + 1
        prog.tt(ALU.add, cols_a[:, 0:1], lnp_sum[:, i0c:i0c + 1],
                lnp_sum[:, i1c:i1c + 1], waits=[r_ready, cols_free], inc=(sV, 1))
        prog.ts(ln_m[:, t:t + 1], cols_a[:, 0:1], 1.0 / D, None, ALU.mult,
                waits=[w(sV)], inc=(sV, 1))
        prog.tt(ALU.add, cols_a[:, 1:2], lnp_sq[:, i0c:i0c + 1],
                lnp_sq[:, i1c:i1c + 1], waits=[sq_ready], inc=(sV, 1))
        prog.ts(cols_a[:, 1:2], cols_a[:, 1:2], 1.0 / D, None, ALU.mult,
                waits=[w(sV)], inc=(sV, 1))
        prog.act(cols_a[:, 2:3], ln_m[:, t:t + 1], AF.Square, waits=[w(sV)],
                 inc=(sA, 1))
        prog.tt(ALU.subtract, cols_a[:, 3:4], cols_a[:, 1:2], cols_a[:, 2:3],
                waits=[w(sA), w(sV)], inc=(sV, 1))
        prog.ts(cols_a[:, 3:4], cols_a[:, 3:4], EPS, None, ALU.add,
                waits=[w(sV)], inc=(sV, 1))
        prog.recip(cols_a[:, 4:5], cols_a[:, 3:4], waits=[w(sV)], inc=(sV, 1))
        prog.act(ln_rs[:, t:t + 1], cols_a[:, 4:5], AF.Sqrt, waits=[w(sV)],
                 inc=(sA, 1))
        rstd_ready = w(sA)
        prog.tt(ALU.mult, ln_mrs[:, t:t + 1], ln_m[:, t:t + 1], ln_rs[:, t:t + 1],
                waits=[rstd_ready], inc=(sV, 1))
        prog.ts(ln_mrs[:, t:t + 1], ln_mrs[:, t:t + 1], -1.0, None, ALU.mult,
                waits=[w(sV)], inc=(sV, 1))
        prog.red(cols_a[:, 5:6], r_sb[tb][:, :], ALU.max, waits=[r_ready],
                 inc=(sV, 1))
        prog.red(cols_a[:, 6:7], r_sb[tb][:, :], ALU.min, waits=[w(sV)],
                 inc=(sV, 1))
        prog.tt(ALU.subtract, cols_a[:, 5:6], cols_a[:, 5:6], ln_m[:, t:t + 1],
                waits=[w(sV)], inc=(sV, 1))
        prog.tt(ALU.subtract, cols_a[:, 6:7], ln_m[:, t:t + 1], cols_a[:, 6:7],
                waits=[w(sV)], inc=(sV, 1))
        prog.tt(ALU.max, cols_a[:, 5:6], cols_a[:, 5:6], cols_a[:, 6:7],
                waits=[w(sV)], inc=(sV, 1))
        prog.tt(ALU.mult, amax1_cols[:, t:t + 1], cols_a[:, 5:6], ln_rs[:, t:t + 1],
                waits=[w(sV), rstd_ready], inc=(sV, 1))
        amax_ready = w(sV)
        prog.tt(ALU.mult, dq1_cols[:, t:t + 1], amax1_cols[:, t:t + 1],
                gq1_col[:, :], waits=[amax_ready, sc1_ready], inc=(sV, 1))
        prog.recip(cols_a[:, 7:8], amax1_cols[:, t:t + 1], waits=[amax_ready],
                   inc=(sV, 1))
        prog.ts(s1_cols[:, t:t + 1], cols_a[:, 7:8], Q8, None, ALU.mult,
                waits=[w(sV)], inc=(sV, 1))
        qsc_ready = w(sV)
        prog.act(x1f[:, :], r_sb[tb][:, :], AF.Identity, bias=ln_mrs[:, t:t + 1],
                 scale=ln_rs[:, t:t + 1], waits=[qsc_ready, x1f_free], inc=(sA, 1))
        r_free[tb] = [r_spill, w(sA)]
        cols_free = w(sA)
        prog.act(rndt[:, :], x1f[:, :], AF.Identity, bias=cst["magic_col"][:, :],
                 scale=s1_cols[:, t:t + 1], waits=[w(sA), rndt_free], inc=(sA, 1))
        x1f_free = w(sA)
        prog.ts(x1q_tm[tb][:, :], rndt[:, :], -MAGIC, None, ALU.add,
                waits=[w(sA), x1qtm_free[tb]], inc=(sV, 1))
        rndt_free = w(sV)
        x1q_ready = w(sV)
        pend_tr.append((t, tb, x1q_ready))
        xot_free[tb] = w(sP)
        tmp_free[tb] = w(sV)
        if t + 2 < NTILE:
            loads_a.append(tile_loads_a(t + 2))
    flush_tr()
    x1q_done = w(sA)
    r_spills_done = w(sSt)

    if debug:
        prog.dma("gpsimd", dbg_x1q, x1q_fm[:, :], waits=[x1q_done], sem=sSt)
        dcw = [x1q_done, w(sV)]
        prog.dma("gpsimd", dbg_cols[:, 0:16], amax1_cols[:, :], waits=dcw, sem=sSt)
        prog.dma("gpsimd", dbg_cols[:, 16:32], s1_cols[:, :], waits=dcw, sem=sSt)
        prog.dma("gpsimd", dbg_cols[:, 32:48], ln_m[:, :], waits=dcw, sem=sSt)
        prog.dma("gpsimd", dbg_cols[:, 48:64], ln_rs[:, :], waits=dcw, sem=sSt)

    prog.barrier(sB)
    p3a.close()

    if upto <= 31:
        p3ab.close()
        return finish([w(sSt), w(sA), w(sV), w(sT), w(sCast), w(sScan), w(sY)])

    p3b = ExitStack()
    bsb = lambda name, shape, dt: p3b.enter_context(nc.sbuf_tensor(name, shape, dt))
    w1q_sb = [bsb(f"w1q{i}", [128, 16 * 512], BF16) for i in range(2)]
    w1st = [bsb(f"w1st{i}", [128, 512], F32) for i in range(2)]
    w1rnd = bsb("w1rnd", [128, 512], F32)
    b1_sb = bsb("b1_sb", [128, DFF], BF16)
    x2pre = [bsb(f"x2pre{i}", [128, 512], F32) for i in range(2)]
    gt_sb = [bsb(f"gt{i}", [128, 512], BF16) for i in range(2)]
    ps_f1 = [p3b.enter_context(nc.psum_tensor(f"ps_f1{i}", [128, 512], F32))
             for i in range(4)]

    p3a_done = [x1q_done, r_spills_done, w(sT), w(sV), w(sP)]
    for c in range(4):
        prog.dma("gpsimd", b1_sb[:, 2048 * c:2048 * (c + 1)],
                 b1_bc[:, 2048 * c:2048 * (c + 1)], waits=p3a_done, sem=sCast)
    b1_ready = w(sCast)

    w1st_free = [None, None]
    w1rnd_free = None
    w1q_free = [None, None]
    x2_free = [None, None]
    gt_free = [None, None]
    f1_free = [None, None, None, None]

    def stream_w1_block(o):
        nonlocal w1rnd_free
        ob = o % 2
        rdys = []
        for c in range(16):
            cs = c % 2
            prog.dma("sync", w1st[cs][:, :],
                     w1T[128 * c:128 * (c + 1), 512 * o:512 * (o + 1)],
                     waits=[w1st_free[cs], w1q_free[ob] if c == 0 else None]
                     + (p3a_done if o == 0 and c == 0 else []), sem=sLd)
            prog.act(w1rnd[:, :], w1st[cs][:, :], AF.Identity,
                     bias=cst["magic_col"][:, :], scale=sw1_col[:, :],
                     waits=[w(sLd), sc1_ready, w1rnd_free], inc=(sA, 1))
            w1st_free[cs] = w(sA)
            prog.ts(w1q_sb[ob][:, 512 * c:512 * (c + 1)], w1rnd[:, :], -MAGIC,
                    None, ALU.add, waits=[w(sA)], inc=(sV, 1))
            w1rnd_free = w(sV)
        return w(sV)

    w1_ready = [stream_w1_block(0)]
    for o in range(16):
        ob = o % 2
        if o + 1 < 16:
            w1_ready.append(stream_w1_block(o + 1))
        for t in range(NTILE):
            ps = ps_f1[t % 4]
            mmw = [w1_ready[o], x1q_done, f1_free[t % 4]]
            for c in range(16):
                prog.mm(x1q_fm[:, D * c + 128 * t:D * c + 128 * (t + 1)],
                        w1q_sb[ob][:, 512 * c:512 * (c + 1)],
                        ps[:, :], c == 0, c == 15,
                        waits=mmw if c == 0 else (),
                        inc=(sT, 1) if c == 15 else None)
            mm_done = w(sT)
            tb2 = t % 2
            prog.stt(x2pre[tb2][:, :], ps[:, :], dq1_cols[:, t:t + 1],
                     b1_sb[:, 512 * o:512 * (o + 1)], ALU.mult, ALU.add,
                     waits=[mm_done, b1_ready, x2_free[tb2]], inc=(sV, 1))
            f1_free[t % 4] = w(sV)
            prog.act(gt_sb[tb2][:, :], x2pre[tb2][:, :], AF.Gelu_apprx_tanh,
                     waits=[w(sV), gt_free[tb2]], inc=(sA, 1))
            x2_free[tb2] = w(sA)
            gelu_done = w(sA)
            prog.red(amax2_parts[:, 16 * t + o:16 * t + o + 1], gt_sb[tb2][:, :],
                     ALU.max, absval=True, waits=[gelu_done], inc=(sV, 1))
            prog.dma("sync", g_scr[128 * t:128 * (t + 1), 512 * o:512 * (o + 1)],
                     gt_sb[tb2][:, :], waits=[gelu_done], sem=sSt)
            gt_free[tb2] = w(sSt)
        w1q_free[ob] = w(sT)
    amax2c = bsb("amax2c", [128, NTILE], F32)
    for t in range(NTILE):
        prog.red(amax2c[:, t:t + 1], amax2_parts[:, 16 * t:16 * (t + 1)],
                 ALU.max, waits=[w(sV)], inc=(sV, 1))
        prog.tt(ALU.mult, dq2_cols[:, t:t + 1], amax2c[:, t:t + 1],
                gq2_col[:, :], waits=[w(sV), sc2_ready], inc=(sV, 1))
        prog.recip(s2_cols[:, t:t + 1], amax2c[:, t:t + 1],
                   waits=[w(sV)], inc=(sV, 1))
        prog.ts(s2_cols[:, t:t + 1], s2_cols[:, t:t + 1], Q8, None, ALU.mult,
                waits=[w(sV)], inc=(sV, 1))
    s2_ready = w(sV)
    g_stores_done = w(sSt)

    if debug:
        prog.dma("gpsimd", dbg_cols[:, 64:80], s2_cols[:, :], waits=[s2_ready], sem=sSt)
        prog.dma("gpsimd", dbg_cols[:, 80:96], dq2_cols[:, :], waits=[s2_ready], sem=sSt)

    prog.barrier(sB)
    p3b.close()
    p3ab.close()

    if upto <= 32:
        return finish([w(sSt), w(sA), w(sV), w(sT), w(sCast), w(sScan), w(sY)])

    NB = 8
    p3c = ExitStack()
    csb = lambda name, shape, dt: p3c.enter_context(nc.sbuf_tensor(name, shape, dt))
    oacc = csb("oacc", [128, NTILE * 1024], F32)
    w2q_sb = [csb(f"w2q{i}", [128, 8 * 1024], BF16) for i in range(2)]
    w2st = [csb(f"w2st{i}", [128, 1024], F32) for i in range(2)]
    rnd_scr = csb("rnd_scr", [128, 2048], F32)
    g_in = [csb(f"g_in{i}", [128, 1024], BF16) for i in range(2)]
    x2q_tm = csb("x2q_tm", [128, 1024], BF16)
    x2q_fm = [csb(f"x2q_fm{i}", [128, 1024], BF16) for i in range(2)]
    rres = [csb(f"rres{i}", [128, 1024], BF16) for i in range(2)]
    xrec = csb("xrec", [128, 1024], F32)
    tmp2 = csb("tmp2", [128, 1024], F32)
    b2_sb = csb("b2_sb", [128, D], F32)
    r2h0 = csb("r2h0", [128, 1024], BF16)
    r2h1 = csb("r2h1", [128, 1024], F32)
    outst = [csb(f"outst{i}", [128, 1024], F32) for i in range(2)]
    ln2_sum = csb("ln2_sum", [128, 32], F32)
    ln2_sq = csb("ln2_sq", [128, 32], F32)
    cols_c = csb("cols_c", [128, 8], F32)
    sq2_scr = csb("sq2_scr", [128, 1024], BF16)
    r2_scr = nc.dram_tensor("r2_scr", [NTOK, 1024], BF16, kind=scratch_kind).ap()
    ps_f2 = [p3c.enter_context(nc.psum_tensor(f"ps_f2{i}", [128, 1024], F32))
             for i in range(2)]
    ps_t2 = [p3c.enter_context(nc.psum_tensor(f"ps_t2{i}", [128, 512], F32))
             for i in range(2)]

    p3b_done = [g_stores_done, s2_ready, w(sT), w(sA), w(sP)]
    prog.dma("sync", b2_sb[:, :], b2_bc, waits=p3b_done, sem=sLd)
    b2_ready = w(sLd)

    w2st_free = [None, None]
    w2rnd_free = None
    x2rnd_free = None
    w2q_free = [None, None]
    oacc_free = [None] * NTILE
    gin_free = [None, None]
    x2qtm_free = None
    x2qfm_free = [None, None]
    t2_free = [None, None]
    f2_free = [None, None]
    rres_free = [None, None]
    xrec_free = None
    tmp2_free = None
    r2h0_free = None
    r2h1_free = None
    outst_free = [None, None]
    sq2_free = None
    colsc_free = None

    def stream_w2_block(h2, b):
        nonlocal w2rnd_free
        wb = (h2 * NB + b) % 2
        for c in range(8):
            cs = c % 2
            row0 = 1024 * b + 128 * c
            prog.dma("sync", w2st[cs][:, :],
                     w2T[row0:row0 + 128, 1024 * h2:1024 * (h2 + 1)],
                     waits=[w2st_free[cs], w2q_free[wb] if c == 0 else None]
                     + (p3b_done if h2 + b == 0 and c == 0 else []), sem=sLd)
            prog.act(rnd_scr[:, 0:1024], w2st[cs][:, :], AF.Identity,
                     bias=cst["magic_col"][:, :], scale=sw2_col[:, :],
                     waits=[w(sLd), sc2_ready, w2rnd_free], inc=(sA, 1))
            w2st_free[cs] = w(sA)
            prog.ts(w2q_sb[wb][:, 1024 * c:1024 * (c + 1)], rnd_scr[:, 0:1024],
                    -MAGIC, None, ALU.add, waits=[w(sA)], inc=(sV, 1))
            w2rnd_free = w(sV)
        return w(sV)

    w2_ready = {}
    w2_ready[(0, 0)] = stream_w2_block(0, 0)
    seq = [(h2, b) for h2 in range(2) for b in range(NB)]
    for si, (h2, b) in enumerate(seq):
        wb = (h2 * NB + b) % 2
        if si + 1 < len(seq):
            w2_ready[seq[si + 1]] = stream_w2_block(*seq[si + 1])
        for t in range(NTILE):
            tb = t % 2
            prog.dma("sync", g_in[tb][:, :],
                     g_scr[128 * t:128 * (t + 1), 1024 * b:1024 * (b + 1)],
                     waits=[gin_free[tb], g_stores_done], sem=sY)
            prog.act(rnd_scr[:, 1024:2048], g_in[tb][:, :], AF.Identity,
                     bias=cst["magic_col"][:, :], scale=s2_cols[:, t:t + 1],
                     waits=[w(sY), s2_ready, x2rnd_free], inc=(sA, 1))
            gin_free[tb] = w(sA)
            prog.ts(x2q_tm[:, :], rnd_scr[:, 1024:2048], -MAGIC, None, ALU.add,
                    waits=[w(sA), x2qtm_free], inc=(sV, 1))
            x2rnd_free = w(sV)
            x2q_ready = w(sV)
            fmb = t % 2
            for c in range(8):
                prog.mm(x2q_tm[:, 128 * c:128 * (c + 1)], cst["ident_bf"][:, :],
                        ps_t2[(c // 4) % 2][:, 128 * (c % 4):128 * (c % 4 + 1)],
                        True, True,
                        waits=[x2q_ready, t2_free[(c // 4) % 2]] if c % 4 == 0 else (),
                        inc=(sT, 1) if c % 4 == 3 else None)
                if c % 4 == 3:
                    prog.act(x2q_fm[fmb][:, 512 * (c // 4):512 * (c // 4 + 1)],
                             ps_t2[(c // 4) % 2][:, :], AF.Identity,
                             waits=[w(sT), x2qfm_free[fmb] if c == 3 else None],
                             inc=(sA, 1))
                    t2_free[(c // 4) % 2] = w(sA)
            x2qtm_free = w(sT)
            x2fm_ready = w(sA)
            ps = ps_f2[tb]
            mmw = [x2fm_ready, w2_ready[(h2, b)], f2_free[tb]]
            for c in range(8):
                for o2 in range(2):
                    prog.mm(x2q_fm[fmb][:, 128 * c:128 * (c + 1)],
                            w2q_sb[wb][:, 1024 * c + 512 * o2:1024 * c + 512 * (o2 + 1)],
                            ps[:, 512 * o2:512 * (o2 + 1)],
                            c == 0, c == 7,
                            waits=mmw if (c == 0 and o2 == 0) else (),
                            inc=(sT, 1) if (c == 7 and o2 == 1) else None)
            mm_done = w(sT)
            x2qfm_free[fmb] = mm_done
            oreg = oacc[:, 1024 * t:1024 * (t + 1)]
            if b == 0:
                prog.copy(oreg, ps[:, :], waits=[mm_done, oacc_free[t]], inc=(sV, 1))
            else:
                prog.tt(ALU.add, oreg, oreg, ps[:, :], waits=[mm_done], inc=(sV, 1))
            f2_free[tb] = w(sV)
            if b == NB - 1 and t == NTILE - 1:
                w2q_free[wb] = w(sT)
            elif t == NTILE - 1:
                w2q_free[wb] = w(sT)
            if b < NB - 1:
                continue
            acc_done = w(sV)
            prog.dma("sync", rres[tb][:, :],
                     x1_scr[128 * t:128 * (t + 1), 1024 * h2:1024 * (h2 + 1)],
                     waits=[rres_free[tb], r_spills_done], sem=sY)
            prog.act(xrec[:, :], rres[tb][:, :], AF.Identity,
                     bias=ln_mrs[:, t:t + 1], scale=ln_rs[:, t:t + 1],
                     waits=[w(sY), xrec_free], inc=(sA, 1))
            prog.tt(ALU.add, tmp2[:, :], xrec[:, :], b2_sb[:, 1024 * h2:1024 * (h2 + 1)],
                    waits=[w(sA), b2_ready, tmp2_free], inc=(sP, 1), eng="gpsimd")
            xrec_free = w(sP)
            rres_free[tb] = w(sP)
            if h2 == 0:
                prog.stt(r2h0[:, :], oreg, dq2_cols[:, t:t + 1], tmp2[:, :],
                         ALU.mult, ALU.add, accum=ln2_sum[:, 2 * t:2 * t + 1],
                         waits=[acc_done, w(sP), r2h0_free], inc=(sV, 1))
                tmp2_free = w(sV)
                oacc_free[t] = w(sV)
                r2_ready = w(sV)
                prog.act(sq2_scr[:, :], r2h0[:, :], AF.Square,
                         accum=ln2_sq[:, 2 * t:2 * t + 1],
                         waits=[r2_ready, sq2_free], inc=(sA, 1))
                sq2_free = w(sA)
                prog.dma("sync", r2_scr[128 * t:128 * (t + 1), :], r2h0[:, :],
                         waits=[r2_ready], sem=sSt)
                r2h0_free = w(sSt)
            else:
                prog.stt(r2h1[:, :], oreg, dq2_cols[:, t:t + 1], tmp2[:, :],
                         ALU.mult, ALU.add, accum=ln2_sum[:, 2 * t + 1:2 * t + 2],
                         waits=[acc_done, w(sP), r2h1_free], inc=(sV, 1))
                tmp2_free = w(sV)
                r2_ready = w(sV)
                prog.act(sq2_scr[:, :], r2h1[:, :], AF.Square,
                         accum=ln2_sq[:, 2 * t + 1:2 * t + 2],
                         waits=[r2_ready, sq2_free], inc=(sA, 1))
                sq2_stat = w(sA)
                sq2_free = sq2_stat
                prog.tt(ALU.add, cols_c[:, 0:1], ln2_sum[:, 2 * t:2 * t + 1],
                        ln2_sum[:, 2 * t + 1:2 * t + 2],
                        waits=[r2_ready, colsc_free], inc=(sV, 1))
                prog.ts(cols_c[:, 0:1], cols_c[:, 0:1], 1.0 / D, None, ALU.mult,
                        waits=[w(sV)], inc=(sV, 1))
                prog.tt(ALU.add, cols_c[:, 1:2], ln2_sq[:, 2 * t:2 * t + 1],
                        ln2_sq[:, 2 * t + 1:2 * t + 2], waits=[sq2_stat],
                        inc=(sV, 1))
                prog.ts(cols_c[:, 1:2], cols_c[:, 1:2], 1.0 / D, None, ALU.mult,
                        waits=[w(sV)], inc=(sV, 1))
                prog.act(cols_c[:, 2:3], cols_c[:, 0:1], AF.Square,
                         waits=[w(sV)], inc=(sA, 1))
                prog.tt(ALU.subtract, cols_c[:, 3:4], cols_c[:, 1:2], cols_c[:, 2:3],
                        waits=[w(sA), w(sV)], inc=(sV, 1))
                prog.ts(cols_c[:, 3:4], cols_c[:, 3:4], EPS, None, ALU.add,
                        waits=[w(sV)], inc=(sV, 1))
                prog.recip(cols_c[:, 4:5], cols_c[:, 3:4], waits=[w(sV)], inc=(sV, 1))
                prog.act(cols_c[:, 4:5], cols_c[:, 4:5], AF.Sqrt, waits=[w(sV)],
                         inc=(sA, 1))
                prog.tt(ALU.mult, cols_c[:, 5:6], cols_c[:, 0:1], cols_c[:, 4:5],
                        waits=[w(sA)], inc=(sV, 1))
                prog.ts(cols_c[:, 5:6], cols_c[:, 5:6], -1.0, None, ALU.mult,
                        waits=[w(sV)], inc=(sV, 1))
                ln2_ready = w(sV)
                prog.dma("sync", r2h0[:, :], r2_scr[128 * t:128 * (t + 1), :],
                         waits=[r2h0_free], sem=sY)
                prog.act(outst[0][:, :], r2h0[:, :], AF.Identity,
                         bias=cols_c[:, 5:6], scale=cols_c[:, 4:5],
                         waits=[w(sY), ln2_ready, outst_free[0]], inc=(sA, 1))
                r2h0_free = w(sA)
                prog.dma("sync", out[128 * t:128 * (t + 1), 0:1024], outst[0][:, :],
                         waits=[w(sA)], sem=sSt)
                outst_free[0] = w(sSt)
                prog.act(outst[1][:, :], r2h1[:, :], AF.Identity,
                         bias=cols_c[:, 5:6], scale=cols_c[:, 4:5],
                         waits=[ln2_ready, outst_free[1]], inc=(sA, 1))
                r2h1_free = w(sA)
                colsc_free = w(sA)
                prog.dma("sync", out[128 * t:128 * (t + 1), 1024:2048], outst[1][:, :],
                         waits=[w(sA)], sem=sSt)
                outst_free[1] = w(sSt)

    prog.barrier(sB)
    p3c.close()
    return finish([w(sSt), w(sA), w(sV), w(sT), w(sCast), w(sScan), w(sY),
                   w(sLd), w(sP)])


def host_prepare(inputs):
    x = np.asarray(inputs["L0_attn_x"], dtype=np.float32)
    w_attn = np.asarray(inputs["L0_attn_w_attn"], dtype=np.float32)
    b_attn = np.asarray(inputs["L0_attn_b_attn"], dtype=np.float32)
    w_proj = np.asarray(inputs["L0_attn_w_proj"], dtype=np.float32)
    b_proj = np.asarray(inputs["L0_attn_b_proj"], dtype=np.float32)
    W1 = np.asarray(inputs["L0_ffn_W1"], dtype=np.float32)
    B1 = np.asarray(inputs["L0_ffn_B1"], dtype=np.float32)
    W2 = np.asarray(inputs["L0_ffn_W2"], dtype=np.float32)
    B2 = np.asarray(inputs["L0_ffn_B2"], dtype=np.float32)

    shared = dict(
        wqT=np.ascontiguousarray(w_attn[0:D].T),
        wkT=np.ascontiguousarray(w_attn[D:2 * D].T),
        wvT=np.ascontiguousarray(w_attn[2 * D:3 * D].T),
        bq_col=np.ascontiguousarray(b_attn[0:D].reshape(H, 128).T),
        bk_col=np.ascontiguousarray(b_attn[D:2 * D].reshape(H, 128).T),
        bv_bc=np.ascontiguousarray(np.broadcast_to(b_attn[2 * D:3 * D], (128, D))),
        wprojT=np.ascontiguousarray(w_proj.T),
        bproj_bc=np.ascontiguousarray(np.broadcast_to(b_proj, (128, D))),
        w1T=np.ascontiguousarray(W1.T),
        b1_bc=np.ascontiguousarray(np.broadcast_to(B1, (128, DFF))),
        w2T=np.ascontiguousarray(W2.T),
        b2_bc=np.ascontiguousarray(np.broadcast_to(B2, (128, D))),
        ident_bf=np.eye(128, dtype=BFNP),
        ident_f=np.eye(128, dtype=np.float32),
        ones_col_bf=np.ones((128, 1), BFNP),
        ones_row_f=np.ones((1, 128), np.float32),
        ones_row_bf=np.ones((1, 128), BFNP),
    )

    tri = np.triu(np.ones((128, 128), np.float32))
    zeros = np.zeros((128, 128), np.float32)
    ones = np.ones((128, 128), np.float32)
    mask_p = {}
    for p in range(2):
        tiles = [(tri if cp % 2 == p else (zeros if p == 0 else ones))
                 for cp in range(8)]
        mask_p[p] = np.concatenate(tiles, axis=0).astype(BFNP)

    in_maps = []
    own_idx = {}
    for core in range(N_CORES):
        b, p = core // 2, core % 2
        blocks = np.arange(p, 32, 2)
        tok_idx = (blocks[:, None] * 128 + np.arange(128)[None, :]).reshape(-1)
        own_idx[core] = tok_idx
        xb = x[b]
        x_own = xb[tok_idx]
        m = dict(shared)
        m["x_fm"] = np.ascontiguousarray(xb.T)
        m["x_own_fm"] = np.ascontiguousarray(x_own.T)
        m["x_own_tm"] = np.ascontiguousarray(x_own)
        m["masks"] = mask_p[p]
        in_maps.append(m)
    return in_maps, own_idx


_PROG_CACHE = {}


def kernel(**inputs):
    in_maps, own_idx = host_prepare(inputs)
    if "full" not in _PROG_CACHE:
        _PROG_CACHE["full"] = build_program(upto=99, debug=False)
    nc = _PROG_CACHE["full"]
    res = run_bass_kernel_spmd(nc, in_maps, core_ids=list(range(N_CORES)))
    out = np.zeros((4, T, D), dtype=np.float32)
    for core in range(N_CORES):
        b = core // 2
        out[b, own_idx[core]] = res.results[core]["out"]
    return out


# version 05 (final)
# speedup vs baseline: 1.3307x; 1.0389x over previous
import numpy as np
import ml_dtypes
from contextlib import ExitStack

import concourse.bass as bass
import concourse.mybir as mybir
from concourse.bass_utils import run_bass_kernel_spmd

F32 = mybir.dt.float32
BF16 = mybir.dt.bfloat16
AF = mybir.ActivationFunctionType
ALU = mybir.AluOpType
AX = mybir.AxisListType.X
BFNP = ml_dtypes.bfloat16

D = 2048
T = 4096
H = 16
HD = 128
NTOK = 2048
NTILE = 16
NJS = 4
DFF = 8192
MAGIC = 12582912.0
INV_SQRT_HD = float(1.0 / np.sqrt(128.0))
Q8 = 127.0
INV_Q8SQ = float(1.0 / (127.0 * 127.0))
EPS = 1e-5
N_CORES = 8

I0 = [0, 0, 1, 1, 2, 2, 3, 3]


class Sem:
    __slots__ = ("h", "n")

    def __init__(self, handle):
        self.h = handle
        self.n = 0


class Prog:

    ENGINES = ("sync", "scalar", "vector", "tensor", "gpsimd")

    def __init__(self, nc):
        self.nc = nc
        self.ops = {e: [] for e in self.ENGINES}
        self.waited = {e: {} for e in self.ENGINES}

    def emit(self, eng, fn, waits=(), inc=None):
        wl = []
        for item in waits:
            if item is None:
                continue
            s, v = item
            assert v <= s.n, f"wait on future sem value {v} > {s.n}"
            key = id(s)
            if self.waited[eng].get(key, -1) >= v:
                continue
            self.waited[eng][key] = v
            wl.append((s.h, v))
        post = None
        if inc is not None:
            s, amt = inc
            s.n += amt
            post = (s.h, amt)
        wl = tuple(wl)

        def f(e, _fn=fn, _wl=wl, _post=post):
            for h, v in _wl:
                e.wait_ge(h, v)
            ins = _fn(e)
            if _post is not None:
                ins.then_inc(_post[0], _post[1])
        self.ops[eng].append(f)

    def dma(self, q, dst, src, waits=(), sem=None, slow_ok=False):
        self.emit(q,
                  lambda e: e.dma_start(dst, src, allow_slow_non_contiguous=slow_ok),
                  waits, (sem, 16) if sem is not None else None)

    def mm(self, lhsT, rhs, out, start, stop, waits=(), inc=None):
        self.emit("tensor",
                  lambda e: e.matmul(out, lhsT, rhs, start=start, stop=stop),
                  waits, inc)

    def act(self, out, in_, func, bias=0.0, scale=1.0, accum=None, waits=(), inc=None):
        self.emit("scalar",
                  lambda e: e.activation(out, in_, func, bias=bias, scale=scale,
                                         accum_out=accum),
                  waits, inc)

    def tt(self, op, out, a, b, waits=(), inc=None, eng="vector"):
        self.emit(eng, lambda e: e.tensor_tensor(out, a, b, op), waits, inc)

    def ts(self, out, in0, s1, s2, op0, op1=ALU.bypass, accum=None, waits=(), inc=None,
           eng="vector"):
        self.emit(eng,
                  lambda e: e.tensor_scalar(out, in0, s1, s2, op0, op1,
                                            accum_out=accum),
                  waits, inc)

    def stt(self, out, in0, scalar, in1, op0, op1, accum=None, waits=(), inc=None):
        self.emit("vector",
                  lambda e: e.scalar_tensor_tensor(out, in0, scalar, in1, op0, op1,
                                                   accum_out=accum),
                  waits, inc)

    def red(self, out, in_, op, absval=False, waits=(), inc=None):
        self.emit("vector",
                  lambda e: e.tensor_reduce(out, in_, AX, op,
                                            apply_absolute_value=absval),
                  waits, inc)

    def copy(self, out, in_, waits=(), inc=None, eng="vector"):
        self.emit(eng, lambda e: e.tensor_copy(out, in_), waits, inc)

    def recip(self, out, in_, waits=(), inc=None):
        self.emit("vector", lambda e: e.reciprocal(out, in_), waits, inc)

    def memset(self, ap, val, waits=(), inc=None, eng="vector"):
        self.emit(eng, lambda e: e.memset(ap, val), waits, inc)

    def barrier(self, bsem):
        base = bsem.n
        for eng in self.ENGINES:
            self.emit(eng, lambda e: e.nop(), (), (bsem, 1))
        for eng in self.ENGINES:
            self.emit(eng, lambda e: e.nop(), ((bsem, base + 5),), None)

    def replay(self):
        nc = self.nc
        with nc.Block() as block:
            @block.sync
            def _(e):
                for f in self.ops["sync"]:
                    f(e)

            @block.scalar
            def _(e):
                for f in self.ops["scalar"]:
                    f(e)

            @block.vector
            def _(e):
                for f in self.ops["vector"]:
                    f(e)

            @block.tensor
            def _(e):
                for f in self.ops["tensor"]:
                    f(e)

            @block.gpsimd
            def _(e):
                for f in self.ops["gpsimd"]:
                    f(e)


def w(sem):
    return (sem, sem.n)


class Scanner:

    def __init__(self, prog, slots, parts, sems):
        self.prog = prog
        self.slots = slots
        self.parts = parts
        self.sScan, self.sV = sems
        self.work = []
        self.i = 0
        self.slot_free = [None] * len(self.slots)

    def step(self):
        if self.i >= len(self.work):
            return
        src_ap, col = self.work[self.i]
        slot = self.i % len(self.slots)
        p = self.prog
        waits = [self.slot_free[slot]] if self.slot_free[slot] else []
        p.dma("sync", self.slots[slot][:, :], src_ap, waits=waits, sem=self.sScan)
        p.red(self.parts[:, col:col + 1], self.slots[slot][:, :], ALU.max,
              absval=True, waits=[w(self.sScan)], inc=(self.sV, 1))
        self.slot_free[slot] = w(self.sV)
        self.i += 1

    def drain(self):
        while self.i < len(self.work):
            self.step()
        return w(self.sV)


def build_program(upto=99, debug=False):
    nc = bass.Bass("TRN2", target_bir_lowering=False)
    scratch_kind = "ExternalOutput" if debug else "Internal"

    x_fm = nc.dram_tensor("x_fm", [D, T], F32, kind="ExternalInput").ap()
    x_own_fm = nc.dram_tensor("x_own_fm", [D, NTOK], F32, kind="ExternalInput").ap()
    x_own_tm = nc.dram_tensor("x_own_tm", [NTOK, D], F32, kind="ExternalInput").ap()
    wqT = nc.dram_tensor("wqT", [D, D], F32, kind="ExternalInput").ap()
    wkT = nc.dram_tensor("wkT", [D, D], F32, kind="ExternalInput").ap()
    wvT = nc.dram_tensor("wvT", [D, D], F32, kind="ExternalInput").ap()
    bq_col = nc.dram_tensor("bq_col", [128, H], F32, kind="ExternalInput").ap()
    bk_col = nc.dram_tensor("bk_col", [128, H], F32, kind="ExternalInput").ap()
    bv_bc = nc.dram_tensor("bv_bc", [128, D], F32, kind="ExternalInput").ap()
    wprojT = nc.dram_tensor("wprojT", [D, D], F32, kind="ExternalInput").ap()
    bproj_bc = nc.dram_tensor("bproj_bc", [128, D], F32, kind="ExternalInput").ap()
    w1T = nc.dram_tensor("w1T", [D, DFF], F32, kind="ExternalInput").ap()
    b1_bc = nc.dram_tensor("b1_bc", [128, DFF], F32, kind="ExternalInput").ap()
    w2T = nc.dram_tensor("w2T", [DFF, D], F32, kind="ExternalInput").ap()
    b2_bc = nc.dram_tensor("b2_bc", [128, D], F32, kind="ExternalInput").ap()
    masks = nc.dram_tensor("masks", [8 * 128, 128], BF16, kind="ExternalInput").ap()
    ident_bf = nc.dram_tensor("ident_bf", [128, 128], BF16, kind="ExternalInput").ap()
    ident_f = nc.dram_tensor("ident_f", [128, 128], F32, kind="ExternalInput").ap()
    ones_col_bf = nc.dram_tensor("ones_col_bf", [128, 1], BF16, kind="ExternalInput").ap()
    ones_row_f = nc.dram_tensor("ones_row_f", [1, 128], F32, kind="ExternalInput").ap()
    ones_row_bf = nc.dram_tensor("ones_row_bf", [1, 128], BF16, kind="ExternalInput").ap()

    out = nc.dram_tensor("out", [NTOK, D], F32, kind="ExternalOutput").ap()

    q_scr = nc.dram_tensor("q_scr", [H * 128, NTOK], BF16, kind=scratch_kind).ap()
    k_scr = nc.dram_tensor("k_scr", [H * 128, T], BF16, kind=scratch_kind).ap()
    v_scr = nc.dram_tensor("v_scr", [H * T, HD], BF16, kind=scratch_kind).ap()
    y_scr = nc.dram_tensor("y_scr", [NTOK, D], BF16, kind=scratch_kind).ap()
    x1_scr = nc.dram_tensor("x1_scr", [NTOK, D], BF16, kind=scratch_kind).ap()
    g_scr = nc.dram_tensor("g_scr", [NTOK, DFF], BF16, kind=scratch_kind).ap()
    if debug:
        dbg_sc = nc.dram_tensor("dbg_sc", [128, 16], F32, kind="ExternalOutput").ap()
        dbg_cols = nc.dram_tensor("dbg_cols", [128, 96], F32, kind="ExternalOutput").ap()
        dbg_x1q = nc.dram_tensor("dbg_x1q", [128, H * D], BF16,
                                 kind="ExternalOutput").ap()

    prog = Prog(nc)
    ctx = ExitStack()
    sb = lambda name, shape, dt: ctx.enter_context(nc.sbuf_tensor(name, shape, dt))
    mksem = lambda name: Sem(ctx.enter_context(nc.semaphore(name)))

    cst = {
        "ident_bf": sb("c_ident_bf", [128, 128], BF16),
        "ident_f": sb("c_ident_f", [128, 128], F32),
        "ones_col_bf": sb("c_ones_col_bf", [128, 1], BF16),
        "ones_row_f": sb("c_ones_row_f", [1, 128], F32),
        "ones_row_bf": sb("c_ones_row_bf", [1, 128], BF16),
        "masks": sb("c_masks", [128, 8 * 128], BF16),
        "magic_col": sb("c_magic_col", [128, 1], F32),
    }
    sw1_col = sb("sw1_col", [128, 1], F32)
    gq1_col = sb("gq1_col", [128, 1], F32)
    sw2_col = sb("sw2_col", [128, 1], F32)
    gq2_col = sb("gq2_col", [128, 1], F32)
    amax1_cols = sb("amax1_cols", [128, NTILE], F32)
    dq1_cols = sb("dq1_cols", [128, NTILE], F32)
    s1_cols = sb("s1_cols", [128, NTILE], F32)
    amax2_parts = sb("amax2_parts", [128, NTILE * 16], F32)
    dq2_cols = sb("dq2_cols", [128, NTILE], F32)
    s2_cols = sb("s2_cols", [128, NTILE], F32)
    negc_cols = sb("negc_cols", [128, H], F32)

    p_small = ctx.enter_context(nc.psum_tensor("p_small", [128, 512], F32))

    sLd = mksem("sLd")
    sCast = mksem("sCast")
    sSt = mksem("sSt")
    sT = mksem("sT")
    sV = mksem("sV")
    sA = mksem("sA")
    sP = mksem("sP")
    sB = mksem("sB")
    sScan = mksem("sScan")
    sY = mksem("sY")

    prog.dma("sync", cst["ident_bf"][:, :], ident_bf, sem=sLd)
    prog.dma("sync", cst["ident_f"][:, :], ident_f, sem=sLd)
    prog.dma("sync", cst["ones_col_bf"][:, :], ones_col_bf, sem=sLd)
    prog.dma("sync", cst["ones_row_f"][:, :], ones_row_f, sem=sLd)
    prog.dma("sync", cst["ones_row_bf"][:, :], ones_row_bf, sem=sLd)
    prog.dma("sync", cst["masks"][:, :].rearrange("p (c n) -> p c n", n=128),
             masks.rearrange("(c p) n -> p c n", p=128), sem=sLd)
    prog.memset(cst["magic_col"][:, :], MAGIC, inc=(sV, 1))
    const_ld = w(sLd)

    p1 = ExitStack()
    p1sb = lambda name, shape, dt: p1.enter_context(nc.sbuf_tensor(name, shape, dt))
    wgt_sb = p1sb("wgt_sb", [128, H * D], BF16)
    xg = [p1sb(f"xg{i}", [128, 16 * 1024], BF16) for i in range(2)]
    vst = [p1sb(f"vst{i}", [128, 1024], BF16) for i in range(4)]
    kst = [p1sb(f"kst{i}", [128, 512], BF16) for i in range(4)]
    bias_sb = p1sb("bias_sb", [128, D + 2 * H], F32)
    scan_slots = [p1sb(f"scan{i}", [128, 2048], F32) for i in range(4)]
    wparts = p1sb("wparts", [128, 128], F32)
    pv = [p1.enter_context(nc.psum_tensor(f"pv{i}", [128, 1024], F32)) for i in range(2)]
    pk = [p1.enter_context(nc.psum_tensor(f"pk{i}", [128, 512], F32)) for i in range(2)]

    prog.dma("sync", bias_sb[:, 0:D], bv_bc, sem=sLd)
    prog.dma("sync", bias_sb[:, D:D + H], bq_col, sem=sLd)
    prog.dma("sync", bias_sb[:, D + H:D + 2 * H], bk_col, sem=sLd)
    bias_ld = w(sLd)

    scanner = Scanner(prog, scan_slots, wparts, (sScan, sV))
    for t in range(64):
        r, c = t % 16, t // 16
        scanner.work.append(
            (w1T[128 * r:128 * (r + 1), 2048 * c:2048 * (c + 1)], t))
    for t in range(64):
        scanner.work.append((w2T[128 * t:128 * (t + 1), :], 64 + t))

    def load_weight_bf16(dst_sb, srcT, waits):
        for c in range(16):
            prog.dma("gpsimd", dst_sb[:, D * c:D * (c + 1)],
                     srcT[128 * c:128 * (c + 1), :], waits=waits, sem=sCast)
        return w(sCast)

    def load_x_group(slot, src, col0, waits):
        for c in range(16):
            prog.dma("gpsimd", xg[slot][:, 1024 * c:1024 * (c + 1)],
                     src[128 * c:128 * (c + 1), col0:col0 + 1024], waits=waits,
                     sem=sCast)
        return w(sCast)

    wv_ready = load_weight_bf16(wgt_sb, wvT, [])
    xg_free = [None, None]
    vst_free = [None] * 4
    pv_free = [None, None]
    grp_loaded = [load_x_group(0, x_fm, 0, []), load_x_group(1, x_fm, 1024, [])]
    for grp in range(4):
        for blk in range(8):
            b2 = grp * 8 + blk
            for oh in range(2):
                i2 = b2 * 2 + oh
                ps = pv[i2 % 2]
                mmw = [grp_loaded[grp], wv_ready, pv_free[i2 % 2]]
                for c in range(16):
                    for o2 in range(2):
                        last = (c == 15 and o2 == 1)
                        prog.mm(
                            xg[grp % 2][:, 1024 * c + 128 * blk:1024 * c + 128 * (blk + 1)],
                            wgt_sb[:, D * c + 1024 * oh + 512 * o2:
                                   D * c + 1024 * oh + 512 * (o2 + 1)],
                            ps[:, 512 * o2:512 * (o2 + 1)],
                            c == 0, c == 15,
                            waits=mmw if (c == 0 and o2 == 0) else (),
                            inc=(sT, 1) if last else None)
                mm_done = w(sT)
                vslot = i2 % 4
                prog.tt(ALU.add, vst[vslot][:, :], ps[:, :],
                        bias_sb[:, 1024 * oh:1024 * (oh + 1)],
                        waits=[mm_done, bias_ld, vst_free[vslot]], inc=(sV, 1))
                pv_free[i2 % 2] = w(sV)
                ev_done = w(sV)
                src3 = vst[vslot][:, :].rearrange("p (h j) -> p h j", j=128)
                dst3 = v_scr.rearrange("(h t) j -> t h j", h=H)
                for hh in range(2):
                    h0 = oh * 8 + hh * 4
                    prog.dma("sync",
                             dst3[128 * b2:128 * (b2 + 1), h0:h0 + 4, :],
                             src3[:, 4 * hh:4 * (hh + 1), :],
                             waits=[ev_done], sem=sSt)
                vst_free[vslot] = w(sSt)
                scanner.step()
        if grp + 2 < 4:
            xg_free[grp % 2] = w(sT)
            grp_loaded.append(load_x_group(grp % 2, x_fm, 1024 * (grp + 2),
                                           [xg_free[grp % 2]]))
    v_tensor_done = w(sT)
    v_stores_done = w(sSt)

    def emit_kq(srcT, src_x, ngrp, bias_off, dst_scr, prev_done,
                ps_init=None, st_init=None):
        wrdy = load_weight_bf16(wgt_sb, srcT, [prev_done])
        st_free = [st_init] * 4
        ps_free = [ps_init, ps_init]
        loaded = [load_x_group(0, src_x, 0, [prev_done]),
                  load_x_group(1, src_x, 1024, [prev_done])]
        cnt = 0
        for grp in range(ngrp):
            for h in range(H):
                for half in range(2):
                    ps = pk[cnt % 2]
                    mmw = [loaded[grp], wrdy, ps_free[cnt % 2]]
                    for c in range(16):
                        prog.mm(
                            wgt_sb[:, D * c + 128 * h:D * c + 128 * (h + 1)],
                            xg[grp % 2][:, 1024 * c + 512 * half:1024 * c + 512 * (half + 1)],
                            ps[:, :], c == 0, c == 15,
                            waits=mmw if c == 0 else (),
                            inc=(sT, 1) if c == 15 else None)
                    mm_done = w(sT)
                    slot = cnt % 4
                    prog.act(kst[slot][:, :], ps[:, :], AF.Identity,
                             bias=bias_sb[:, bias_off + h:bias_off + h + 1],
                             waits=[mm_done, bias_ld, st_free[slot]], inc=(sA, 1))
                    ps_free[cnt % 2] = w(sA)
                    ev_done = w(sA)
                    prog.dma("sync",
                             dst_scr[128 * h:128 * (h + 1),
                                     1024 * grp + 512 * half:1024 * grp + 512 * (half + 1)],
                             kst[slot][:, :], waits=[ev_done], sem=sSt)
                    st_free[slot] = w(sSt)
                    cnt += 1
                    scanner.step()
            if grp + 2 < ngrp:
                gfree = w(sT)
                loaded.append(load_x_group(grp % 2, src_x, 1024 * (grp + 2), [gfree]))
        return w(sT), w(sSt)

    k_tensor_done, k_stores_done = emit_kq(wkT, x_fm, 4, D + H, k_scr, v_tensor_done)
    k_evac_done = w(sA)
    q_tensor_done, q_stores_done = emit_kq(wqT, x_own_fm, 2, D, q_scr, k_tensor_done,
                                           ps_init=k_evac_done, st_init=k_stores_done)

    scan_red_done = scanner.drain()

    def finish_scan(tag, part_off, s_col, gq_col):
        col = p1sb(f"{tag}_col", [128, 1], F32)
        g1 = p1sb(f"{tag}_g1", [1, 1], F32)
        g2 = p1sb(f"{tag}_g2", [1, 2], F32)
        prog.red(col[:, :], wparts[:, part_off:part_off + 64], ALU.max,
                 waits=[scan_red_done], inc=(sV, 1))
        vdone = w(sV)
        prog.mm(col[:, :], cst["ident_f"][:, :], p_small[0:1, 0:128], True, True,
                waits=[const_ld, vdone], inc=(sT, 1))
        prog.red(g1[:, :], p_small[0:1, 0:128], ALU.max, waits=[w(sT)], inc=(sV, 1))
        prog.recip(g2[:, 0:1], g1[:, :], waits=[w(sV)], inc=(sV, 1))
        prog.ts(g2[:, 0:1], g2[:, 0:1], Q8, None, ALU.mult, waits=[w(sV)], inc=(sV, 1))
        prog.ts(g2[:, 1:2], g1[:, :], INV_Q8SQ, None, ALU.mult, waits=[w(sV)],
                inc=(sV, 1))
        prog.mm(cst["ones_row_f"][:, :], g2[:, :], p_small[:, 130:132], True, True,
                waits=[w(sV)], inc=(sT, 1))
        tdone = w(sT)
        prog.copy(s_col[:, :], p_small[:, 130:131], waits=[tdone], inc=(sV, 1))
        prog.copy(gq_col[:, :], p_small[:, 131:132], waits=[tdone], inc=(sV, 1))
        return w(sV)

    sc1_ready = finish_scan("w1s", 0, sw1_col, gq1_col)
    sc2_ready = finish_scan("w2s", 64, sw2_col, gq2_col)

    if debug:
        prog.dma("gpsimd", dbg_sc[:, 0:1], sw1_col[:, :], waits=[sc1_ready], sem=sSt, slow_ok=True)
        prog.dma("gpsimd", dbg_sc[:, 1:2], gq1_col[:, :], waits=[sc1_ready], sem=sSt, slow_ok=True)
        prog.dma("gpsimd", dbg_sc[:, 2:3], sw2_col[:, :], waits=[sc2_ready], sem=sSt, slow_ok=True)
        prog.dma("gpsimd", dbg_sc[:, 3:4], gq2_col[:, :], waits=[sc2_ready], sem=sSt, slow_ok=True)

    def finish(last_waits):
        prog.emit("gpsimd", lambda e: e.nop(), tuple(last_waits), None)
        prog.barrier(sB)
        prog.replay()
        ctx.close()
        return nc

    prog.barrier(sB)
    p1.close()

    if upto <= 1:
        return finish([w(sSt), w(sA), w(sV), w(sT), w(sCast), w(sScan)])

    p2 = ExitStack()
    p2sb = lambda name, shape, dt: p2.enter_context(nc.sbuf_tensor(name, shape, dt))
    k_sb = [p2sb(f"k_sb{i}", [128, T], BF16) for i in range(2)]
    v_sb = [p2sb(f"v_sb{i}", [128, T], BF16) for i in range(2)]
    qh_sb = [p2sb(f"qh_sb{i}", [128, NTOK], BF16) for i in range(2)]
    sq_scr = p2sb("sq_scr", [128, T], BF16)
    ring1 = p2sb("ring1", [128, 8 * 512], BF16)
    ring2 = p2sb("ring2", [128, 8 * 512], BF16)
    linv_sb = p2sb("linv_sb", [1, 512], F32)
    linv_bc = p2sb("linv_bc", [128, 512], F32)
    lnl8 = p2sb("lnl8", [1, 512], F32)
    lnl2 = p2sb("lnl2", [1, 512], F32)
    negmh = [p2sb(f"negmh{i}", [1, 512], BF16) for i in range(2)]
    cstar = p2sb("cstar", [1, 8], F32)
    nrm_col = p2sb("nrm_col", [128, 1], F32)
    negc8_cols = p2sb("negc8_cols", [128, H], F32)
    negc_sc = p2sb("negc_sc", [1, H], F32)
    yst = [p2sb(f"yst{i}", [128, 512], BF16) for i in range(2)]
    ps_s = [p2.enter_context(nc.psum_tensor(f"ps_s{i}", [128, 512], F32))
            for i in range(4)]
    ps_y = p2.enter_context(nc.psum_tensor("ps_y", [128, 512], F32))
    ps_l = p2.enter_context(nc.psum_tensor("ps_l", [128, 512], F32))
    ps_misc = p2.enter_context(nc.psum_tensor("ps_misc", [128, 512], F32))

    p1_done = [q_tensor_done, q_stores_done, k_stores_done, v_stores_done,
               sc1_ready, sc2_ready]

    def load_head(h, extra):
        hb = h % 2
        waits = list(extra)
        prog.dma("sync", k_sb[hb][:, :], k_scr[128 * h:128 * (h + 1), :],
                 waits=waits, sem=sY)
        vsrc = v_scr[h * T:(h + 1) * T, :].rearrange("(c p) j -> p c j", p=128)
        vdst = v_sb[hb][:, :].rearrange("p (c j) -> p c j", j=128)
        for part in range(4):
            prog.dma("sync", vdst[:, 8 * part:8 * (part + 1), :],
                     vsrc[:, 8 * part:8 * (part + 1), :], waits=waits, sem=sY)
        prog.dma("sync", qh_sb[hb][:, :], q_scr[128 * h:128 * (h + 1), :],
                 waits=waits, sem=sY)
        return w(sY)

    loads_done = [None] * (H + 1)
    loads_done[0] = load_head(0, p1_done)
    sbuf_cons = [None] * (H + 2)
    misc_free = None
    psmall_free = sc2_ready
    psy_free = None
    psl_free = None
    yst_free = [None, None]
    s_exp_done = {}
    ring1_cons = {}
    ring2_cons = {}
    spin = 0
    g1 = 0
    g2 = 0
    negmh_ready = [None] * NJS
    negmh_free = [None, None]
    lnl8_free = [None]
    lnl2_free = [None]
    linv_free = [None]

    LAG = 6

    def emit_sweep1(h, js):
        nonlocal spin, g1, misc_free
        hb = h % 2
        C = 8 * js + 8
        negc8_ready = (sA, negc8_done[h])
        pend = []

        def flush1():
            qlo_, slot_, lw_, ci_ = pend.pop(0)
            prog.mm(cst["ones_col_bf"][:, :],
                    ring1[:, 512 * slot_ + qlo_:512 * (slot_ + 1)],
                    ps_misc[0:1, qlo_:512], ci_ == 0, ci_ == C - 1,
                    waits=lw_, inc=(sT, 1))
            ring1_cons[g1_base + ci_] = w(sT)

        g1_base = g1
        for ci in range(C):
            cp = ci - 8 * js
            qlo = 128 * I0[cp] if cp >= 0 else 0
            slot = ci % 8
            prog.mm(k_sb[hb][:, 128 * ci:128 * (ci + 1)],
                    qh_sb[hb][:, 512 * js + qlo:512 * (js + 1)],
                    ps_s[spin % 4][:, qlo:512], True, True,
                    waits=[loads_done[h], negc8_ready, s_exp_done.get(spin - 4)],
                    inc=(sT, 1))
            smm_done = w(sT)
            prog.act(ring1[:, 512 * slot + qlo:512 * (slot + 1)],
                     ps_s[spin % 4][:, qlo:512], AF.Exp,
                     bias=negc8_cols[:, h:h + 1], scale=INV_SQRT_HD / 8.0,
                     waits=[smm_done, ring1_cons.get(g1_base + ci - 8)], inc=(sA, 1))
            s_exp_done[spin] = w(sA)
            expd = w(sA)
            if cp >= 0:
                col = 512 * slot + qlo
                prog.tt(ALU.mult, ring1[:, col:col + 128],
                        ring1[:, col:col + 128],
                        cst["masks"][:, 128 * cp:128 * (cp + 1)],
                        waits=[expd, const_ld], inc=(sV, 1))
                mask_done = w(sV)
            else:
                mask_done = expd
            pend.append((qlo, slot, [expd, mask_done,
                                     misc_free if ci == 0 else None], ci))
            if len(pend) > LAG:
                flush1()
            spin += 1
        while pend:
            flush1()
        g1 += C
        l8_done = w(sT)
        prog.act(lnl8[:, :], ps_misc[0:1, 0:512], AF.Ln,
                 waits=[l8_done, lnl8_free[0]], inc=(sA, 1))
        misc_free = w(sA)
        prog.ts(negmh[js % 2][:, :], lnl8[:, :], float(-8.0 / INV_SQRT_HD),
                negc_sc[0:1, h:h + 1], ALU.mult, ALU.add,
                waits=[w(sA), negmh_free[js % 2]], inc=(sV, 1))
        lnl8_free[0] = w(sV)
        negmh_ready[js] = w(sV)

    def emit_sweep2(h, js):
        nonlocal spin, g2, misc_free, psy_free, psl_free
        hb = h % 2
        C = 8 * js + 8
        pend = []

        def flush2():
            qlo_, slot_, lw_, ci_ = pend.pop(0)
            prog.mm(cst["ones_col_bf"][:, :],
                    ring2[:, 512 * slot_ + qlo_:512 * (slot_ + 1)],
                    ps_l[0:1, qlo_:512], ci_ == 0, ci_ == C - 1,
                    waits=lw_)
            prog.mm(v_sb[hb][:, 128 * ci_:128 * (ci_ + 1)],
                    ring2[:, 512 * slot_ + qlo_:512 * (slot_ + 1)],
                    ps_y[:, qlo_:512], ci_ == 0, ci_ == C - 1,
                    inc=(sT, 1))
            ring2_cons[g2_base + ci_] = w(sT)

        g2_base = g2
        for ci in range(C):
            cp = ci - 8 * js
            qlo = 128 * I0[cp] if cp >= 0 else 0
            slot = ci % 8
            reg = ps_s[spin % 4][:, qlo:512]
            prog.mm(cst["ones_row_bf"][:, :], negmh[js % 2][0:1, qlo:512],
                    reg, True, False,
                    waits=[loads_done[h], negmh_ready[js],
                           s_exp_done.get(spin - 4)])
            prog.mm(k_sb[hb][:, 128 * ci:128 * (ci + 1)],
                    qh_sb[hb][:, 512 * js + qlo:512 * (js + 1)],
                    reg, False, True, inc=(sT, 1))
            smm_done = w(sT)
            prog.act(ring2[:, 512 * slot + qlo:512 * (slot + 1)],
                     ps_s[spin % 4][:, qlo:512], AF.Exp, scale=INV_SQRT_HD,
                     waits=[smm_done, ring2_cons.get(g2_base + ci - 8)], inc=(sA, 1))
            s_exp_done[spin] = w(sA)
            expd = w(sA)
            if cp >= 0:
                col = 512 * slot + qlo
                prog.tt(ALU.mult, ring2[:, col:col + 128],
                        ring2[:, col:col + 128],
                        cst["masks"][:, 128 * cp:128 * (cp + 1)],
                        waits=[expd, const_ld], inc=(sV, 1))
                mask_done = w(sV)
            else:
                mask_done = expd
            pend.append((qlo, slot,
                         [expd, mask_done,
                          psl_free if ci == 0 else None,
                          psy_free if ci == 0 else None], ci))
            if len(pend) > LAG:
                flush2()
            spin += 1
        while pend:
            flush2()
        g2 += C
        av_done = w(sT)
        prog.recip(linv_sb[:, :], ps_l[0:1, 0:512],
                   waits=[av_done, linv_free[0]], inc=(sV, 1))
        prog.mm(cst["ones_row_f"][:, :], linv_sb[:, :], ps_l[:, 0:512],
                True, True, waits=[w(sV)], inc=(sT, 1))
        linv_free[0] = w(sT)
        prog.act(linv_bc[:, :], ps_l[:, 0:512], AF.Identity,
                 waits=[w(sT)], inc=(sA, 1))
        psl_free = w(sA)
        ysl = (h * NJS + js) % 2
        prog.tt(ALU.mult, yst[ysl][:, :], ps_y[:, :], linv_bc[:, :],
                waits=[psl_free, yst_free[ysl]], inc=(sV, 1))
        psy_free = w(sV)
        ynorm_done = w(sV)
        for qt in range(4):
            qtg = 4 * js + qt
            prog.dma("sync",
                     y_scr[128 * qtg:128 * (qtg + 1), 128 * h:128 * (h + 1)],
                     yst[ysl][:, 128 * qt:128 * (qt + 1)],
                     waits=[ynorm_done], sem=sSt)
        yst_free[ysl] = w(sSt)
        negmh_free[js % 2] = av_done

    negc8_done = [None] * H
    for h in range(H):
        hb = h % 2
        prog.tt(ALU.mult, sq_scr[:, :], k_sb[hb][:, :], k_sb[hb][:, :],
                waits=[loads_done[h]], inc=(sV, 1))
        sqk = w(sV)
        for c in range(32):
            prog.mm(sq_scr[:, 128 * c:128 * (c + 1)], cst["ones_col_bf"][:, :],
                    ps_misc[:, c:c + 1], True, True,
                    waits=[sqk, misc_free] if c == 0 else (),
                    inc=(sT, 1) if c == 31 else None)
        knorm_mm = w(sT)
        prog.red(nrm_col[:, :], ps_misc[:, 0:32], ALU.max, waits=[knorm_mm],
                 inc=(sV, 1))
        misc_free = w(sV)
        prog.mm(nrm_col[:, :], cst["ident_f"][:, :], p_small[0:1, 0:128],
                True, True, waits=[w(sV), psmall_free], inc=(sT, 1))
        prog.red(cstar[:, 1:2], p_small[0:1, 0:128], ALU.max, waits=[w(sT)],
                 inc=(sV, 1))
        kn_done = w(sV)
        prog.tt(ALU.mult, sq_scr[:, 0:NTOK], qh_sb[hb][:, :], qh_sb[hb][:, :],
                waits=[knorm_mm], inc=(sV, 1))
        sqq = w(sV)
        for c in range(16):
            prog.mm(sq_scr[:, 128 * c:128 * (c + 1)], cst["ones_col_bf"][:, :],
                    ps_misc[:, 32 + c:33 + c], True, True,
                    waits=[sqq, misc_free] if c == 0 else (),
                    inc=(sT, 1) if c == 15 else None)
        qnorm_mm = w(sT)
        prog.red(nrm_col[:, :], ps_misc[:, 32:48], ALU.max,
                 waits=[qnorm_mm, kn_done], inc=(sV, 1))
        misc_free = w(sV)
        prog.mm(nrm_col[:, :], cst["ident_f"][:, :], p_small[0:1, 0:128],
                True, True, waits=[w(sV), kn_done], inc=(sT, 1))
        prog.red(cstar[:, 0:1], p_small[0:1, 0:128], ALU.max, waits=[w(sT)],
                 inc=(sV, 1))
        prog.tt(ALU.mult, cstar[:, 2:3], cstar[:, 0:1], cstar[:, 1:2],
                waits=[w(sV)], inc=(sV, 1))
        prog.act(cstar[:, 3:4], cstar[:, 2:3], AF.Ln, scale=float(1.0 / HD),
                 waits=[w(sV)], inc=(sA, 1))
        prog.act(cstar[:, 4:5], cstar[:, 3:4], AF.Exp, scale=0.5,
                 waits=[w(sA)], inc=(sA, 1))
        cs_done = w(sA)
        prog.ts(cstar[:, 5:6], cstar[:, 4:5], -0.125, None, ALU.mult,
                waits=[cs_done], inc=(sV, 1))
        prog.ts(negc_sc[0:1, h:h + 1], cstar[:, 4:5],
                float(-1.0 / INV_SQRT_HD), None, ALU.mult,
                waits=[cs_done], inc=(sV, 1))
        prog.mm(cst["ones_row_f"][:, :], cstar[:, 5:6], p_small[:, 130:131],
                True, True, waits=[w(sV)], inc=(sT, 1))
        prog.act(negc8_cols[:, h:h + 1], p_small[:, 130:131], AF.Identity,
                 waits=[w(sT)], inc=(sA, 1))
        psmall_free = w(sA)
        negc8_done[h] = sA.n

        if h + 1 < H:
            loads_done[h + 1] = load_head(h + 1, [sbuf_cons[h + 1]])

        emit_sweep1(h, 0)
        for js in range(1, NJS):
            emit_sweep1(h, js)
            emit_sweep2(h, js - 1)
        emit_sweep2(h, NJS - 1)
        sbuf_cons[h + 2] = w(sT)

    y_stores_done = w(sSt)
    prog.barrier(sB)
    p2.close()

    if upto <= 2:
        return finish([w(sSt), w(sA), w(sV), w(sT), w(sCast), w(sScan), w(sY)])

    p2_done = [w(sT), w(sA), w(sV), y_stores_done]

    ln_m = sb("ln_m", [128, NTILE], F32)
    ln_rs = sb("ln_rs", [128, NTILE], F32)
    ln_mrs = sb("ln_mrs", [128, NTILE], F32)
    p3ab = ExitStack()
    x1q_fm = p3ab.enter_context(nc.sbuf_tensor("x1q_fm", [128, H * D], BF16))

    p3a = ExitStack()
    asb = lambda name, shape, dt: p3a.enter_context(nc.sbuf_tensor(name, shape, dt))
    wproj_sb = asb("wproj_sb", [128, H * D], BF16)
    yt_sb = [asb(f"yt{i}", [128, D], BF16) for i in range(2)]
    xot_sb = [asb(f"xot{i}", [128, D], BF16) for i in range(2)]
    bprj_sb = asb("bprj_sb", [128, D], BF16)
    tmp_sb = [asb(f"tmpa{i}", [128, D], BF16) for i in range(2)]
    r_sb = [asb(f"ra{i}", [128, D], BF16) for i in range(2)]
    rq_scr = asb("rq_scr", [128, 1024], BF16)
    x1f = asb("x1f", [128, D], F32)
    rndt = asb("rndt", [128, D], F32)
    x1q_tm = [asb(f"x1q_tm{i}", [128, D], BF16) for i in range(2)]
    lnp_sum = asb("lnp_sum", [128, 32], F32)
    lnp_sq = asb("lnp_sq", [128, 32], F32)
    cols_a = asb("cols_a", [128, 8], F32)
    nmag_col = asb("nmag_col", [128, 1], F32)
    ps_pj = [p3a.enter_context(nc.psum_tensor(f"ps_pj{i}", [128, 1024], F32))
             for i in range(2)]
    ps_tr = [p3a.enter_context(nc.psum_tensor(f"ps_tr{i}", [128, 512], F32))
             for i in range(2)]

    prog.memset(nmag_col[:, :], -MAGIC, waits=p2_done, inc=(sV, 1))
    for c in range(16):
        prog.dma("gpsimd", wproj_sb[:, D * c:D * (c + 1)],
                 wprojT[128 * c:128 * (c + 1), :], waits=p2_done, sem=sCast)
    prog.dma("gpsimd", bprj_sb[:, :], bproj_bc, waits=p2_done, sem=sCast)
    wproj_ready = w(sCast)

    yt_free = [p2_done[0], p2_done[0]]
    xot_free = [None, None]
    tmp_free = [None, None]
    r_free = [None, None]
    cols_free = None
    x1f_free = None
    pj_free = [None, None]
    tr_free = [None, None]
    rndt_free = None
    x1qtm_free = [None, None]
    rq_free = None
    pend_tr = []

    def flush_tr():
        if not pend_tr:
            return
        t_, tb_, ready_ = pend_tr.pop(0)
        for c in range(16):
            prog.mm(x1q_tm[tb_][:, 128 * c:128 * (c + 1)], cst["ident_bf"][:, :],
                    ps_tr[(c // 4) % 2][:, 128 * (c % 4):128 * (c % 4 + 1)],
                    True, True,
                    waits=[ready_, const_ld, tr_free[(c // 4) % 2]]
                    if c % 4 == 0 else (),
                    inc=(sT, 1) if c % 4 == 3 else None)
            if c % 4 == 3:
                grp_done = w(sT)
                dst = x1q_fm[:, :].rearrange("p (cc tk) -> p cc tk", tk=D)[
                    :, c - 3:c + 1, 128 * t_:128 * (t_ + 1)]
                prog.act(dst, ps_tr[(c // 4) % 2][:, :], AF.Identity,
                         waits=[grp_done], inc=(sA, 1))
                tr_free[(c // 4) % 2] = w(sA)
        x1qtm_free[tb_] = w(sT)

    def tile_loads_a(t):
        tb = t % 2
        prog.dma("sync", yt_sb[tb][:, :], y_scr[128 * t:128 * (t + 1), :],
                 waits=[yt_free[tb], y_stores_done], sem=sY)
        prog.dma("gpsimd", xot_sb[tb][:, :], x_own_tm[128 * t:128 * (t + 1), :],
                 waits=[xot_free[tb]], sem=sCast)
        return w(sY), w(sCast)

    loads_a = [tile_loads_a(0), tile_loads_a(1)]
    for t in range(NTILE):
        tb = t % 2
        y_ld, x_ld = loads_a[t]
        prog.tt(ALU.add, tmp_sb[tb][:, :], xot_sb[tb][:, :], bprj_sb[:, :],
                waits=[x_ld, wproj_ready, tmp_free[tb]], inc=(sP, 1), eng="gpsimd")
        tmp_ready = w(sP)
        for half in range(2):
            ps = ps_pj[half]
            mmw = [y_ld, wproj_ready, pj_free[half]]
            for c in range(16):
                for o2 in range(2):
                    of0 = 1024 * half + 512 * o2
                    prog.mm(yt_sb[tb][:, 128 * c:128 * (c + 1)],
                            wproj_sb[:, D * c + of0:D * c + of0 + 512],
                            ps[:, 512 * o2:512 * (o2 + 1)],
                            c == 0, c == 15,
                            waits=mmw if (c == 0 and o2 == 0) else (),
                            inc=(sT, 1) if (c == 15 and o2 == 1) else None)
            mm_done = w(sT)
            prog.stt(r_sb[tb][:, 1024 * half:1024 * (half + 1)], ps[:, :], 1.0,
                     tmp_sb[tb][:, 1024 * half:1024 * (half + 1)], ALU.mult, ALU.add,
                     accum=lnp_sum[:, 2 * t % 32 + half:2 * t % 32 + half + 1],
                     waits=[mm_done, tmp_ready] + (r_free[tb] or []), inc=(sV, 1))
            pj_free[half] = w(sV)
            prog.act(rq_scr[:, :], r_sb[tb][:, 1024 * half:1024 * (half + 1)],
                     AF.Square, accum=lnp_sq[:, 2 * t % 32 + half:2 * t % 32 + half + 1],
                     waits=[w(sV), rq_free], inc=(sA, 1))
            rq_free = w(sA)
        yt_free[tb] = w(sT)
        flush_tr()
        r_ready = w(sV)
        sq_ready = w(sA)
        prog.dma("sync", x1_scr[128 * t:128 * (t + 1), :], r_sb[tb][:, :],
                 waits=[r_ready], sem=sSt)
        r_spill = w(sSt)
        i0c, i1c = 2 * t % 32, 2 * t % 32 + 1
        prog.tt(ALU.add, cols_a[:, 0:1], lnp_sum[:, i0c:i0c + 1],
                lnp_sum[:, i1c:i1c + 1], waits=[r_ready, cols_free], inc=(sV, 1))
        prog.ts(ln_m[:, t:t + 1], cols_a[:, 0:1], 1.0 / D, None, ALU.mult,
                waits=[w(sV)], inc=(sV, 1))
        prog.tt(ALU.add, cols_a[:, 1:2], lnp_sq[:, i0c:i0c + 1],
                lnp_sq[:, i1c:i1c + 1], waits=[sq_ready], inc=(sV, 1))
        prog.ts(cols_a[:, 1:2], cols_a[:, 1:2], 1.0 / D, None, ALU.mult,
                waits=[w(sV)], inc=(sV, 1))
        prog.act(cols_a[:, 2:3], ln_m[:, t:t + 1], AF.Square, waits=[w(sV)],
                 inc=(sA, 1))
        prog.tt(ALU.subtract, cols_a[:, 3:4], cols_a[:, 1:2], cols_a[:, 2:3],
                waits=[w(sA), w(sV)], inc=(sV, 1))
        prog.ts(cols_a[:, 3:4], cols_a[:, 3:4], EPS, None, ALU.add,
                waits=[w(sV)], inc=(sV, 1))
        prog.recip(cols_a[:, 4:5], cols_a[:, 3:4], waits=[w(sV)], inc=(sV, 1))
        prog.act(ln_rs[:, t:t + 1], cols_a[:, 4:5], AF.Sqrt, waits=[w(sV)],
                 inc=(sA, 1))
        rstd_ready = w(sA)
        prog.tt(ALU.mult, ln_mrs[:, t:t + 1], ln_m[:, t:t + 1], ln_rs[:, t:t + 1],
                waits=[rstd_ready], inc=(sV, 1))
        prog.ts(ln_mrs[:, t:t + 1], ln_mrs[:, t:t + 1], -1.0, None, ALU.mult,
                waits=[w(sV)], inc=(sV, 1))
        prog.red(cols_a[:, 5:6], r_sb[tb][:, :], ALU.max, waits=[r_ready],
                 inc=(sV, 1))
        prog.red(cols_a[:, 6:7], r_sb[tb][:, :], ALU.min, waits=[w(sV)],
                 inc=(sV, 1))
        prog.tt(ALU.subtract, cols_a[:, 5:6], cols_a[:, 5:6], ln_m[:, t:t + 1],
                waits=[w(sV)], inc=(sV, 1))
        prog.tt(ALU.subtract, cols_a[:, 6:7], ln_m[:, t:t + 1], cols_a[:, 6:7],
                waits=[w(sV)], inc=(sV, 1))
        prog.tt(ALU.max, cols_a[:, 5:6], cols_a[:, 5:6], cols_a[:, 6:7],
                waits=[w(sV)], inc=(sV, 1))
        prog.tt(ALU.mult, amax1_cols[:, t:t + 1], cols_a[:, 5:6], ln_rs[:, t:t + 1],
                waits=[w(sV), rstd_ready], inc=(sV, 1))
        amax_ready = w(sV)
        prog.tt(ALU.mult, dq1_cols[:, t:t + 1], amax1_cols[:, t:t + 1],
                gq1_col[:, :], waits=[amax_ready, sc1_ready], inc=(sV, 1))
        prog.recip(cols_a[:, 7:8], amax1_cols[:, t:t + 1], waits=[amax_ready],
                   inc=(sV, 1))
        prog.ts(s1_cols[:, t:t + 1], cols_a[:, 7:8], Q8, None, ALU.mult,
                waits=[w(sV)], inc=(sV, 1))
        qsc_ready = w(sV)
        prog.act(x1f[:, :], r_sb[tb][:, :], AF.Identity, bias=ln_mrs[:, t:t + 1],
                 scale=ln_rs[:, t:t + 1], waits=[qsc_ready, x1f_free], inc=(sA, 1))
        r_free[tb] = [r_spill, w(sA)]
        cols_free = w(sA)
        prog.act(rndt[:, :], x1f[:, :], AF.Identity, bias=cst["magic_col"][:, :],
                 scale=s1_cols[:, t:t + 1], waits=[w(sA), rndt_free], inc=(sA, 1))
        x1f_free = w(sA)
        prog.ts(x1q_tm[tb][:, :], rndt[:, :], -MAGIC, None, ALU.add,
                waits=[w(sA), x1qtm_free[tb]], inc=(sV, 1))
        rndt_free = w(sV)
        x1q_ready = w(sV)
        pend_tr.append((t, tb, x1q_ready))
        xot_free[tb] = w(sP)
        tmp_free[tb] = w(sV)
        if t + 2 < NTILE:
            loads_a.append(tile_loads_a(t + 2))
    flush_tr()
    x1q_done = w(sA)
    r_spills_done = w(sSt)

    if debug:
        prog.dma("gpsimd", dbg_x1q, x1q_fm[:, :], waits=[x1q_done], sem=sSt)
        dcw = [x1q_done, w(sV)]
        prog.dma("gpsimd", dbg_cols[:, 0:16], amax1_cols[:, :], waits=dcw, sem=sSt)
        prog.dma("gpsimd", dbg_cols[:, 16:32], s1_cols[:, :], waits=dcw, sem=sSt)
        prog.dma("gpsimd", dbg_cols[:, 32:48], ln_m[:, :], waits=dcw, sem=sSt)
        prog.dma("gpsimd", dbg_cols[:, 48:64], ln_rs[:, :], waits=dcw, sem=sSt)

    prog.barrier(sB)
    p3a.close()

    if upto <= 31:
        p3ab.close()
        return finish([w(sSt), w(sA), w(sV), w(sT), w(sCast), w(sScan), w(sY)])

    p3b = ExitStack()
    bsb = lambda name, shape, dt: p3b.enter_context(nc.sbuf_tensor(name, shape, dt))
    w1q_sb = [bsb(f"w1q{i}", [128, 16 * 512], BF16) for i in range(2)]
    w1st = [bsb(f"w1st{i}", [128, 512], F32) for i in range(2)]
    w1rnd = bsb("w1rnd", [128, 512], F32)
    b1_sb = bsb("b1_sb", [128, DFF], BF16)
    x2pre = [bsb(f"x2pre{i}", [128, 512], F32) for i in range(2)]
    gt_sb = [bsb(f"gt{i}", [128, 512], BF16) for i in range(2)]
    ps_f1 = [p3b.enter_context(nc.psum_tensor(f"ps_f1{i}", [128, 512], F32))
             for i in range(4)]

    p3a_done = [x1q_done, r_spills_done, w(sT), w(sV), w(sP)]
    for c in range(4):
        prog.dma("gpsimd", b1_sb[:, 2048 * c:2048 * (c + 1)],
                 b1_bc[:, 2048 * c:2048 * (c + 1)], waits=p3a_done, sem=sCast)
    b1_ready = w(sCast)

    w1st_free = [None, None]
    w1rnd_free = None
    w1q_free = [None, None]
    x2_free = [None, None]
    gt_free = [None, None]
    f1_free = [None, None, None, None]

    def w1_chunk(o, c):
        nonlocal w1rnd_free
        ob = o % 2
        cs = c % 2
        prog.dma("sync", w1st[cs][:, :],
                 w1T[128 * c:128 * (c + 1), 512 * o:512 * (o + 1)],
                 waits=[w1st_free[cs], w1q_free[ob] if c == 0 else None]
                 + (p3a_done if o == 0 and c == 0 else []), sem=sLd)
        prog.act(w1rnd[:, :], w1st[cs][:, :], AF.Identity,
                 bias=cst["magic_col"][:, :], scale=sw1_col[:, :],
                 waits=[w(sLd), sc1_ready, w1rnd_free], inc=(sA, 1))
        w1st_free[cs] = w(sA)
        prog.ts(w1q_sb[ob][:, 512 * c:512 * (c + 1)], w1rnd[:, :], -MAGIC,
                None, ALU.add, waits=[w(sA)], inc=(sV, 1))
        w1rnd_free = w(sV)

    for c in range(16):
        w1_chunk(0, c)
    w1_ready = [w(sV)]
    for o in range(16):
        ob = o % 2
        for t in range(NTILE):
            if o + 1 < 16:
                w1_chunk(o + 1, t)
                if t == NTILE - 1:
                    w1_ready.append(w(sV))
            ps = ps_f1[t % 4]
            mmw = [w1_ready[o], x1q_done, f1_free[t % 4]]
            for c in range(16):
                prog.mm(x1q_fm[:, D * c + 128 * t:D * c + 128 * (t + 1)],
                        w1q_sb[ob][:, 512 * c:512 * (c + 1)],
                        ps[:, :], c == 0, c == 15,
                        waits=mmw if c == 0 else (),
                        inc=(sT, 1) if c == 15 else None)
            mm_done = w(sT)
            tb2 = t % 2
            prog.stt(x2pre[tb2][:, :], ps[:, :], dq1_cols[:, t:t + 1],
                     b1_sb[:, 512 * o:512 * (o + 1)], ALU.mult, ALU.add,
                     waits=[mm_done, b1_ready, x2_free[tb2]], inc=(sV, 1))
            f1_free[t % 4] = w(sV)
            prog.act(gt_sb[tb2][:, :], x2pre[tb2][:, :], AF.Gelu_apprx_tanh,
                     waits=[w(sV), gt_free[tb2]], inc=(sA, 1))
            x2_free[tb2] = w(sA)
            gelu_done = w(sA)
            prog.red(amax2_parts[:, 16 * t + o:16 * t + o + 1], gt_sb[tb2][:, :],
                     ALU.max, absval=True, waits=[gelu_done], inc=(sV, 1))
            prog.dma("sync", g_scr[128 * t:128 * (t + 1), 512 * o:512 * (o + 1)],
                     gt_sb[tb2][:, :], waits=[gelu_done], sem=sSt)
            gt_free[tb2] = w(sSt)
        w1q_free[ob] = w(sT)
    amax2c = bsb("amax2c", [128, NTILE], F32)
    for t in range(NTILE):
        prog.red(amax2c[:, t:t + 1], amax2_parts[:, 16 * t:16 * (t + 1)],
                 ALU.max, waits=[w(sV)], inc=(sV, 1))
        prog.tt(ALU.mult, dq2_cols[:, t:t + 1], amax2c[:, t:t + 1],
                gq2_col[:, :], waits=[w(sV), sc2_ready], inc=(sV, 1))
        prog.recip(s2_cols[:, t:t + 1], amax2c[:, t:t + 1],
                   waits=[w(sV)], inc=(sV, 1))
        prog.ts(s2_cols[:, t:t + 1], s2_cols[:, t:t + 1], Q8, None, ALU.mult,
                waits=[w(sV)], inc=(sV, 1))
    s2_ready = w(sV)
    g_stores_done = w(sSt)

    if debug:
        prog.dma("gpsimd", dbg_cols[:, 64:80], s2_cols[:, :], waits=[s2_ready], sem=sSt)
        prog.dma("gpsimd", dbg_cols[:, 80:96], dq2_cols[:, :], waits=[s2_ready], sem=sSt)

    prog.barrier(sB)
    p3b.close()
    p3ab.close()

    if upto <= 32:
        return finish([w(sSt), w(sA), w(sV), w(sT), w(sCast), w(sScan), w(sY)])

    NB = 8
    p3c = ExitStack()
    csb = lambda name, shape, dt: p3c.enter_context(nc.sbuf_tensor(name, shape, dt))
    oacc = csb("oacc", [128, NTILE * 1024], F32)
    w2q_sb = [csb(f"w2q{i}", [128, 8 * 1024], BF16) for i in range(2)]
    w2st = [csb(f"w2st{i}", [128, 1024], F32) for i in range(2)]
    rnd_scr = csb("rnd_scr", [128, 2048], F32)
    g_in = [csb(f"g_in{i}", [128, 1024], BF16) for i in range(2)]
    x2q_tm = [csb(f"x2q_tm{i}", [128, 1024], BF16) for i in range(2)]
    x2q_fm = [csb(f"x2q_fm{i}", [128, 1024], BF16) for i in range(2)]
    rres = [csb(f"rres{i}", [128, 1024], BF16) for i in range(2)]
    xrec = csb("xrec", [128, 1024], F32)
    tmp2 = csb("tmp2", [128, 1024], F32)
    b2_sb = csb("b2_sb", [128, D], F32)
    r2h0 = csb("r2h0", [128, 1024], BF16)
    r2h1 = csb("r2h1", [128, 1024], F32)
    outst = [csb(f"outst{i}", [128, 1024], F32) for i in range(2)]
    ln2_sum = csb("ln2_sum", [128, 32], F32)
    ln2_sq = csb("ln2_sq", [128, 32], F32)
    cols_c = csb("cols_c", [128, 8], F32)
    sq2_scr = csb("sq2_scr", [128, 1024], BF16)
    r2_scr = nc.dram_tensor("r2_scr", [NTOK, 1024], BF16, kind=scratch_kind).ap()
    ps_f2 = [p3c.enter_context(nc.psum_tensor(f"ps_f2{i}", [128, 1024], F32))
             for i in range(2)]
    ps_t2 = [p3c.enter_context(nc.psum_tensor(f"ps_t2{i}", [128, 512], F32))
             for i in range(2)]

    p3b_done = [g_stores_done, s2_ready, w(sT), w(sA), w(sP)]
    prog.dma("sync", b2_sb[:, :], b2_bc, waits=p3b_done, sem=sLd)
    b2_ready = w(sLd)

    w2st_free = [None, None]
    w2rnd_free = None
    x2rnd_free = [None]
    w2q_free = [None, None]
    oacc_free = [None] * NTILE
    gin_free = [None, None]
    x2qtm_free = [None, None]
    x2qfm_free = [None, None]
    t2_free = [None, None]
    f2_free = [None, None]
    rres_free = [None, None]
    xrec_free = None
    tmp2_free = None
    r2h0_free = None
    r2h1_free = None
    outst_free = [None, None]
    sq2_free = None
    colsc_free = None

    def stream_w2_block(h2, b):
        nonlocal w2rnd_free
        wb = (h2 * NB + b) % 2
        for c in range(8):
            cs = c % 2
            row0 = 1024 * b + 128 * c
            prog.dma("sync", w2st[cs][:, :],
                     w2T[row0:row0 + 128, 1024 * h2:1024 * (h2 + 1)],
                     waits=[w2st_free[cs], w2q_free[wb] if c == 0 else None]
                     + (p3b_done if h2 + b == 0 and c == 0 else []), sem=sLd)
            prog.act(rnd_scr[:, 0:1024], w2st[cs][:, :], AF.Identity,
                     bias=cst["magic_col"][:, :], scale=sw2_col[:, :],
                     waits=[w(sLd), sc2_ready, w2rnd_free], inc=(sA, 1))
            w2st_free[cs] = w(sA)
            prog.ts(w2q_sb[wb][:, 1024 * c:1024 * (c + 1)], rnd_scr[:, 0:1024],
                    -MAGIC, None, ALU.add, waits=[w(sA)], inc=(sV, 1))
            w2rnd_free = w(sV)
        return w(sV)

    w2_ready = {}
    w2_ready[(0, 0)] = stream_w2_block(0, 0)
    seq = [(h2, b) for h2 in range(2) for b in range(NB)]
    units = [(si, t) for si in range(len(seq)) for t in range(NTILE)]
    prep_done = {}

    def prep_x2q(si, t):
        h2_, b_ = seq[si]
        tb_ = t % 2
        prog.dma("sync", g_in[tb_][:, :],
                 g_scr[128 * t:128 * (t + 1), 1024 * b_:1024 * (b_ + 1)],
                 waits=[gin_free[tb_], g_stores_done], sem=sY)
        prog.act(rnd_scr[:, 1024:2048], g_in[tb_][:, :], AF.Identity,
                 bias=cst["magic_col"][:, :], scale=s2_cols[:, t:t + 1],
                 waits=[w(sY), s2_ready, x2rnd_free[0]], inc=(sA, 1))
        gin_free[tb_] = w(sA)
        prog.ts(x2q_tm[tb_][:, :], rnd_scr[:, 1024:2048], -MAGIC, None, ALU.add,
                waits=[w(sA), x2qtm_free[tb_]], inc=(sV, 1))
        x2rnd_free[0] = w(sV)
        x2q_ready = w(sV)
        for c in range(8):
            prog.mm(x2q_tm[tb_][:, 128 * c:128 * (c + 1)], cst["ident_bf"][:, :],
                    ps_t2[(c // 4) % 2][:, 128 * (c % 4):128 * (c % 4 + 1)],
                    True, True,
                    waits=[x2q_ready, t2_free[(c // 4) % 2]] if c % 4 == 0 else (),
                    inc=(sT, 1) if c % 4 == 3 else None)
            if c % 4 == 3:
                prog.act(x2q_fm[tb_][:, 512 * (c // 4):512 * (c // 4 + 1)],
                         ps_t2[(c // 4) % 2][:, :], AF.Identity,
                         waits=[w(sT), x2qfm_free[tb_] if c == 3 else None],
                         inc=(sA, 1))
                t2_free[(c // 4) % 2] = w(sA)
        x2qtm_free[tb_] = w(sT)
        prep_done[(si, t)] = w(sA)

    prep_x2q(0, 0)
    for ui, (si, t) in enumerate(units):
        h2, b = seq[si]
        wb = (h2 * NB + b) % 2
        tb = t % 2
        if t == 0 and si + 1 < len(seq):
            w2_ready[seq[si + 1]] = stream_w2_block(*seq[si + 1])
        if ui + 1 < len(units):
            prep_x2q(*units[ui + 1])
        ps = ps_f2[tb]
        mmw = [prep_done[(si, t)], w2_ready[(h2, b)], f2_free[tb]]
        for c in range(8):
            for o2 in range(2):
                prog.mm(x2q_fm[tb][:, 128 * c:128 * (c + 1)],
                        w2q_sb[wb][:, 1024 * c + 512 * o2:1024 * c + 512 * (o2 + 1)],
                        ps[:, 512 * o2:512 * (o2 + 1)],
                        c == 0, c == 7,
                        waits=mmw if (c == 0 and o2 == 0) else (),
                        inc=(sT, 1) if (c == 7 and o2 == 1) else None)
        mm_done = w(sT)
        x2qfm_free[tb] = mm_done
        oreg = oacc[:, 1024 * t:1024 * (t + 1)]
        if b == 0:
            prog.copy(oreg, ps[:, :], waits=[mm_done, oacc_free[t]], inc=(sV, 1))
        else:
            prog.tt(ALU.add, oreg, oreg, ps[:, :], waits=[mm_done], inc=(sV, 1))
        f2_free[tb] = w(sV)
        if t == NTILE - 1:
            w2q_free[wb] = w(sT)
        if b < NB - 1:
            continue
        acc_done = w(sV)
        prog.dma("sync", rres[tb][:, :],
                 x1_scr[128 * t:128 * (t + 1), 1024 * h2:1024 * (h2 + 1)],
                 waits=[rres_free[tb], r_spills_done], sem=sY)
        prog.act(xrec[:, :], rres[tb][:, :], AF.Identity,
                 bias=ln_mrs[:, t:t + 1], scale=ln_rs[:, t:t + 1],
                 waits=[w(sY), xrec_free], inc=(sA, 1))
        prog.tt(ALU.add, tmp2[:, :], xrec[:, :], b2_sb[:, 1024 * h2:1024 * (h2 + 1)],
                waits=[w(sA), b2_ready, tmp2_free], inc=(sP, 1), eng="gpsimd")
        xrec_free = w(sP)
        rres_free[tb] = w(sP)
        if h2 == 0:
            prog.stt(r2h0[:, :], oreg, dq2_cols[:, t:t + 1], tmp2[:, :],
                     ALU.mult, ALU.add, accum=ln2_sum[:, 2 * t:2 * t + 1],
                     waits=[acc_done, w(sP), r2h0_free], inc=(sV, 1))
            tmp2_free = w(sV)
            oacc_free[t] = w(sV)
            r2_ready = w(sV)
            prog.act(sq2_scr[:, :], r2h0[:, :], AF.Square,
                     accum=ln2_sq[:, 2 * t:2 * t + 1],
                     waits=[r2_ready, sq2_free], inc=(sA, 1))
            sq2_free = w(sA)
            prog.dma("sync", r2_scr[128 * t:128 * (t + 1), :], r2h0[:, :],
                     waits=[r2_ready], sem=sSt)
            r2h0_free = w(sSt)
        else:
            prog.stt(r2h1[:, :], oreg, dq2_cols[:, t:t + 1], tmp2[:, :],
                     ALU.mult, ALU.add, accum=ln2_sum[:, 2 * t + 1:2 * t + 2],
                     waits=[acc_done, w(sP), r2h1_free], inc=(sV, 1))
            tmp2_free = w(sV)
            r2_ready = w(sV)
            prog.act(sq2_scr[:, :], r2h1[:, :], AF.Square,
                     accum=ln2_sq[:, 2 * t + 1:2 * t + 2],
                     waits=[r2_ready, sq2_free], inc=(sA, 1))
            sq2_stat = w(sA)
            sq2_free = sq2_stat
            prog.tt(ALU.add, cols_c[:, 0:1], ln2_sum[:, 2 * t:2 * t + 1],
                    ln2_sum[:, 2 * t + 1:2 * t + 2],
                    waits=[r2_ready, colsc_free], inc=(sV, 1))
            prog.ts(cols_c[:, 0:1], cols_c[:, 0:1], 1.0 / D, None, ALU.mult,
                    waits=[w(sV)], inc=(sV, 1))
            prog.tt(ALU.add, cols_c[:, 1:2], ln2_sq[:, 2 * t:2 * t + 1],
                    ln2_sq[:, 2 * t + 1:2 * t + 2], waits=[sq2_stat],
                    inc=(sV, 1))
            prog.ts(cols_c[:, 1:2], cols_c[:, 1:2], 1.0 / D, None, ALU.mult,
                    waits=[w(sV)], inc=(sV, 1))
            prog.act(cols_c[:, 2:3], cols_c[:, 0:1], AF.Square,
                     waits=[w(sV)], inc=(sA, 1))
            prog.tt(ALU.subtract, cols_c[:, 3:4], cols_c[:, 1:2], cols_c[:, 2:3],
                    waits=[w(sA), w(sV)], inc=(sV, 1))
            prog.ts(cols_c[:, 3:4], cols_c[:, 3:4], EPS, None, ALU.add,
                    waits=[w(sV)], inc=(sV, 1))
            prog.recip(cols_c[:, 4:5], cols_c[:, 3:4], waits=[w(sV)], inc=(sV, 1))
            prog.act(cols_c[:, 4:5], cols_c[:, 4:5], AF.Sqrt, waits=[w(sV)],
                     inc=(sA, 1))
            prog.tt(ALU.mult, cols_c[:, 5:6], cols_c[:, 0:1], cols_c[:, 4:5],
                    waits=[w(sA)], inc=(sV, 1))
            prog.ts(cols_c[:, 5:6], cols_c[:, 5:6], -1.0, None, ALU.mult,
                    waits=[w(sV)], inc=(sV, 1))
            ln2_ready = w(sV)
            prog.dma("sync", r2h0[:, :], r2_scr[128 * t:128 * (t + 1), :],
                     waits=[r2h0_free], sem=sY)
            prog.act(outst[0][:, :], r2h0[:, :], AF.Identity,
                     bias=cols_c[:, 5:6], scale=cols_c[:, 4:5],
                     waits=[w(sY), ln2_ready, outst_free[0]], inc=(sA, 1))
            r2h0_free = w(sA)
            prog.dma("sync", out[128 * t:128 * (t + 1), 0:1024], outst[0][:, :],
                     waits=[w(sA)], sem=sSt)
            outst_free[0] = w(sSt)
            prog.act(outst[1][:, :], r2h1[:, :], AF.Identity,
                     bias=cols_c[:, 5:6], scale=cols_c[:, 4:5],
                     waits=[ln2_ready, outst_free[1]], inc=(sA, 1))
            r2h1_free = w(sA)
            colsc_free = w(sA)
            prog.dma("sync", out[128 * t:128 * (t + 1), 1024:2048], outst[1][:, :],
                     waits=[w(sA)], sem=sSt)
            outst_free[1] = w(sSt)

    prog.barrier(sB)
    p3c.close()
    return finish([w(sSt), w(sA), w(sV), w(sT), w(sCast), w(sScan), w(sY),
                   w(sLd), w(sP)])


def host_prepare(inputs):
    x = np.asarray(inputs["L0_attn_x"], dtype=np.float32)
    w_attn = np.asarray(inputs["L0_attn_w_attn"], dtype=np.float32)
    b_attn = np.asarray(inputs["L0_attn_b_attn"], dtype=np.float32)
    w_proj = np.asarray(inputs["L0_attn_w_proj"], dtype=np.float32)
    b_proj = np.asarray(inputs["L0_attn_b_proj"], dtype=np.float32)
    W1 = np.asarray(inputs["L0_ffn_W1"], dtype=np.float32)
    B1 = np.asarray(inputs["L0_ffn_B1"], dtype=np.float32)
    W2 = np.asarray(inputs["L0_ffn_W2"], dtype=np.float32)
    B2 = np.asarray(inputs["L0_ffn_B2"], dtype=np.float32)

    shared = dict(
        wqT=np.ascontiguousarray(w_attn[0:D].T),
        wkT=np.ascontiguousarray(w_attn[D:2 * D].T),
        wvT=np.ascontiguousarray(w_attn[2 * D:3 * D].T),
        bq_col=np.ascontiguousarray(b_attn[0:D].reshape(H, 128).T),
        bk_col=np.ascontiguousarray(b_attn[D:2 * D].reshape(H, 128).T),
        bv_bc=np.ascontiguousarray(np.broadcast_to(b_attn[2 * D:3 * D], (128, D))),
        wprojT=np.ascontiguousarray(w_proj.T),
        bproj_bc=np.ascontiguousarray(np.broadcast_to(b_proj, (128, D))),
        w1T=np.ascontiguousarray(W1.T),
        b1_bc=np.ascontiguousarray(np.broadcast_to(B1, (128, DFF))),
        w2T=np.ascontiguousarray(W2.T),
        b2_bc=np.ascontiguousarray(np.broadcast_to(B2, (128, D))),
        ident_bf=np.eye(128, dtype=BFNP),
        ident_f=np.eye(128, dtype=np.float32),
        ones_col_bf=np.ones((128, 1), BFNP),
        ones_row_f=np.ones((1, 128), np.float32),
        ones_row_bf=np.ones((1, 128), BFNP),
    )

    tri = np.triu(np.ones((128, 128), np.float32))
    zeros = np.zeros((128, 128), np.float32)
    ones = np.ones((128, 128), np.float32)
    mask_p = {}
    for p in range(2):
        tiles = [(tri if cp % 2 == p else (zeros if p == 0 else ones))
                 for cp in range(8)]
        mask_p[p] = np.concatenate(tiles, axis=0).astype(BFNP)

    in_maps = []
    own_idx = {}
    for core in range(N_CORES):
        b, p = core // 2, core % 2
        blocks = np.arange(p, 32, 2)
        tok_idx = (blocks[:, None] * 128 + np.arange(128)[None, :]).reshape(-1)
        own_idx[core] = tok_idx
        xb = x[b]
        x_own = xb[tok_idx]
        m = dict(shared)
        m["x_fm"] = np.ascontiguousarray(xb.T)
        m["x_own_fm"] = np.ascontiguousarray(x_own.T)
        m["x_own_tm"] = np.ascontiguousarray(x_own)
        m["masks"] = mask_p[p]
        in_maps.append(m)
    return in_maps, own_idx


_PROG_CACHE = {}


def kernel(**inputs):
    in_maps, own_idx = host_prepare(inputs)
    if "full" not in _PROG_CACHE:
        _PROG_CACHE["full"] = build_program(upto=99, debug=False)
    nc = _PROG_CACHE["full"]
    res = run_bass_kernel_spmd(nc, in_maps, core_ids=list(range(N_CORES)))
    out = np.zeros((4, T, D), dtype=np.float32)
    for core in range(N_CORES):
        b = core // 2
        out[b, own_idx[core]] = res.results[core]["out"]
    return out


# version 06
# speedup vs baseline: 1.3432x; 1.0094x over previous
import numpy as np
import ml_dtypes
from contextlib import ExitStack

import concourse.bass as bass
import concourse.mybir as mybir
from concourse.bass_utils import run_bass_kernel_spmd

F32 = mybir.dt.float32
BF16 = mybir.dt.bfloat16
AF = mybir.ActivationFunctionType
ALU = mybir.AluOpType
AX = mybir.AxisListType.X
BFNP = ml_dtypes.bfloat16

D = 2048
T = 4096
H = 16
HD = 128
NTOK = 2048
NTILE = 16
NJS = 4
DFF = 8192
MAGIC = 12582912.0
INV_SQRT_HD = float(1.0 / np.sqrt(128.0))
Q8 = 127.0
INV_Q8SQ = float(1.0 / (127.0 * 127.0))
EPS = 1e-5
N_CORES = 8

I0 = [0, 0, 1, 1, 2, 2, 3, 3]


class Sem:
    __slots__ = ("h", "n")

    def __init__(self, handle):
        self.h = handle
        self.n = 0


class Prog:

    ENGINES = ("sync", "scalar", "vector", "tensor", "gpsimd")

    def __init__(self, nc):
        self.nc = nc
        self.ops = {e: [] for e in self.ENGINES}
        self.waited = {e: {} for e in self.ENGINES}

    def emit(self, eng, fn, waits=(), inc=None):
        wl = []
        for item in waits:
            if item is None:
                continue
            s, v = item
            assert v <= s.n, f"wait on future sem value {v} > {s.n}"
            key = id(s)
            if self.waited[eng].get(key, -1) >= v:
                continue
            self.waited[eng][key] = v
            wl.append((s.h, v))
        post = None
        if inc is not None:
            s, amt = inc
            s.n += amt
            post = (s.h, amt)
        wl = tuple(wl)

        def f(e, _fn=fn, _wl=wl, _post=post):
            for h, v in _wl:
                e.wait_ge(h, v)
            ins = _fn(e)
            if _post is not None:
                ins.then_inc(_post[0], _post[1])
        self.ops[eng].append(f)

    def dma(self, q, dst, src, waits=(), sem=None, slow_ok=False):
        self.emit(q,
                  lambda e: e.dma_start(dst, src, allow_slow_non_contiguous=slow_ok),
                  waits, (sem, 16) if sem is not None else None)

    def mm(self, lhsT, rhs, out, start, stop, waits=(), inc=None):
        self.emit("tensor",
                  lambda e: e.matmul(out, lhsT, rhs, start=start, stop=stop),
                  waits, inc)

    def act(self, out, in_, func, bias=0.0, scale=1.0, accum=None, waits=(), inc=None):
        self.emit("scalar",
                  lambda e: e.activation(out, in_, func, bias=bias, scale=scale,
                                         accum_out=accum),
                  waits, inc)

    def tt(self, op, out, a, b, waits=(), inc=None, eng="vector"):
        self.emit(eng, lambda e: e.tensor_tensor(out, a, b, op), waits, inc)

    def ts(self, out, in0, s1, s2, op0, op1=ALU.bypass, accum=None, waits=(), inc=None,
           eng="vector"):
        self.emit(eng,
                  lambda e: e.tensor_scalar(out, in0, s1, s2, op0, op1,
                                            accum_out=accum),
                  waits, inc)

    def stt(self, out, in0, scalar, in1, op0, op1, accum=None, waits=(), inc=None):
        self.emit("vector",
                  lambda e: e.scalar_tensor_tensor(out, in0, scalar, in1, op0, op1,
                                                   accum_out=accum),
                  waits, inc)

    def red(self, out, in_, op, absval=False, waits=(), inc=None):
        self.emit("vector",
                  lambda e: e.tensor_reduce(out, in_, AX, op,
                                            apply_absolute_value=absval),
                  waits, inc)

    def copy(self, out, in_, waits=(), inc=None, eng="vector"):
        self.emit(eng, lambda e: e.tensor_copy(out, in_), waits, inc)

    def recip(self, out, in_, waits=(), inc=None):
        self.emit("vector", lambda e: e.reciprocal(out, in_), waits, inc)

    def memset(self, ap, val, waits=(), inc=None, eng="vector"):
        self.emit(eng, lambda e: e.memset(ap, val), waits, inc)

    def barrier(self, bsem):
        base = bsem.n
        for eng in self.ENGINES:
            self.emit(eng, lambda e: e.nop(), (), (bsem, 1))
        for eng in self.ENGINES:
            self.emit(eng, lambda e: e.nop(), ((bsem, base + 5),), None)

    def replay(self):
        nc = self.nc
        with nc.Block() as block:
            @block.sync
            def _(e):
                for f in self.ops["sync"]:
                    f(e)

            @block.scalar
            def _(e):
                for f in self.ops["scalar"]:
                    f(e)

            @block.vector
            def _(e):
                for f in self.ops["vector"]:
                    f(e)

            @block.tensor
            def _(e):
                for f in self.ops["tensor"]:
                    f(e)

            @block.gpsimd
            def _(e):
                for f in self.ops["gpsimd"]:
                    f(e)


def w(sem):
    return (sem, sem.n)


class Scanner:

    def __init__(self, prog, slots, parts, sems):
        self.prog = prog
        self.slots = slots
        self.parts = parts
        self.sScan, self.sV = sems
        self.work = []
        self.i = 0
        self.slot_free = [None] * len(self.slots)

    def step(self):
        if self.i >= len(self.work):
            return
        src_ap, col = self.work[self.i]
        slot = self.i % len(self.slots)
        p = self.prog
        waits = [self.slot_free[slot]] if self.slot_free[slot] else []
        p.dma("sync", self.slots[slot][:, :], src_ap, waits=waits, sem=self.sScan)
        p.red(self.parts[:, col:col + 1], self.slots[slot][:, :], ALU.max,
              absval=True, waits=[w(self.sScan)], inc=(self.sV, 1))
        self.slot_free[slot] = w(self.sV)
        self.i += 1

    def drain(self):
        while self.i < len(self.work):
            self.step()
        return w(self.sV)


def build_program(upto=99, debug=False):
    nc = bass.Bass("TRN2", target_bir_lowering=False)
    scratch_kind = "ExternalOutput" if debug else "Internal"

    x_fm = nc.dram_tensor("x_fm", [D, T], F32, kind="ExternalInput").ap()
    x_own_fm = nc.dram_tensor("x_own_fm", [D, NTOK], F32, kind="ExternalInput").ap()
    x_own_tm = nc.dram_tensor("x_own_tm", [NTOK, D], F32, kind="ExternalInput").ap()
    wqT = nc.dram_tensor("wqT", [D, D], F32, kind="ExternalInput").ap()
    wkT = nc.dram_tensor("wkT", [D, D], F32, kind="ExternalInput").ap()
    wvT = nc.dram_tensor("wvT", [D, D], F32, kind="ExternalInput").ap()
    bq_col = nc.dram_tensor("bq_col", [128, H], F32, kind="ExternalInput").ap()
    bk_col = nc.dram_tensor("bk_col", [128, H], F32, kind="ExternalInput").ap()
    bv_bc = nc.dram_tensor("bv_bc", [128, D], F32, kind="ExternalInput").ap()
    wprojT = nc.dram_tensor("wprojT", [D, D], F32, kind="ExternalInput").ap()
    bproj_bc = nc.dram_tensor("bproj_bc", [128, D], F32, kind="ExternalInput").ap()
    w1T = nc.dram_tensor("w1T", [D, DFF], F32, kind="ExternalInput").ap()
    b1_bc = nc.dram_tensor("b1_bc", [128, DFF], F32, kind="ExternalInput").ap()
    w2T = nc.dram_tensor("w2T", [DFF, D], F32, kind="ExternalInput").ap()
    b2_bc = nc.dram_tensor("b2_bc", [128, D], F32, kind="ExternalInput").ap()
    masks = nc.dram_tensor("masks", [8 * 128, 128], BF16, kind="ExternalInput").ap()
    ident_bf = nc.dram_tensor("ident_bf", [128, 128], BF16, kind="ExternalInput").ap()
    ident_f = nc.dram_tensor("ident_f", [128, 128], F32, kind="ExternalInput").ap()
    ones_col_bf = nc.dram_tensor("ones_col_bf", [128, 1], BF16, kind="ExternalInput").ap()
    ones_row_f = nc.dram_tensor("ones_row_f", [1, 128], F32, kind="ExternalInput").ap()
    ones_row_bf = nc.dram_tensor("ones_row_bf", [1, 128], BF16, kind="ExternalInput").ap()

    out = nc.dram_tensor("out", [NTOK, D], F32, kind="ExternalOutput").ap()

    q_scr = nc.dram_tensor("q_scr", [H * 128, NTOK], BF16, kind=scratch_kind).ap()
    k_scr = nc.dram_tensor("k_scr", [H * 128, T], BF16, kind=scratch_kind).ap()
    v_scr = nc.dram_tensor("v_scr", [H * T, HD], BF16, kind=scratch_kind).ap()
    y_scr = nc.dram_tensor("y_scr", [NTOK, D], BF16, kind=scratch_kind).ap()
    x1_scr = nc.dram_tensor("x1_scr", [NTOK, D], BF16, kind=scratch_kind).ap()
    g_scr = nc.dram_tensor("g_scr", [NTOK, DFF], BF16, kind=scratch_kind).ap()
    if debug:
        dbg_sc = nc.dram_tensor("dbg_sc", [128, 16], F32, kind="ExternalOutput").ap()
        dbg_cols = nc.dram_tensor("dbg_cols", [128, 96], F32, kind="ExternalOutput").ap()
        dbg_x1q = nc.dram_tensor("dbg_x1q", [128, H * D], BF16,
                                 kind="ExternalOutput").ap()

    prog = Prog(nc)
    ctx = ExitStack()
    sb = lambda name, shape, dt: ctx.enter_context(nc.sbuf_tensor(name, shape, dt))
    mksem = lambda name: Sem(ctx.enter_context(nc.semaphore(name)))

    cst = {
        "ident_bf": sb("c_ident_bf", [128, 128], BF16),
        "ident_f": sb("c_ident_f", [128, 128], F32),
        "ones_col_bf": sb("c_ones_col_bf", [128, 1], BF16),
        "ones_row_f": sb("c_ones_row_f", [1, 128], F32),
        "ones_row_bf": sb("c_ones_row_bf", [1, 128], BF16),
        "masks": sb("c_masks", [128, 8 * 128], BF16),
        "magic_col": sb("c_magic_col", [128, 1], F32),
    }
    sw1_col = sb("sw1_col", [128, 1], F32)
    gq1_col = sb("gq1_col", [128, 1], F32)
    sw2_col = sb("sw2_col", [128, 1], F32)
    gq2_col = sb("gq2_col", [128, 1], F32)
    amax1_cols = sb("amax1_cols", [128, NTILE], F32)
    dq1_cols = sb("dq1_cols", [128, NTILE], F32)
    s1_cols = sb("s1_cols", [128, NTILE], F32)
    amax2_parts = sb("amax2_parts", [128, NTILE * 16], F32)
    dq2_cols = sb("dq2_cols", [128, NTILE], F32)
    s2_cols = sb("s2_cols", [128, NTILE], F32)
    negc_cols = sb("negc_cols", [128, H], F32)

    p_small = ctx.enter_context(nc.psum_tensor("p_small", [128, 512], F32))

    sLd = mksem("sLd")
    sCast = mksem("sCast")
    sSt = mksem("sSt")
    sT = mksem("sT")
    sV = mksem("sV")
    sA = mksem("sA")
    sP = mksem("sP")
    sB = mksem("sB")
    sScan = mksem("sScan")
    sY = mksem("sY")

    prog.dma("sync", cst["ident_bf"][:, :], ident_bf, sem=sLd)
    prog.dma("sync", cst["ident_f"][:, :], ident_f, sem=sLd)
    prog.dma("sync", cst["ones_col_bf"][:, :], ones_col_bf, sem=sLd)
    prog.dma("sync", cst["ones_row_f"][:, :], ones_row_f, sem=sLd)
    prog.dma("sync", cst["ones_row_bf"][:, :], ones_row_bf, sem=sLd)
    prog.dma("sync", cst["masks"][:, :].rearrange("p (c n) -> p c n", n=128),
             masks.rearrange("(c p) n -> p c n", p=128), sem=sLd)
    prog.memset(cst["magic_col"][:, :], MAGIC, inc=(sV, 1))
    const_ld = w(sLd)

    p1 = ExitStack()
    p1sb = lambda name, shape, dt: p1.enter_context(nc.sbuf_tensor(name, shape, dt))
    wgt_sb = p1sb("wgt_sb", [128, H * D], BF16)
    xg = [p1sb(f"xg{i}", [128, 16 * 1024], BF16) for i in range(2)]
    vst = [p1sb(f"vst{i}", [128, 1024], BF16) for i in range(4)]
    kst = [p1sb(f"kst{i}", [128, 512], BF16) for i in range(4)]
    bias_sb = p1sb("bias_sb", [128, D + 2 * H], F32)
    scan_slots = [p1sb(f"scan{i}", [128, 2048], F32) for i in range(4)]
    wparts = p1sb("wparts", [128, 128], F32)
    pv = [p1.enter_context(nc.psum_tensor(f"pv{i}", [128, 1024], F32)) for i in range(2)]
    pk = [p1.enter_context(nc.psum_tensor(f"pk{i}", [128, 512], F32)) for i in range(2)]

    prog.dma("sync", bias_sb[:, 0:D], bv_bc, sem=sLd)
    prog.dma("sync", bias_sb[:, D:D + H], bq_col, sem=sLd)
    prog.dma("sync", bias_sb[:, D + H:D + 2 * H], bk_col, sem=sLd)
    bias_ld = w(sLd)

    scanner = Scanner(prog, scan_slots, wparts, (sScan, sV))
    for t in range(64):
        r, c = t % 16, t // 16
        scanner.work.append(
            (w1T[128 * r:128 * (r + 1), 2048 * c:2048 * (c + 1)], t))
    for t in range(64):
        scanner.work.append((w2T[128 * t:128 * (t + 1), :], 64 + t))

    def load_weight_bf16(dst_sb, srcT, waits):
        for c in range(16):
            prog.dma("gpsimd", dst_sb[:, D * c:D * (c + 1)],
                     srcT[128 * c:128 * (c + 1), :], waits=waits, sem=sCast)
        return w(sCast)

    def load_x_group(slot, src, col0, waits):
        for c in range(16):
            prog.dma("gpsimd", xg[slot][:, 1024 * c:1024 * (c + 1)],
                     src[128 * c:128 * (c + 1), col0:col0 + 1024], waits=waits,
                     sem=sCast)
        return w(sCast)

    wv_ready = load_weight_bf16(wgt_sb, wvT, [])
    xg_free = [None, None]
    vst_free = [None] * 4
    pv_free = [None, None]
    grp_loaded = [load_x_group(0, x_fm, 0, []), load_x_group(1, x_fm, 1024, [])]
    for grp in range(4):
        for blk in range(8):
            b2 = grp * 8 + blk
            for oh in range(2):
                i2 = b2 * 2 + oh
                ps = pv[i2 % 2]
                mmw = [grp_loaded[grp], wv_ready, pv_free[i2 % 2]]
                for c in range(16):
                    for o2 in range(2):
                        last = (c == 15 and o2 == 1)
                        prog.mm(
                            xg[grp % 2][:, 1024 * c + 128 * blk:1024 * c + 128 * (blk + 1)],
                            wgt_sb[:, D * c + 1024 * oh + 512 * o2:
                                   D * c + 1024 * oh + 512 * (o2 + 1)],
                            ps[:, 512 * o2:512 * (o2 + 1)],
                            c == 0, c == 15,
                            waits=mmw if (c == 0 and o2 == 0) else (),
                            inc=(sT, 1) if last else None)
                mm_done = w(sT)
                vslot = i2 % 4
                prog.tt(ALU.add, vst[vslot][:, :], ps[:, :],
                        bias_sb[:, 1024 * oh:1024 * (oh + 1)],
                        waits=[mm_done, bias_ld, vst_free[vslot]], inc=(sV, 1))
                pv_free[i2 % 2] = w(sV)
                ev_done = w(sV)
                src3 = vst[vslot][:, :].rearrange("p (h j) -> p h j", j=128)
                dst3 = v_scr.rearrange("(h t) j -> t h j", h=H)
                for hh in range(2):
                    h0 = oh * 8 + hh * 4
                    prog.dma("sync",
                             dst3[128 * b2:128 * (b2 + 1), h0:h0 + 4, :],
                             src3[:, 4 * hh:4 * (hh + 1), :],
                             waits=[ev_done], sem=sSt)
                vst_free[vslot] = w(sSt)
                scanner.step()
        if grp + 2 < 4:
            xg_free[grp % 2] = w(sT)
            grp_loaded.append(load_x_group(grp % 2, x_fm, 1024 * (grp + 2),
                                           [xg_free[grp % 2]]))
    v_tensor_done = w(sT)
    v_stores_done = w(sSt)

    def emit_kq(srcT, src_x, ngrp, bias_off, dst_scr, prev_done,
                ps_init=None, st_init=None):
        wrdy = load_weight_bf16(wgt_sb, srcT, [prev_done])
        st_free = [st_init] * 4
        ps_free = [ps_init, ps_init]
        loaded = [load_x_group(0, src_x, 0, [prev_done]),
                  load_x_group(1, src_x, 1024, [prev_done])]
        cnt = 0
        for grp in range(ngrp):
            for h in range(H):
                for half in range(2):
                    ps = pk[cnt % 2]
                    mmw = [loaded[grp], wrdy, ps_free[cnt % 2]]
                    for c in range(16):
                        prog.mm(
                            wgt_sb[:, D * c + 128 * h:D * c + 128 * (h + 1)],
                            xg[grp % 2][:, 1024 * c + 512 * half:1024 * c + 512 * (half + 1)],
                            ps[:, :], c == 0, c == 15,
                            waits=mmw if c == 0 else (),
                            inc=(sT, 1) if c == 15 else None)
                    mm_done = w(sT)
                    slot = cnt % 4
                    prog.act(kst[slot][:, :], ps[:, :], AF.Identity,
                             bias=bias_sb[:, bias_off + h:bias_off + h + 1],
                             waits=[mm_done, bias_ld, st_free[slot]], inc=(sA, 1))
                    ps_free[cnt % 2] = w(sA)
                    ev_done = w(sA)
                    prog.dma("sync",
                             dst_scr[128 * h:128 * (h + 1),
                                     1024 * grp + 512 * half:1024 * grp + 512 * (half + 1)],
                             kst[slot][:, :], waits=[ev_done], sem=sSt)
                    st_free[slot] = w(sSt)
                    cnt += 1
                    scanner.step()
            if grp + 2 < ngrp:
                gfree = w(sT)
                loaded.append(load_x_group(grp % 2, src_x, 1024 * (grp + 2), [gfree]))
        return w(sT), w(sSt)

    k_tensor_done, k_stores_done = emit_kq(wkT, x_fm, 4, D + H, k_scr, v_tensor_done)
    k_evac_done = w(sA)
    q_tensor_done, q_stores_done = emit_kq(wqT, x_own_fm, 2, D, q_scr, k_tensor_done,
                                           ps_init=k_evac_done, st_init=k_stores_done)

    scan_red_done = scanner.drain()

    def finish_scan(tag, part_off, s_col, gq_col):
        col = p1sb(f"{tag}_col", [128, 1], F32)
        g1 = p1sb(f"{tag}_g1", [1, 1], F32)
        g2 = p1sb(f"{tag}_g2", [1, 2], F32)
        prog.red(col[:, :], wparts[:, part_off:part_off + 64], ALU.max,
                 waits=[scan_red_done], inc=(sV, 1))
        vdone = w(sV)
        prog.mm(col[:, :], cst["ident_f"][:, :], p_small[0:1, 0:128], True, True,
                waits=[const_ld, vdone], inc=(sT, 1))
        prog.red(g1[:, :], p_small[0:1, 0:128], ALU.max, waits=[w(sT)], inc=(sV, 1))
        prog.recip(g2[:, 0:1], g1[:, :], waits=[w(sV)], inc=(sV, 1))
        prog.ts(g2[:, 0:1], g2[:, 0:1], Q8, None, ALU.mult, waits=[w(sV)], inc=(sV, 1))
        prog.ts(g2[:, 1:2], g1[:, :], INV_Q8SQ, None, ALU.mult, waits=[w(sV)],
                inc=(sV, 1))
        prog.mm(cst["ones_row_f"][:, :], g2[:, :], p_small[:, 130:132], True, True,
                waits=[w(sV)], inc=(sT, 1))
        tdone = w(sT)
        prog.copy(s_col[:, :], p_small[:, 130:131], waits=[tdone], inc=(sV, 1))
        prog.copy(gq_col[:, :], p_small[:, 131:132], waits=[tdone], inc=(sV, 1))
        return w(sV)

    sc1_ready = finish_scan("w1s", 0, sw1_col, gq1_col)
    sc2_ready = finish_scan("w2s", 64, sw2_col, gq2_col)

    if debug:
        prog.dma("gpsimd", dbg_sc[:, 0:1], sw1_col[:, :], waits=[sc1_ready], sem=sSt, slow_ok=True)
        prog.dma("gpsimd", dbg_sc[:, 1:2], gq1_col[:, :], waits=[sc1_ready], sem=sSt, slow_ok=True)
        prog.dma("gpsimd", dbg_sc[:, 2:3], sw2_col[:, :], waits=[sc2_ready], sem=sSt, slow_ok=True)
        prog.dma("gpsimd", dbg_sc[:, 3:4], gq2_col[:, :], waits=[sc2_ready], sem=sSt, slow_ok=True)

    def finish(last_waits):
        prog.emit("gpsimd", lambda e: e.nop(), tuple(last_waits), None)
        prog.barrier(sB)
        prog.replay()
        ctx.close()
        return nc

    prog.barrier(sB)
    p1.close()

    if upto <= 1:
        return finish([w(sSt), w(sA), w(sV), w(sT), w(sCast), w(sScan)])

    p2 = ExitStack()
    p2sb = lambda name, shape, dt: p2.enter_context(nc.sbuf_tensor(name, shape, dt))
    k_sb = [p2sb(f"k_sb{i}", [128, T], BF16) for i in range(2)]
    v_sb = [p2sb(f"v_sb{i}", [128, T], BF16) for i in range(2)]
    qh_sb = [p2sb(f"qh_sb{i}", [128, NTOK], BF16) for i in range(2)]
    sq_scr = p2sb("sq_scr", [128, T], BF16)
    ring1 = p2sb("ring1", [128, 8 * 512], BF16)
    ring2 = p2sb("ring2", [128, 8 * 512], BF16)
    linv_sb = p2sb("linv_sb", [1, 512], F32)
    linv_bc = p2sb("linv_bc", [128, 512], F32)
    lnl8 = p2sb("lnl8", [1, 512], F32)
    lnl2 = p2sb("lnl2", [1, 512], F32)
    negmh = [p2sb(f"negmh{i}", [1, 512], BF16) for i in range(2)]
    cstar = p2sb("cstar", [1, 8], F32)
    nrm_col = p2sb("nrm_col", [128, 1], F32)
    negc8_cols = p2sb("negc8_cols", [128, H], F32)
    negc_sc = p2sb("negc_sc", [1, H], F32)
    yst = [p2sb(f"yst{i}", [128, 512], BF16) for i in range(2)]
    ps_s = [p2.enter_context(nc.psum_tensor(f"ps_s{i}", [128, 512], F32))
            for i in range(4)]
    ps_y = p2.enter_context(nc.psum_tensor("ps_y", [128, 512], F32))
    ps_l = p2.enter_context(nc.psum_tensor("ps_l", [128, 512], F32))
    ps_misc = p2.enter_context(nc.psum_tensor("ps_misc", [128, 512], F32))

    p1_done = [q_tensor_done, q_stores_done, k_stores_done, v_stores_done,
               sc1_ready, sc2_ready]

    def load_head(h, extra):
        hb = h % 2
        waits = list(extra)
        prog.dma("sync", k_sb[hb][:, :], k_scr[128 * h:128 * (h + 1), :],
                 waits=waits, sem=sY)
        vsrc = v_scr[h * T:(h + 1) * T, :].rearrange("(c p) j -> p c j", p=128)
        vdst = v_sb[hb][:, :].rearrange("p (c j) -> p c j", j=128)
        for part in range(4):
            prog.dma("sync", vdst[:, 8 * part:8 * (part + 1), :],
                     vsrc[:, 8 * part:8 * (part + 1), :], waits=waits, sem=sY)
        prog.dma("sync", qh_sb[hb][:, :], q_scr[128 * h:128 * (h + 1), :],
                 waits=waits, sem=sY)
        return w(sY)

    loads_done = [None] * (H + 1)
    loads_done[0] = load_head(0, p1_done)
    sbuf_cons = [None] * (H + 2)
    misc_free = None
    psmall_free = sc2_ready
    psy_free = None
    psl_free = None
    yst_free = [None, None]
    s_exp_done = {}
    ring1_cons = {}
    ring2_cons = {}
    spin = 0
    g1 = 0
    g2 = 0
    negmh_ready = [None] * NJS
    negmh_free = [None, None]
    lnl8_free = [None]
    lnl2_free = [None]
    linv_free = [None]

    LAG = 6

    def emit_sweep1(h, js):
        nonlocal spin, g1, misc_free
        hb = h % 2
        C = 8 * js + 8
        negc8_ready = (sA, negc8_done[h])
        pend = []

        def flush1():
            qlo_, slot_, lw_, ci_ = pend.pop(0)
            prog.mm(cst["ones_col_bf"][:, :],
                    ring1[:, 512 * slot_ + qlo_:512 * (slot_ + 1)],
                    ps_misc[0:1, qlo_:512], ci_ == 0, ci_ == C - 1,
                    waits=lw_, inc=(sT, 1))
            ring1_cons[g1_base + ci_] = w(sT)

        g1_base = g1
        for ci in range(C):
            cp = ci - 8 * js
            qlo = 128 * I0[cp] if cp >= 0 else 0
            slot = ci % 8
            prog.mm(k_sb[hb][:, 128 * ci:128 * (ci + 1)],
                    qh_sb[hb][:, 512 * js + qlo:512 * (js + 1)],
                    ps_s[spin % 4][:, qlo:512], True, True,
                    waits=[loads_done[h], negc8_ready, s_exp_done.get(spin - 4)],
                    inc=(sT, 1))
            smm_done = w(sT)
            prog.act(ring1[:, 512 * slot + qlo:512 * (slot + 1)],
                     ps_s[spin % 4][:, qlo:512], AF.Exp,
                     bias=negc8_cols[:, h:h + 1], scale=INV_SQRT_HD / 8.0,
                     waits=[smm_done, ring1_cons.get(g1_base + ci - 8)], inc=(sA, 1))
            s_exp_done[spin] = w(sA)
            expd = w(sA)
            if cp >= 0:
                col = 512 * slot + qlo
                prog.tt(ALU.mult, ring1[:, col:col + 128],
                        ring1[:, col:col + 128],
                        cst["masks"][:, 128 * cp:128 * (cp + 1)],
                        waits=[expd, const_ld], inc=(sV, 1))
                mask_done = w(sV)
            else:
                mask_done = expd
            pend.append((qlo, slot, [expd, mask_done,
                                     misc_free if ci == 0 else None], ci))
            if len(pend) > LAG:
                flush1()
            spin += 1
        while pend:
            flush1()
        g1 += C
        l8_done = w(sT)
        prog.act(lnl8[:, :], ps_misc[0:1, 0:512], AF.Ln,
                 waits=[l8_done, lnl8_free[0]], inc=(sA, 1))
        misc_free = w(sA)
        prog.ts(negmh[js % 2][:, :], lnl8[:, :], float(-8.0 / INV_SQRT_HD),
                negc_sc[0:1, h:h + 1], ALU.mult, ALU.add,
                waits=[w(sA), negmh_free[js % 2]], inc=(sV, 1))
        lnl8_free[0] = w(sV)
        negmh_ready[js] = w(sV)

    def emit_sweep2(h, js):
        nonlocal spin, g2, misc_free, psy_free, psl_free
        hb = h % 2
        C = 8 * js + 8
        pend = []

        def flush2():
            qlo_, slot_, lw_, ci_ = pend.pop(0)
            prog.mm(cst["ones_col_bf"][:, :],
                    ring2[:, 512 * slot_ + qlo_:512 * (slot_ + 1)],
                    ps_l[0:1, qlo_:512], ci_ == 0, ci_ == C - 1,
                    waits=lw_)
            prog.mm(v_sb[hb][:, 128 * ci_:128 * (ci_ + 1)],
                    ring2[:, 512 * slot_ + qlo_:512 * (slot_ + 1)],
                    ps_y[:, qlo_:512], ci_ == 0, ci_ == C - 1,
                    inc=(sT, 1))
            ring2_cons[g2_base + ci_] = w(sT)

        g2_base = g2
        for ci in range(C):
            cp = ci - 8 * js
            qlo = 128 * I0[cp] if cp >= 0 else 0
            slot = ci % 8
            reg = ps_s[spin % 4][:, qlo:512]
            prog.mm(cst["ones_row_bf"][:, :], negmh[js % 2][0:1, qlo:512],
                    reg, True, False,
                    waits=[loads_done[h], negmh_ready[js],
                           s_exp_done.get(spin - 4)])
            prog.mm(k_sb[hb][:, 128 * ci:128 * (ci + 1)],
                    qh_sb[hb][:, 512 * js + qlo:512 * (js + 1)],
                    reg, False, True, inc=(sT, 1))
            smm_done = w(sT)
            prog.act(ring2[:, 512 * slot + qlo:512 * (slot + 1)],
                     ps_s[spin % 4][:, qlo:512], AF.Exp, scale=INV_SQRT_HD,
                     waits=[smm_done, ring2_cons.get(g2_base + ci - 8)], inc=(sA, 1))
            s_exp_done[spin] = w(sA)
            expd = w(sA)
            if cp >= 0:
                col = 512 * slot + qlo
                prog.tt(ALU.mult, ring2[:, col:col + 128],
                        ring2[:, col:col + 128],
                        cst["masks"][:, 128 * cp:128 * (cp + 1)],
                        waits=[expd, const_ld], inc=(sV, 1))
                mask_done = w(sV)
            else:
                mask_done = expd
            pend.append((qlo, slot,
                         [expd, mask_done,
                          psl_free if ci == 0 else None,
                          psy_free if ci == 0 else None], ci))
            if len(pend) > LAG:
                flush2()
            spin += 1
        while pend:
            flush2()
        g2 += C
        av_done = w(sT)
        prog.recip(linv_sb[:, :], ps_l[0:1, 0:512],
                   waits=[av_done, linv_free[0]], inc=(sV, 1))
        prog.mm(cst["ones_row_f"][:, :], linv_sb[:, :], ps_l[:, 0:512],
                True, True, waits=[w(sV)], inc=(sT, 1))
        linv_free[0] = w(sT)
        prog.act(linv_bc[:, :], ps_l[:, 0:512], AF.Identity,
                 waits=[w(sT)], inc=(sA, 1))
        psl_free = w(sA)
        ysl = (h * NJS + js) % 2
        prog.tt(ALU.mult, yst[ysl][:, :], ps_y[:, :], linv_bc[:, :],
                waits=[psl_free, yst_free[ysl]], inc=(sV, 1))
        psy_free = w(sV)
        ynorm_done = w(sV)
        for qt in range(4):
            qtg = 4 * js + qt
            prog.dma("sync",
                     y_scr[128 * qtg:128 * (qtg + 1), 128 * h:128 * (h + 1)],
                     yst[ysl][:, 128 * qt:128 * (qt + 1)],
                     waits=[ynorm_done], sem=sSt)
        yst_free[ysl] = w(sSt)
        negmh_free[js % 2] = av_done

    negc8_done = [None] * H
    for h in range(H):
        hb = h % 2
        prog.tt(ALU.mult, sq_scr[:, :], k_sb[hb][:, :], k_sb[hb][:, :],
                waits=[loads_done[h]], inc=(sV, 1))
        sqk = w(sV)
        for c in range(32):
            prog.mm(sq_scr[:, 128 * c:128 * (c + 1)], cst["ones_col_bf"][:, :],
                    ps_misc[:, c:c + 1], True, True,
                    waits=[sqk, misc_free] if c == 0 else (),
                    inc=(sT, 1) if c == 31 else None)
        knorm_mm = w(sT)
        prog.red(nrm_col[:, :], ps_misc[:, 0:32], ALU.max, waits=[knorm_mm],
                 inc=(sV, 1))
        misc_free = w(sV)
        prog.mm(nrm_col[:, :], cst["ident_f"][:, :], p_small[0:1, 0:128],
                True, True, waits=[w(sV), psmall_free], inc=(sT, 1))
        prog.red(cstar[:, 1:2], p_small[0:1, 0:128], ALU.max, waits=[w(sT)],
                 inc=(sV, 1))
        kn_done = w(sV)
        prog.tt(ALU.mult, sq_scr[:, 0:NTOK], qh_sb[hb][:, :], qh_sb[hb][:, :],
                waits=[knorm_mm], inc=(sV, 1))
        sqq = w(sV)
        for c in range(16):
            prog.mm(sq_scr[:, 128 * c:128 * (c + 1)], cst["ones_col_bf"][:, :],
                    ps_misc[:, 32 + c:33 + c], True, True,
                    waits=[sqq, misc_free] if c == 0 else (),
                    inc=(sT, 1) if c == 15 else None)
        qnorm_mm = w(sT)
        prog.red(nrm_col[:, :], ps_misc[:, 32:48], ALU.max,
                 waits=[qnorm_mm, kn_done], inc=(sV, 1))
        misc_free = w(sV)
        prog.mm(nrm_col[:, :], cst["ident_f"][:, :], p_small[0:1, 0:128],
                True, True, waits=[w(sV), kn_done], inc=(sT, 1))
        prog.red(cstar[:, 0:1], p_small[0:1, 0:128], ALU.max, waits=[w(sT)],
                 inc=(sV, 1))
        prog.tt(ALU.mult, cstar[:, 2:3], cstar[:, 0:1], cstar[:, 1:2],
                waits=[w(sV)], inc=(sV, 1))
        prog.act(cstar[:, 3:4], cstar[:, 2:3], AF.Ln, scale=float(1.0 / HD),
                 waits=[w(sV)], inc=(sA, 1))
        prog.act(cstar[:, 4:5], cstar[:, 3:4], AF.Exp, scale=0.5,
                 waits=[w(sA)], inc=(sA, 1))
        cs_done = w(sA)
        prog.ts(cstar[:, 5:6], cstar[:, 4:5], -0.125, None, ALU.mult,
                waits=[cs_done], inc=(sV, 1))
        prog.ts(negc_sc[0:1, h:h + 1], cstar[:, 4:5],
                float(-1.0 / INV_SQRT_HD), None, ALU.mult,
                waits=[cs_done], inc=(sV, 1))
        prog.mm(cst["ones_row_f"][:, :], cstar[:, 5:6], p_small[:, 130:131],
                True, True, waits=[w(sV)], inc=(sT, 1))
        prog.act(negc8_cols[:, h:h + 1], p_small[:, 130:131], AF.Identity,
                 waits=[w(sT)], inc=(sA, 1))
        psmall_free = w(sA)
        negc8_done[h] = sA.n

        if h + 1 < H:
            loads_done[h + 1] = load_head(h + 1, [sbuf_cons[h + 1]])

        emit_sweep1(h, 0)
        for js in range(1, NJS):
            emit_sweep1(h, js)
            emit_sweep2(h, js - 1)
        emit_sweep2(h, NJS - 1)
        sbuf_cons[h + 2] = w(sT)

    y_stores_done = w(sSt)
    prog.barrier(sB)
    p2.close()

    if upto <= 2:
        return finish([w(sSt), w(sA), w(sV), w(sT), w(sCast), w(sScan), w(sY)])

    p2_done = [w(sT), w(sA), w(sV), y_stores_done]

    ln_m = sb("ln_m", [128, NTILE], F32)
    ln_rs = sb("ln_rs", [128, NTILE], F32)
    ln_mrs = sb("ln_mrs", [128, NTILE], F32)
    p3ab = ExitStack()
    x1q_fm = p3ab.enter_context(nc.sbuf_tensor("x1q_fm", [128, H * D], BF16))

    p3a = ExitStack()
    asb = lambda name, shape, dt: p3a.enter_context(nc.sbuf_tensor(name, shape, dt))
    wproj_sb = asb("wproj_sb", [128, H * D], BF16)
    yt_sb = [asb(f"yt{i}", [128, D], BF16) for i in range(2)]
    xot_sb = [asb(f"xot{i}", [128, D], BF16) for i in range(2)]
    bprj_sb = asb("bprj_sb", [128, D], BF16)
    tmp_sb = [asb(f"tmpa{i}", [128, D], BF16) for i in range(2)]
    r_sb = [asb(f"ra{i}", [128, D], BF16) for i in range(2)]
    rq_scr = asb("rq_scr", [128, 1024], BF16)
    x1f = asb("x1f", [128, D], F32)
    rndt = asb("rndt", [128, D], F32)
    x1q_tm = [asb(f"x1q_tm{i}", [128, D], BF16) for i in range(2)]
    lnp_sum = asb("lnp_sum", [128, 32], F32)
    lnp_sq = asb("lnp_sq", [128, 32], F32)
    cols_a = asb("cols_a", [128, 8], F32)
    nmag_col = asb("nmag_col", [128, 1], F32)
    ps_pj = [p3a.enter_context(nc.psum_tensor(f"ps_pj{i}", [128, 1024], F32))
             for i in range(2)]
    ps_tr = [p3a.enter_context(nc.psum_tensor(f"ps_tr{i}", [128, 512], F32))
             for i in range(2)]

    prog.memset(nmag_col[:, :], -MAGIC, waits=p2_done, inc=(sV, 1))
    for c in range(16):
        prog.dma("gpsimd", wproj_sb[:, D * c:D * (c + 1)],
                 wprojT[128 * c:128 * (c + 1), :], waits=p2_done, sem=sCast)
    prog.dma("gpsimd", bprj_sb[:, :], bproj_bc, waits=p2_done, sem=sCast)
    wproj_ready = w(sCast)

    yt_free = [p2_done[0], p2_done[0]]
    xot_free = [None, None]
    tmp_free = [None, None]
    r_free = [None, None]
    cols_free = None
    x1f_free = None
    pj_free = [None, None]
    tr_free = [None, None]
    rndt_free = None
    x1qtm_free = [None, None]
    rq_free = None
    pend_tr = []

    def flush_tr():
        if not pend_tr:
            return
        t_, tb_, ready_ = pend_tr.pop(0)
        for c in range(16):
            prog.mm(x1q_tm[tb_][:, 128 * c:128 * (c + 1)], cst["ident_bf"][:, :],
                    ps_tr[(c // 4) % 2][:, 128 * (c % 4):128 * (c % 4 + 1)],
                    True, True,
                    waits=[ready_, const_ld, tr_free[(c // 4) % 2]]
                    if c % 4 == 0 else (),
                    inc=(sT, 1) if c % 4 == 3 else None)
            if c % 4 == 3:
                grp_done = w(sT)
                dst = x1q_fm[:, :].rearrange("p (cc tk) -> p cc tk", tk=D)[
                    :, c - 3:c + 1, 128 * t_:128 * (t_ + 1)]
                prog.act(dst, ps_tr[(c // 4) % 2][:, :], AF.Identity,
                         waits=[grp_done], inc=(sA, 1))
                tr_free[(c // 4) % 2] = w(sA)
        x1qtm_free[tb_] = w(sT)

    def tile_loads_a(t):
        tb = t % 2
        prog.dma("sync", yt_sb[tb][:, :], y_scr[128 * t:128 * (t + 1), :],
                 waits=[yt_free[tb], y_stores_done], sem=sY)
        prog.dma("gpsimd", xot_sb[tb][:, :], x_own_tm[128 * t:128 * (t + 1), :],
                 waits=[xot_free[tb]], sem=sCast)
        return w(sY), w(sCast)

    loads_a = [tile_loads_a(0), tile_loads_a(1)]
    for t in range(NTILE):
        tb = t % 2
        y_ld, x_ld = loads_a[t]
        prog.tt(ALU.add, tmp_sb[tb][:, :], xot_sb[tb][:, :], bprj_sb[:, :],
                waits=[x_ld, wproj_ready, tmp_free[tb]], inc=(sP, 1), eng="gpsimd")
        tmp_ready = w(sP)
        for half in range(2):
            ps = ps_pj[half]
            mmw = [y_ld, wproj_ready, pj_free[half]]
            for c in range(16):
                for o2 in range(2):
                    of0 = 1024 * half + 512 * o2
                    prog.mm(yt_sb[tb][:, 128 * c:128 * (c + 1)],
                            wproj_sb[:, D * c + of0:D * c + of0 + 512],
                            ps[:, 512 * o2:512 * (o2 + 1)],
                            c == 0, c == 15,
                            waits=mmw if (c == 0 and o2 == 0) else (),
                            inc=(sT, 1) if (c == 15 and o2 == 1) else None)
            mm_done = w(sT)
            prog.stt(r_sb[tb][:, 1024 * half:1024 * (half + 1)], ps[:, :], 1.0,
                     tmp_sb[tb][:, 1024 * half:1024 * (half + 1)], ALU.mult, ALU.add,
                     accum=lnp_sum[:, 2 * t % 32 + half:2 * t % 32 + half + 1],
                     waits=[mm_done, tmp_ready] + (r_free[tb] or []), inc=(sV, 1))
            pj_free[half] = w(sV)
            prog.act(rq_scr[:, :], r_sb[tb][:, 1024 * half:1024 * (half + 1)],
                     AF.Square, accum=lnp_sq[:, 2 * t % 32 + half:2 * t % 32 + half + 1],
                     waits=[w(sV), rq_free], inc=(sA, 1))
            rq_free = w(sA)
        yt_free[tb] = w(sT)
        flush_tr()
        r_ready = w(sV)
        sq_ready = w(sA)
        prog.dma("sync", x1_scr[128 * t:128 * (t + 1), :], r_sb[tb][:, :],
                 waits=[r_ready], sem=sSt)
        r_spill = w(sSt)
        i0c, i1c = 2 * t % 32, 2 * t % 32 + 1
        prog.tt(ALU.add, cols_a[:, 0:1], lnp_sum[:, i0c:i0c + 1],
                lnp_sum[:, i1c:i1c + 1], waits=[r_ready, cols_free], inc=(sV, 1))
        prog.ts(ln_m[:, t:t + 1], cols_a[:, 0:1], 1.0 / D, None, ALU.mult,
                waits=[w(sV)], inc=(sV, 1))
        prog.tt(ALU.add, cols_a[:, 1:2], lnp_sq[:, i0c:i0c + 1],
                lnp_sq[:, i1c:i1c + 1], waits=[sq_ready], inc=(sV, 1))
        prog.ts(cols_a[:, 1:2], cols_a[:, 1:2], 1.0 / D, None, ALU.mult,
                waits=[w(sV)], inc=(sV, 1))
        prog.act(cols_a[:, 2:3], ln_m[:, t:t + 1], AF.Square, waits=[w(sV)],
                 inc=(sA, 1))
        prog.tt(ALU.subtract, cols_a[:, 3:4], cols_a[:, 1:2], cols_a[:, 2:3],
                waits=[w(sA), w(sV)], inc=(sV, 1))
        prog.ts(cols_a[:, 3:4], cols_a[:, 3:4], EPS, None, ALU.add,
                waits=[w(sV)], inc=(sV, 1))
        prog.recip(cols_a[:, 4:5], cols_a[:, 3:4], waits=[w(sV)], inc=(sV, 1))
        prog.act(ln_rs[:, t:t + 1], cols_a[:, 4:5], AF.Sqrt, waits=[w(sV)],
                 inc=(sA, 1))
        rstd_ready = w(sA)
        prog.tt(ALU.mult, ln_mrs[:, t:t + 1], ln_m[:, t:t + 1], ln_rs[:, t:t + 1],
                waits=[rstd_ready], inc=(sV, 1))
        prog.ts(ln_mrs[:, t:t + 1], ln_mrs[:, t:t + 1], -1.0, None, ALU.mult,
                waits=[w(sV)], inc=(sV, 1))
        prog.red(cols_a[:, 5:6], r_sb[tb][:, :], ALU.max, waits=[r_ready],
                 inc=(sV, 1))
        prog.red(cols_a[:, 6:7], r_sb[tb][:, :], ALU.min, waits=[w(sV)],
                 inc=(sV, 1))
        prog.tt(ALU.subtract, cols_a[:, 5:6], cols_a[:, 5:6], ln_m[:, t:t + 1],
                waits=[w(sV)], inc=(sV, 1))
        prog.tt(ALU.subtract, cols_a[:, 6:7], ln_m[:, t:t + 1], cols_a[:, 6:7],
                waits=[w(sV)], inc=(sV, 1))
        prog.tt(ALU.max, cols_a[:, 5:6], cols_a[:, 5:6], cols_a[:, 6:7],
                waits=[w(sV)], inc=(sV, 1))
        prog.tt(ALU.mult, amax1_cols[:, t:t + 1], cols_a[:, 5:6], ln_rs[:, t:t + 1],
                waits=[w(sV), rstd_ready], inc=(sV, 1))
        amax_ready = w(sV)
        prog.tt(ALU.mult, dq1_cols[:, t:t + 1], amax1_cols[:, t:t + 1],
                gq1_col[:, :], waits=[amax_ready, sc1_ready], inc=(sV, 1))
        prog.recip(cols_a[:, 7:8], amax1_cols[:, t:t + 1], waits=[amax_ready],
                   inc=(sV, 1))
        prog.ts(s1_cols[:, t:t + 1], cols_a[:, 7:8], Q8, None, ALU.mult,
                waits=[w(sV)], inc=(sV, 1))
        qsc_ready = w(sV)
        prog.act(x1f[:, :], r_sb[tb][:, :], AF.Identity, bias=ln_mrs[:, t:t + 1],
                 scale=ln_rs[:, t:t + 1], waits=[qsc_ready, x1f_free], inc=(sA, 1))
        r_free[tb] = [r_spill, w(sA)]
        cols_free = w(sA)
        prog.act(rndt[:, :], x1f[:, :], AF.Identity, bias=cst["magic_col"][:, :],
                 scale=s1_cols[:, t:t + 1], waits=[w(sA), rndt_free], inc=(sA, 1))
        x1f_free = w(sA)
        prog.ts(x1q_tm[tb][:, :], rndt[:, :], -MAGIC, None, ALU.add,
                waits=[w(sA), x1qtm_free[tb]], inc=(sV, 1))
        rndt_free = w(sV)
        x1q_ready = w(sV)
        pend_tr.append((t, tb, x1q_ready))
        xot_free[tb] = w(sP)
        tmp_free[tb] = w(sV)
        if t + 2 < NTILE:
            loads_a.append(tile_loads_a(t + 2))
    flush_tr()
    x1q_done = w(sA)
    r_spills_done = w(sSt)

    if debug:
        prog.dma("gpsimd", dbg_x1q, x1q_fm[:, :], waits=[x1q_done], sem=sSt)
        dcw = [x1q_done, w(sV)]
        prog.dma("gpsimd", dbg_cols[:, 0:16], amax1_cols[:, :], waits=dcw, sem=sSt)
        prog.dma("gpsimd", dbg_cols[:, 16:32], s1_cols[:, :], waits=dcw, sem=sSt)
        prog.dma("gpsimd", dbg_cols[:, 32:48], ln_m[:, :], waits=dcw, sem=sSt)
        prog.dma("gpsimd", dbg_cols[:, 48:64], ln_rs[:, :], waits=dcw, sem=sSt)

    prog.barrier(sB)
    p3a.close()

    if upto <= 31:
        p3ab.close()
        return finish([w(sSt), w(sA), w(sV), w(sT), w(sCast), w(sScan), w(sY)])

    p3b = ExitStack()
    bsb = lambda name, shape, dt: p3b.enter_context(nc.sbuf_tensor(name, shape, dt))
    w1q_sb = [bsb(f"w1q{i}", [128, 16 * 512], BF16) for i in range(2)]
    w1st = [bsb(f"w1st{i}", [128, 512], F32) for i in range(2)]
    w1rnd = bsb("w1rnd", [128, 512], F32)
    b1_sb = bsb("b1_sb", [128, DFF], BF16)
    x2pre = [bsb(f"x2pre{i}", [128, 512], F32) for i in range(4)]
    gt_sb = [bsb(f"gt{i}", [128, 512], BF16) for i in range(4)]
    ps_f1 = [p3b.enter_context(nc.psum_tensor(f"ps_f1{i}", [128, 512], F32))
             for i in range(4)]

    p3a_done = [x1q_done, r_spills_done, w(sT), w(sV), w(sP)]
    for c in range(4):
        prog.dma("gpsimd", b1_sb[:, 2048 * c:2048 * (c + 1)],
                 b1_bc[:, 2048 * c:2048 * (c + 1)], waits=p3a_done, sem=sCast)
    b1_ready = w(sCast)

    w1st_free = [None, None]
    w1rnd_free = None
    w1q_free = [None, None]
    x2_free = [None] * 4
    gt_free = [None] * 4
    f1_free = [None, None, None, None]

    def w1_chunk(o, c):
        nonlocal w1rnd_free
        ob = o % 2
        cs = c % 2
        prog.dma("sync", w1st[cs][:, :],
                 w1T[128 * c:128 * (c + 1), 512 * o:512 * (o + 1)],
                 waits=[w1st_free[cs], w1q_free[ob] if c == 0 else None]
                 + (p3a_done if o == 0 and c == 0 else []), sem=sLd)
        prog.act(w1rnd[:, :], w1st[cs][:, :], AF.Identity,
                 bias=cst["magic_col"][:, :], scale=sw1_col[:, :],
                 waits=[w(sLd), sc1_ready, w1rnd_free], inc=(sA, 1))
        w1st_free[cs] = w(sA)
        prog.ts(w1q_sb[ob][:, 512 * c:512 * (c + 1)], w1rnd[:, :], -MAGIC,
                None, ALU.add, waits=[w(sA)], inc=(sV, 1))
        w1rnd_free = w(sV)

    for c in range(16):
        w1_chunk(0, c)
    w1_ready = [w(sV)]
    for o in range(16):
        ob = o % 2
        for t in range(NTILE):
            if o + 1 < 16:
                w1_chunk(o + 1, t)
                if t == NTILE - 1:
                    w1_ready.append(w(sV))
            ps = ps_f1[t % 4]
            mmw = [w1_ready[o], x1q_done, f1_free[t % 4]]
            for c in range(16):
                prog.mm(x1q_fm[:, D * c + 128 * t:D * c + 128 * (t + 1)],
                        w1q_sb[ob][:, 512 * c:512 * (c + 1)],
                        ps[:, :], c == 0, c == 15,
                        waits=mmw if c == 0 else (),
                        inc=(sT, 1) if c == 15 else None)
            mm_done = w(sT)
            tb2 = t % 4
            prog.stt(x2pre[tb2][:, :], ps[:, :], dq1_cols[:, t:t + 1],
                     b1_sb[:, 512 * o:512 * (o + 1)], ALU.mult, ALU.add,
                     waits=[mm_done, b1_ready, x2_free[tb2]], inc=(sV, 1))
            f1_free[t % 4] = w(sV)
            prog.act(gt_sb[tb2][:, :], x2pre[tb2][:, :], AF.Gelu_apprx_tanh,
                     waits=[w(sV), gt_free[tb2]], inc=(sA, 1))
            x2_free[tb2] = w(sA)
            gelu_done = w(sA)
            prog.red(amax2_parts[:, 16 * t + o:16 * t + o + 1], gt_sb[tb2][:, :],
                     ALU.max, absval=True, waits=[gelu_done], inc=(sV, 1))
            prog.dma("sync", g_scr[128 * t:128 * (t + 1), 512 * o:512 * (o + 1)],
                     gt_sb[tb2][:, :], waits=[gelu_done], sem=sSt)
            gt_free[tb2] = w(sSt)
        w1q_free[ob] = w(sT)
    amax2c = bsb("amax2c", [128, NTILE], F32)
    for t in range(NTILE):
        prog.red(amax2c[:, t:t + 1], amax2_parts[:, 16 * t:16 * (t + 1)],
                 ALU.max, waits=[w(sV)], inc=(sV, 1))
        prog.tt(ALU.mult, dq2_cols[:, t:t + 1], amax2c[:, t:t + 1],
                gq2_col[:, :], waits=[w(sV), sc2_ready], inc=(sV, 1))
        prog.recip(s2_cols[:, t:t + 1], amax2c[:, t:t + 1],
                   waits=[w(sV)], inc=(sV, 1))
        prog.ts(s2_cols[:, t:t + 1], s2_cols[:, t:t + 1], Q8, None, ALU.mult,
                waits=[w(sV)], inc=(sV, 1))
    s2_ready = w(sV)
    g_stores_done = w(sSt)

    if debug:
        prog.dma("gpsimd", dbg_cols[:, 64:80], s2_cols[:, :], waits=[s2_ready], sem=sSt)
        prog.dma("gpsimd", dbg_cols[:, 80:96], dq2_cols[:, :], waits=[s2_ready], sem=sSt)

    prog.barrier(sB)
    p3b.close()
    p3ab.close()

    if upto <= 32:
        return finish([w(sSt), w(sA), w(sV), w(sT), w(sCast), w(sScan), w(sY)])

    NB = 8
    p3c = ExitStack()
    csb = lambda name, shape, dt: p3c.enter_context(nc.sbuf_tensor(name, shape, dt))
    oacc = csb("oacc", [128, NTILE * 1024], F32)
    w2q_sb = [csb(f"w2q{i}", [128, 8 * 1024], BF16) for i in range(2)]
    w2st = [csb(f"w2st{i}", [128, 1024], F32) for i in range(2)]
    rnd_scr = csb("rnd_scr", [128, 2048], F32)
    g_in = [csb(f"g_in{i}", [128, 1024], BF16) for i in range(3)]
    x2q_tm = [csb(f"x2q_tm{i}", [128, 1024], BF16) for i in range(3)]
    x2q_fm = [csb(f"x2q_fm{i}", [128, 1024], BF16) for i in range(3)]
    rres = [csb(f"rres{i}", [128, 1024], BF16) for i in range(2)]
    xrec = csb("xrec", [128, 1024], F32)
    tmp2 = csb("tmp2", [128, 1024], F32)
    b2_sb = csb("b2_sb", [128, D], F32)
    r2h0 = csb("r2h0", [128, 1024], BF16)
    r2h1 = csb("r2h1", [128, 1024], F32)
    outst = [csb(f"outst{i}", [128, 1024], F32) for i in range(2)]
    ln2_sum = csb("ln2_sum", [128, 32], F32)
    ln2_sq = csb("ln2_sq", [128, 32], F32)
    cols_c = csb("cols_c", [128, 8], F32)
    sq2_scr = csb("sq2_scr", [128, 1024], BF16)
    r2_scr = nc.dram_tensor("r2_scr", [NTOK, 1024], BF16, kind=scratch_kind).ap()
    ps_f2 = [p3c.enter_context(nc.psum_tensor(f"ps_f2{i}", [128, 1024], F32))
             for i in range(2)]
    ps_t2 = [p3c.enter_context(nc.psum_tensor(f"ps_t2{i}", [128, 512], F32))
             for i in range(2)]

    p3b_done = [g_stores_done, s2_ready, w(sT), w(sA), w(sP)]
    prog.dma("sync", b2_sb[:, :], b2_bc, waits=p3b_done, sem=sLd)
    b2_ready = w(sLd)

    w2st_free = [None, None]
    w2rnd_free = None
    x2rnd_free = [None]
    w2q_free = [None, None]
    oacc_free = [None] * NTILE
    gin_free = [None, None, None]
    x2qtm_free = [None, None, None]
    x2qfm_free = [None, None, None]
    t2_free = [None, None]
    f2_free = [None, None]
    rres_free = [None, None]
    xrec_free = None
    tmp2_free = None
    r2h0_free = None
    r2h1_free = None
    outst_free = [None, None]
    sq2_free = None
    colsc_free = None

    def stream_w2_block(h2, b):
        nonlocal w2rnd_free
        wb = (h2 * NB + b) % 2
        for c in range(8):
            cs = c % 2
            row0 = 1024 * b + 128 * c
            prog.dma("sync", w2st[cs][:, :],
                     w2T[row0:row0 + 128, 1024 * h2:1024 * (h2 + 1)],
                     waits=[w2st_free[cs], w2q_free[wb] if c == 0 else None]
                     + (p3b_done if h2 + b == 0 and c == 0 else []), sem=sLd)
            prog.act(rnd_scr[:, 0:1024], w2st[cs][:, :], AF.Identity,
                     bias=cst["magic_col"][:, :], scale=sw2_col[:, :],
                     waits=[w(sLd), sc2_ready, w2rnd_free], inc=(sA, 1))
            w2st_free[cs] = w(sA)
            prog.ts(w2q_sb[wb][:, 1024 * c:1024 * (c + 1)], rnd_scr[:, 0:1024],
                    -MAGIC, None, ALU.add, waits=[w(sA)], inc=(sV, 1))
            w2rnd_free = w(sV)
        return w(sV)

    w2_ready = {}
    w2_ready[(0, 0)] = stream_w2_block(0, 0)
    seq = [(h2, b) for h2 in range(2) for b in range(NB)]
    units = [(si, t) for si in range(len(seq)) for t in range(NTILE)]
    prep_done = {}

    def prep_x2q(ui_):
        si, t = units[ui_]
        h2_, b_ = seq[si]
        tb_ = ui_ % 3
        prog.dma("sync", g_in[tb_][:, :],
                 g_scr[128 * t:128 * (t + 1), 1024 * b_:1024 * (b_ + 1)],
                 waits=[gin_free[tb_], g_stores_done], sem=sY)
        prog.act(rnd_scr[:, 1024:2048], g_in[tb_][:, :], AF.Identity,
                 bias=cst["magic_col"][:, :], scale=s2_cols[:, t:t + 1],
                 waits=[w(sY), s2_ready, x2rnd_free[0]], inc=(sA, 1))
        gin_free[tb_] = w(sA)
        prog.ts(x2q_tm[tb_][:, :], rnd_scr[:, 1024:2048], -MAGIC, None, ALU.add,
                waits=[w(sA), x2qtm_free[tb_]], inc=(sV, 1))
        x2rnd_free[0] = w(sV)
        x2q_ready = w(sV)
        for c in range(8):
            prog.mm(x2q_tm[tb_][:, 128 * c:128 * (c + 1)], cst["ident_bf"][:, :],
                    ps_t2[(c // 4) % 2][:, 128 * (c % 4):128 * (c % 4 + 1)],
                    True, True,
                    waits=[x2q_ready, t2_free[(c // 4) % 2]] if c % 4 == 0 else (),
                    inc=(sT, 1) if c % 4 == 3 else None)
            if c % 4 == 3:
                prog.act(x2q_fm[tb_][:, 512 * (c // 4):512 * (c // 4 + 1)],
                         ps_t2[(c // 4) % 2][:, :], AF.Identity,
                         waits=[w(sT), x2qfm_free[tb_] if c == 3 else None],
                         inc=(sA, 1))
                t2_free[(c // 4) % 2] = w(sA)
        x2qtm_free[tb_] = w(sT)
        prep_done[ui_] = w(sA)

    prep_x2q(0)
    prep_x2q(1)
    for ui, (si, t) in enumerate(units):
        h2, b = seq[si]
        wb = (h2 * NB + b) % 2
        tb = t % 2
        ub = ui % 3
        if t == 0 and si + 1 < len(seq):
            w2_ready[seq[si + 1]] = stream_w2_block(*seq[si + 1])
        if ui + 2 < len(units):
            prep_x2q(ui + 2)
        ps = ps_f2[tb]
        mmw = [prep_done[ui], w2_ready[(h2, b)], f2_free[tb]]
        for c in range(8):
            for o2 in range(2):
                prog.mm(x2q_fm[ub][:, 128 * c:128 * (c + 1)],
                        w2q_sb[wb][:, 1024 * c + 512 * o2:1024 * c + 512 * (o2 + 1)],
                        ps[:, 512 * o2:512 * (o2 + 1)],
                        c == 0, c == 7,
                        waits=mmw if (c == 0 and o2 == 0) else (),
                        inc=(sT, 1) if (c == 7 and o2 == 1) else None)
        mm_done = w(sT)
        x2qfm_free[ub] = mm_done
        oreg = oacc[:, 1024 * t:1024 * (t + 1)]
        if b == 0:
            prog.copy(oreg, ps[:, :], waits=[mm_done, oacc_free[t]], inc=(sV, 1))
        else:
            prog.tt(ALU.add, oreg, oreg, ps[:, :], waits=[mm_done], inc=(sV, 1))
        f2_free[tb] = w(sV)
        if t == NTILE - 1:
            w2q_free[wb] = w(sT)
        if b < NB - 1:
            continue
        acc_done = w(sV)
        prog.dma("sync", rres[tb][:, :],
                 x1_scr[128 * t:128 * (t + 1), 1024 * h2:1024 * (h2 + 1)],
                 waits=[rres_free[tb], r_spills_done], sem=sY)
        prog.act(xrec[:, :], rres[tb][:, :], AF.Identity,
                 bias=ln_mrs[:, t:t + 1], scale=ln_rs[:, t:t + 1],
                 waits=[w(sY), xrec_free], inc=(sA, 1))
        prog.tt(ALU.add, tmp2[:, :], xrec[:, :], b2_sb[:, 1024 * h2:1024 * (h2 + 1)],
                waits=[w(sA), b2_ready, tmp2_free], inc=(sP, 1), eng="gpsimd")
        xrec_free = w(sP)
        rres_free[tb] = w(sP)
        if h2 == 0:
            prog.stt(r2h0[:, :], oreg, dq2_cols[:, t:t + 1], tmp2[:, :],
                     ALU.mult, ALU.add, accum=ln2_sum[:, 2 * t:2 * t + 1],
                     waits=[acc_done, w(sP), r2h0_free], inc=(sV, 1))
            tmp2_free = w(sV)
            oacc_free[t] = w(sV)
            r2_ready = w(sV)
            prog.act(sq2_scr[:, :], r2h0[:, :], AF.Square,
                     accum=ln2_sq[:, 2 * t:2 * t + 1],
                     waits=[r2_ready, sq2_free], inc=(sA, 1))
            sq2_free = w(sA)
            prog.dma("sync", r2_scr[128 * t:128 * (t + 1), :], r2h0[:, :],
                     waits=[r2_ready], sem=sSt)
            r2h0_free = w(sSt)
        else:
            prog.stt(r2h1[:, :], oreg, dq2_cols[:, t:t + 1], tmp2[:, :],
                     ALU.mult, ALU.add, accum=ln2_sum[:, 2 * t + 1:2 * t + 2],
                     waits=[acc_done, w(sP), r2h1_free], inc=(sV, 1))
            tmp2_free = w(sV)
            r2_ready = w(sV)
            prog.act(sq2_scr[:, :], r2h1[:, :], AF.Square,
                     accum=ln2_sq[:, 2 * t + 1:2 * t + 2],
                     waits=[r2_ready, sq2_free], inc=(sA, 1))
            sq2_stat = w(sA)
            sq2_free = sq2_stat
            prog.tt(ALU.add, cols_c[:, 0:1], ln2_sum[:, 2 * t:2 * t + 1],
                    ln2_sum[:, 2 * t + 1:2 * t + 2],
                    waits=[r2_ready, colsc_free], inc=(sV, 1))
            prog.ts(cols_c[:, 0:1], cols_c[:, 0:1], 1.0 / D, None, ALU.mult,
                    waits=[w(sV)], inc=(sV, 1))
            prog.tt(ALU.add, cols_c[:, 1:2], ln2_sq[:, 2 * t:2 * t + 1],
                    ln2_sq[:, 2 * t + 1:2 * t + 2], waits=[sq2_stat],
                    inc=(sV, 1))
            prog.ts(cols_c[:, 1:2], cols_c[:, 1:2], 1.0 / D, None, ALU.mult,
                    waits=[w(sV)], inc=(sV, 1))
            prog.act(cols_c[:, 2:3], cols_c[:, 0:1], AF.Square,
                     waits=[w(sV)], inc=(sA, 1))
            prog.tt(ALU.subtract, cols_c[:, 3:4], cols_c[:, 1:2], cols_c[:, 2:3],
                    waits=[w(sA), w(sV)], inc=(sV, 1))
            prog.ts(cols_c[:, 3:4], cols_c[:, 3:4], EPS, None, ALU.add,
                    waits=[w(sV)], inc=(sV, 1))
            prog.recip(cols_c[:, 4:5], cols_c[:, 3:4], waits=[w(sV)], inc=(sV, 1))
            prog.act(cols_c[:, 4:5], cols_c[:, 4:5], AF.Sqrt, waits=[w(sV)],
                     inc=(sA, 1))
            prog.tt(ALU.mult, cols_c[:, 5:6], cols_c[:, 0:1], cols_c[:, 4:5],
                    waits=[w(sA)], inc=(sV, 1))
            prog.ts(cols_c[:, 5:6], cols_c[:, 5:6], -1.0, None, ALU.mult,
                    waits=[w(sV)], inc=(sV, 1))
            ln2_ready = w(sV)
            prog.dma("sync", r2h0[:, :], r2_scr[128 * t:128 * (t + 1), :],
                     waits=[r2h0_free], sem=sY)
            prog.act(outst[0][:, :], r2h0[:, :], AF.Identity,
                     bias=cols_c[:, 5:6], scale=cols_c[:, 4:5],
                     waits=[w(sY), ln2_ready, outst_free[0]], inc=(sA, 1))
            r2h0_free = w(sA)
            prog.dma("sync", out[128 * t:128 * (t + 1), 0:1024], outst[0][:, :],
                     waits=[w(sA)], sem=sSt)
            outst_free[0] = w(sSt)
            prog.act(outst[1][:, :], r2h1[:, :], AF.Identity,
                     bias=cols_c[:, 5:6], scale=cols_c[:, 4:5],
                     waits=[ln2_ready, outst_free[1]], inc=(sA, 1))
            r2h1_free = w(sA)
            colsc_free = w(sA)
            prog.dma("sync", out[128 * t:128 * (t + 1), 1024:2048], outst[1][:, :],
                     waits=[w(sA)], sem=sSt)
            outst_free[1] = w(sSt)

    prog.barrier(sB)
    p3c.close()
    return finish([w(sSt), w(sA), w(sV), w(sT), w(sCast), w(sScan), w(sY),
                   w(sLd), w(sP)])


def host_prepare(inputs):
    x = np.asarray(inputs["L0_attn_x"], dtype=np.float32)
    w_attn = np.asarray(inputs["L0_attn_w_attn"], dtype=np.float32)
    b_attn = np.asarray(inputs["L0_attn_b_attn"], dtype=np.float32)
    w_proj = np.asarray(inputs["L0_attn_w_proj"], dtype=np.float32)
    b_proj = np.asarray(inputs["L0_attn_b_proj"], dtype=np.float32)
    W1 = np.asarray(inputs["L0_ffn_W1"], dtype=np.float32)
    B1 = np.asarray(inputs["L0_ffn_B1"], dtype=np.float32)
    W2 = np.asarray(inputs["L0_ffn_W2"], dtype=np.float32)
    B2 = np.asarray(inputs["L0_ffn_B2"], dtype=np.float32)

    shared = dict(
        wqT=np.ascontiguousarray(w_attn[0:D].T),
        wkT=np.ascontiguousarray(w_attn[D:2 * D].T),
        wvT=np.ascontiguousarray(w_attn[2 * D:3 * D].T),
        bq_col=np.ascontiguousarray(b_attn[0:D].reshape(H, 128).T),
        bk_col=np.ascontiguousarray(b_attn[D:2 * D].reshape(H, 128).T),
        bv_bc=np.ascontiguousarray(np.broadcast_to(b_attn[2 * D:3 * D], (128, D))),
        wprojT=np.ascontiguousarray(w_proj.T),
        bproj_bc=np.ascontiguousarray(np.broadcast_to(b_proj, (128, D))),
        w1T=np.ascontiguousarray(W1.T),
        b1_bc=np.ascontiguousarray(np.broadcast_to(B1, (128, DFF))),
        w2T=np.ascontiguousarray(W2.T),
        b2_bc=np.ascontiguousarray(np.broadcast_to(B2, (128, D))),
        ident_bf=np.eye(128, dtype=BFNP),
        ident_f=np.eye(128, dtype=np.float32),
        ones_col_bf=np.ones((128, 1), BFNP),
        ones_row_f=np.ones((1, 128), np.float32),
        ones_row_bf=np.ones((1, 128), BFNP),
    )

    tri = np.triu(np.ones((128, 128), np.float32))
    zeros = np.zeros((128, 128), np.float32)
    ones = np.ones((128, 128), np.float32)
    mask_p = {}
    for p in range(2):
        tiles = [(tri if cp % 2 == p else (zeros if p == 0 else ones))
                 for cp in range(8)]
        mask_p[p] = np.concatenate(tiles, axis=0).astype(BFNP)

    in_maps = []
    own_idx = {}
    for core in range(N_CORES):
        b, p = core // 2, core % 2
        blocks = np.arange(p, 32, 2)
        tok_idx = (blocks[:, None] * 128 + np.arange(128)[None, :]).reshape(-1)
        own_idx[core] = tok_idx
        xb = x[b]
        x_own = xb[tok_idx]
        m = dict(shared)
        m["x_fm"] = np.ascontiguousarray(xb.T)
        m["x_own_fm"] = np.ascontiguousarray(x_own.T)
        m["x_own_tm"] = np.ascontiguousarray(x_own)
        m["masks"] = mask_p[p]
        in_maps.append(m)
    return in_maps, own_idx


_PROG_CACHE = {}


def kernel(**inputs):
    in_maps, own_idx = host_prepare(inputs)
    if "full" not in _PROG_CACHE:
        _PROG_CACHE["full"] = build_program(upto=99, debug=False)
    nc = _PROG_CACHE["full"]
    res = run_bass_kernel_spmd(nc, in_maps, core_ids=list(range(N_CORES)))
    out = np.zeros((4, T, D), dtype=np.float32)
    for core in range(N_CORES):
        b = core // 2
        out[b, own_idx[core]] = res.results[core]["out"]
    return out
